# version 13
# baseline (speedup 1.0000x reference)
import numpy as np
from contextlib import ExitStack

import concourse.bass as bass
import concourse.mybir as mybir
from concourse.bass_utils import run_bass_kernel_spmd
from concourse.alu_op_type import AluOpType as op

F32 = mybir.dt.float32
BF16 = mybir.dt.bfloat16
AF = mybir.ActivationFunctionType
AX = mybir.AxisListType.X

D = 2048
H = 16
C = 128
QKV_N = 3 * D
MAGIC = float(3 * 2 ** 22)
NEG = -60000.0
EPS = 1e-5
L2EPS = 1e-6
LN_HALF_DK = float(-0.5 * np.log(128.0))
LN2 = float(np.log(2.0))

ENGINES = ("pe", "act", "dve", "sp", "pool")
BLOCK_NAME = dict(pe="tensor", act="scalar", dve="vector", sp="sync", pool="gpsimd")


class P:

    NLANES = 4
    CHANNELS = ("pe", "act", "dve", "sp", "pool") + tuple(
        f"{e}_d{l}" for e in ("sp", "pool") for l in range(4))

    def __init__(self, nc, es):
        self.nc = nc
        self.sem = {ch: es.enter_context(nc.semaphore(f"sem_{ch}"))
                    for ch in self.CHANNELS}
        self.q = {e: [] for e in ENGINES}
        self.cnt = {ch: 0 for ch in self.CHANNELS}
        self.lane = {"sp": 0, "pool": 0}
        self.acc = {e: ([], []) for e in ENGINES}

    @staticmethod
    def _eng_of(ch):
        return ch.split("_")[0]

    @staticmethod
    def _ext(ap):
        if ap is None or isinstance(ap, (int, float)):
            return None
        try:
            lo = ap.offset
            hi = lo + 1
            for step, count in list(ap.ap)[1:]:
                hi += step * (count - 1)
            return (ap.tensor.name, lo, hi)
        except Exception:
            return (getattr(getattr(ap, "tensor", None), "name", "?"), 0, 1 << 60)

    @staticmethod
    def _hits(x, y):
        if x[0] == "__any__" or y[0] == "__any__":
            return True
        return x[0] == y[0] and x[1] < y[2] and y[1] < x[2]

    def _conflict(self, e, writes, reads):
        tw, tr = self.acc[e]
        for w in writes:
            if w is not None and any(self._hits(x, w) for x in tw + tr):
                return True
        for r in reads:
            if r is not None and any(self._hits(x, r) for x in tw):
                return True
        return False

    def _emit(self, e, fn, deps, inc, ch=None, drain=None, w=(), r=()):
        ch = ch or e
        if drain is None and e in ("act", "dve"):
            if w or r:
                writes = [self._ext(a) for a in w]
                reads = [self._ext(a) for a in r]
            else:
                writes = [("__any__", 0, 1 << 60)]
                reads = []
            drain = self._conflict(e, writes, reads) if (w or r) else True
            if drain:
                self.acc[e] = ([], [])
            self.acc[e][0].extend(x for x in writes if x)
            self.acc[e][1].extend(x for x in reads if x)
        if drain:
            self.q[e].append((lambda g: g.drain(), (), 0, ch))
        deps = tuple(d for d in deps
                     if d is not None and (self._eng_of(d[0]) != e or "_d" in d[0]))
        self.q[e].append((fn, deps, inc, ch))
        if inc:
            self.cnt[ch] += inc
            return (ch, self.cnt[ch])
        return None

    def raw(self, e, fn, deps=(), fence=True):
        return self._emit(e, fn, deps, 1 if fence else 0)

    def nopf(self, e, deps=()):
        if e in ("pe", "act", "dve"):
            return self._emit(e, lambda g: g.drain(), deps, 1, drain=False)
        return self._emit(e, lambda g: g.nop(), deps, 1, drain=False)

    def wait_self_dma(self, e, tok):
        sem = self.sem[tok[0]]
        v = tok[1]
        self.q[e].append((lambda g, sem=sem, v=v: g.wait_ge(sem, v) or None, (), 0, e))

    def dma_quiesce(self, e):
        deps = [(f"{e}_d{l}", self.cnt[f"{e}_d{l}"]) for l in range(self.NLANES)]
        return self.nopf(e, deps=deps)

    def barrier(self):
        t1 = [self.dma_quiesce(e) if e in ("sp", "pool") else self.nopf(e)
              for e in ENGINES]
        return [self.nopf(e, deps=t1) for e in ENGINES]

    def dma(self, e, outp, inp, deps=()):
        l = self.lane[e]
        self.lane[e] = (l + 1) % self.NLANES
        ch = f"{e}_d{l}"
        gate = (ch, self.cnt[ch])
        return self._emit(e, lambda g, o=outp, i=inp: g.dma_start(out=o, in_=i),
                          list(deps) + [gate], 16, ch=ch)

    def mm(self, dst, lhs, rhs, start, stop, deps=(), fence=False):
        return self._emit("pe", lambda t, d=dst, l=lhs, r=rhs, a=start, b=stop:
                          t.matmul(d, l, r, start=a, stop=b), deps,
                          1 if fence else 0)

    def tr(self, dst, src, ident, deps=(), fence=True):
        return self._emit("pe", lambda t, d=dst, s=src, i=ident:
                          t.transpose(d, s, i), deps, 1 if fence else 0)

    def cp(self, e, dst, src, deps=(), fence=True):
        return self._emit(e, (lambda v, d=dst, s=src: v.tensor_copy(d, s)) if e == "dve"
                          else (lambda s_, d=dst, s=src: s_.activation(
                              d, s, AF.Copy, bias=0.0, scale=1.0)),
                          deps, 1 if fence else 0, w=(dst,), r=(src,))

    def ts(self, dst, src, s1, s2, o1, o2=None, deps=(), fence=True, accum=None):
        def fn(v, d=dst, s=src, s1=s1, s2=s2, o1=o1, o2=o2, ac=accum):
            if o2 is None:
                return v.tensor_scalar(d, s, s1, None, o1, accum_out=ac) \
                    if ac is not None else v.tensor_scalar(d, s, s1, None, o1)
            return v.tensor_scalar(d, s, s1, s2, o1, o2, accum_out=ac) \
                if ac is not None else v.tensor_scalar(d, s, s1, s2, o1, o2)
        aps = lambda z: z if hasattr(z, "offset") else None
        return self._emit("dve", fn, deps, 1 if fence else 0,
                          w=(dst, accum), r=(src, aps(s1), aps(s2)))

    def tt(self, dst, a, b, o, deps=(), fence=True):
        return self._emit("dve", lambda v, d=dst, a=a, b=b, o=o:
                          v.tensor_tensor(d, a, b, o), deps, 1 if fence else 0,
                          w=(dst,), r=(a, b))

    def stt(self, dst, in0, sc, in1, o1, o2, deps=(), fence=True, accum=None):
        def fn(v, d=dst, i0=in0, sc=sc, i1=in1, o1=o1, o2=o2, ac=accum):
            if ac is not None:
                return v.scalar_tensor_tensor(d, i0, sc, i1, o1, o2, accum_out=ac)
            return v.scalar_tensor_tensor(d, i0, sc, i1, o1, o2)
        aps = lambda z: z if hasattr(z, "offset") else None
        return self._emit("dve", fn, deps, 1 if fence else 0,
                          w=(dst, accum), r=(in0, in1, aps(sc)))

    def act(self, dst, src, func, bias=0.0, scale=1.0, deps=(), fence=True,
            accum=None):
        def fn(s_, d=dst, s=src, f=func, b=bias, sc=scale, ac=accum):
            if ac is not None:
                return s_.activation(d, s, f, bias=b, scale=sc, accum_out=ac)
            return s_.activation(d, s, f, bias=b, scale=sc)
        aps = lambda z: z if hasattr(z, "offset") else None
        return self._emit("act", fn, deps, 1 if fence else 0,
                          w=(dst, accum), r=(src, aps(bias), aps(scale)))

    def red(self, dst, src, o, absv=False, deps=(), fence=True):
        return self._emit("dve", lambda v, d=dst, s=src, o=o, a=absv:
                          v.tensor_reduce(d, s, AX, o, apply_absolute_value=a),
                          deps, 1 if fence else 0, w=(dst,), r=(src,))

    def mset(self, e, ap, val, deps=(), fence=True):
        return self._emit(e, lambda g, a=ap, v=val: g.memset(a, v), deps,
                          1 if fence else 0, w=(ap,))

    def replay(self):
        with self.nc.Block() as block:
            for e in ENGINES:
                deco = getattr(block, BLOCK_NAME[e])
                items = self.q[e]
                sems = self.sem

                def body(eng, items=items):
                    last = {}
                    for fn, deps, inc, ch in items:
                        for (pch, v) in deps:
                            if last.get(pch, 0) < v:
                                eng.wait_ge(sems[pch], v)
                                last[pch] = v
                        inst = fn(eng)
                        if inc:
                            inst.then_inc(sems[ch], inc)
                deco(body)


def build_program(T=4096, dbg=False, phases="all"):
    locstop = 99
    if "/" in phases:
        phases, _loc = phases.split("/")
        locstop = int(_loc)
    nc = bass.Bass("TRN2", target_bir_lowering=False)
    NCH = T // C
    T1 = min(512, T)
    NT1 = T // T1
    CPT = T1 // 128
    T3 = min(512, T)
    NT3 = T // T3
    kdbg = "ExternalOutput" if dbg else "Internal"

    x_in = nc.dram_tensor("x", [T, D], F32, kind="ExternalInput").ap()
    w_qkv = nc.dram_tensor("w_qkv", [D, QKV_N], F32, kind="ExternalInput").ap()
    w_ab = nc.dram_tensor("w_ab", [D, 2 * H + D], F32, kind="ExternalInput").ap()
    conv_w = nc.dram_tensor("conv_w", [QKV_N, 4], F32, kind="ExternalInput").ap()
    a_log = nc.dram_tensor("a_log", [H, 1], F32, kind="ExternalInput").ap()
    dt_bias = nc.dram_tensor("dt_bias", [H, 1], F32, kind="ExternalInput").ap()
    w_o = nc.dram_tensor("w_o", [D, D], F32, kind="ExternalInput").ap()
    ffn_w = nc.dram_tensor("ffn_w", [D, D], F32, kind="ExternalInput").ap()
    c_ident = nc.dram_tensor("c_ident", [128, 128], F32, kind="ExternalInput").ap()
    c_masks = nc.dram_tensor("c_masks", [128, 3 * 1024], F32, kind="ExternalInput").ap()
    c_ind = nc.dram_tensor("c_ind", [48, D], F32, kind="ExternalInput").ap()
    c_oh = nc.dram_tensor("c_oh", [128, H * H], F32, kind="ExternalInput").ap()
    c_wtile = nc.dram_tensor("c_wtile", [128, D], F32, kind="ExternalInput").ap()
    c_btile = nc.dram_tensor("c_btile", [128, D], F32, kind="ExternalInput").ap()
    y_out = nc.dram_tensor("y", [T, D], F32, kind="ExternalOutput").ap()

    qkc_s = nc.dram_tensor("qkc_s", [NCH, 48, 128, 128], BF16, kind=kdbg).ap()
    sg_s = nc.dram_tensor("sg_s", [T, D], BF16, kind=kdbg).ap()
    of_s = nc.dram_tensor("of_s", [NCH, H, 128, 128], BF16, kind=kdbg).ap()
    tern_s = nc.dram_tensor("tern_s", [D, D], BF16, kind=kdbg).ap()
    gH_d = nc.dram_tensor("gH_d", [H, T], F32, kind=kdbg).ap()
    lnbH_d = nc.dram_tensor("lnbH_d", [H, T], F32, kind=kdbg).ap()
    if dbg:
        alpha_d = nc.dram_tensor("alpha_d", [1, 4], F32, kind=kdbg).ap()

    es = ExitStack()
    SG = es.enter_context
    p = P(nc, es)

    ident_b = SG(nc.sbuf_tensor("ident_b", [128, 128], BF16)).ap()
    ident_f = SG(nc.sbuf_tensor("ident_f", [128, 128], F32)).ap()
    btile = SG(nc.sbuf_tensor("btile", [128, D], F32)).ap()
    nega = SG(nc.sbuf_tensor("nega", [H, 2], F32)).ap()
    dtb = SG(nc.sbuf_tensor("dtb", [H, 1], F32)).ap()
    ones1f = SG(nc.sbuf_tensor("ones1f", [1, 128], F32)).ap()
    alpha_t = SG(nc.sbuf_tensor("alpha_t", [128, 4], F32)).ap()
    csts = SG(nc.sbuf_tensor("csts", [128, 4], F32)).ap()
    es12 = ExitStack()
    SG12 = es12.enter_context
    masks_b = SG12(nc.sbuf_tensor("masks_b", [128, 3 * 1024], BF16)).ap()
    oh_b = SG12(nc.sbuf_tensor("oh_b", [128, H * H], BF16)).ap()
    wtile = SG12(nc.sbuf_tensor("wtile", [128, D], F32)).ap()
    ones16 = SG12(nc.sbuf_tensor("ones16", [H, 128], F32)).ap()

    tc = []
    tc.append(p.dma("pool", ident_b[:, :], c_ident[:, :]))
    tc.append(p.dma("sp", ident_f[:, :], c_ident[:, :]))
    tc.append(p.dma("pool", masks_b[:, :], c_masks[:, :]))
    tc.append(p.dma("pool", oh_b[:, :], c_oh[:, :]))
    tc.append(p.dma("sp", wtile[:, :], c_wtile[:, :]))
    tc.append(p.dma("sp", btile[:, :], c_btile[:, :]))
    tc.append(p.dma("sp", dtb[:, :], dt_bias[:, :]))
    tk_a = p.dma("sp", nega[:, 1:2], a_log[:, :])
    tk_na = p.act(nega[:, 0:1], nega[:, 1:2], AF.Exp, deps=[tk_a])
    tc.append(p.ts(nega[:, 0:1], nega[:, 0:1], -1.0, None, op.mult, deps=[tk_na]))
    p.mset("dve", ones16[:, :], 1.0)
    tc.append(p.mset("dve", ones1f[:, :], 1.0))
    p.mset("dve", csts[:, 0:1], L2EPS, fence=False)
    tc.append(p.mset("dve", csts[:, 1:2], EPS))
    nc.const_aps.register(F32, L2EPS, csts[:, 0:1])
    nc.const_aps.register(F32, EPS, csts[:, 1:2])
    TC = tuple(tc)

    esA = ExitStack()
    SA = esA.enter_context
    xf = SA(nc.sbuf_tensor("xf", [128, 16 * T1], BF16)).ap()
    xb16 = SA(nc.sbuf_tensor("xb16", [128, D], BF16)).ap()
    wqb = SA(nc.sbuf_tensor("wqb", [128, 2 * 2048], BF16)).ap()
    wabb = SA(nc.sbuf_tensor("wabb", [128, 16 * 256], BF16)).ap()
    wabsm = SA(nc.sbuf_tensor("wabsm", [128, 16 * 32], BF16)).ap()
    qpre = SA(nc.sbuf_tensor("qpre", [128, 2 * (T1 + 4)], BF16)).ap()
    tails = SA(nc.sbuf_tensor("tails", [128, 3 * 48], BF16)).ap()
    sq_out = SA(nc.sbuf_tensor("sq_out", [128, 2 * T1], BF16)).ap()
    sgate = SA(nc.sbuf_tensor("sgate", [128, 2 * 256], BF16)).ap()
    sab = SA(nc.sbuf_tensor("sab", [128, 32 * (T1 // 128)], F32)).ap()
    wcv = SA(nc.sbuf_tensor("wcv", [128, 192], F32)).ap()
    diag4 = SA(nc.sbuf_tensor("diag4", [128, 2 * 512], BF16)).ap()
    st1 = SA(nc.sbuf_tensor("st1", [H, 7 * 128], F32)).ap()
    ps_q = SA(nc.psum_tensor("ps_q", [128, 512], F32)).ap()
    ps_s1 = SA(nc.psum_tensor("ps_s1", [128, 512], F32)).ap()
    qc = SA(nc.sbuf_tensor("qc", [128, 2 * 6144], BF16)).ap()
    sgc = SA(nc.sbuf_tensor("sgc", [128, 2 * D], BF16)).ap()
    S_sb = SA(nc.sbuf_tensor("S_sb", [128, H * 128], F32)).ap()
    S_bf = SA(nc.sbuf_tensor("S_bf", [128, H * 128], BF16)).ap()
    sqgb = SA(nc.sbuf_tensor("sqgb", [128, 2048], BF16)).ap()
    indb = SA(nc.sbuf_tensor("indb", [57, D], BF16)).ap()
    lA = SA(nc.sbuf_tensor("lA", [57, 128], BF16)).ap()
    lST = SA(nc.sbuf_tensor("lST", [57, 128], BF16)).ap()
    lST2 = SA(nc.sbuf_tensor("lST2", [57, 128], BF16)).ap()
    mst = SA(nc.sbuf_tensor("mst", [H, 8 * 128], F32)).ap()
    mpc = SA(nc.sbuf_tensor("mpc", [H, 9 * 128], BF16)).ap()
    lnssq = SA(nc.sbuf_tensor("lnssq", [H, 256], F32)).ap()
    colsb = SA(nc.sbuf_tensor("colsb", [128, 2 * 64], F32)).ap()
    gcbb = SA(nc.sbuf_tensor("gcbb", [128, 2 * 16], F32)).ap()
    expd = SA(nc.sbuf_tensor("expd", [128, 512], F32)).ap()
    n0b = SA(nc.sbuf_tensor("n0b", [128, 2048], BF16)).ap()
    kdecb = SA(nc.sbuf_tensor("kdecb", [128, 2 * 2048], BF16)).ap()
    r0b = SA(nc.sbuf_tensor("r0b", [128, 2048], BF16)).ap()
    ntok = SA(nc.sbuf_tensor("ntok", [128, 2 * 512], BF16)).ap()
    nbar = SA(nc.sbuf_tensor("nbar", [128, 2 * 512], BF16)).ap()
    wchn = SA(nc.sbuf_tensor("wchn", [128, 2 * 1024], BF16)).ap()
    trvb = SA(nc.sbuf_tensor("trvb", [128, 2 * 2048], BF16)).ap()
    gtrb = SA(nc.sbuf_tensor("gtrb", [128, 1024], BF16)).ap()
    gbarb = SA(nc.sbuf_tensor("gbarb", [128, 2 * 2048], BF16)).ap()
    wbarb = SA(nc.sbuf_tensor("wbarb", [128, 2 * 2048], BF16)).ap()
    ub = SA(nc.sbuf_tensor("ub", [128, 1024], BF16)).ap()
    qst = SA(nc.sbuf_tensor("qst", [128, 2048], F32)).ap()
    ost = SA(nc.sbuf_tensor("ost", [128, 1024], F32)).ap()
    onb = SA(nc.sbuf_tensor("onb", [128, 1024], BF16)).ap()
    ofst = SA(nc.sbuf_tensor("ofst", [128, 1024], BF16)).ap()
    rms = SA(nc.sbuf_tensor("rms", [128, 48], F32)).ap()
    grow = SA(nc.sbuf_tensor("grow", [1, 16], F32)).ap()
    ghb = SA(nc.sbuf_tensor("ghb", [H, 2 * 256], F32)).ap()
    pCH = SA(nc.psum_tensor("pCH", [128, 1024], F32)).ap()
    pSQ = SA(nc.psum_tensor("pSQ", [128, 1024], F32)).ap()
    pSU = SA(nc.psum_tensor("pSU", [128, 1024], F32)).ap()

    tk_wcv = p.dma("pool", wcv[:, :].rearrange("p (fb j) -> p fb j", j=4),
                   conv_w[:, :].rearrange("(fb p) j -> p fb j", p=128))
    tk_t0 = p.mset("dve", tails[:, :], 0.0)
    s1t = {"tail": {fb: tk_t0 for fb in range(48)}, "fb": {}, "sq": {},
           "gd": {}, "bg": None, "xfree": None, "ps1": None, "tile": {}}
    NTB1 = T1 // 128
    SEGW = min(512, T1)
    NSEG = T1 // SEGW

    def softplus(out_ap, in_ap, tmp_ap, deps, scale=1.0):
        p.act(tmp_ap, in_ap, AF.Abs, scale=scale, deps=deps, fence=False)
        p.act(tmp_ap, tmp_ap, AF.Exp, scale=-1.0, fence=False)
        p.act(tmp_ap, tmp_ap, AF.Ln, bias=1.0, fence=False)
        t1 = p.act(out_ap, in_ap, AF.Relu, scale=scale, deps=deps)
        return p.tt(out_ap, out_ap, tmp_ap, op.add, deps=[t1])

    def wq_dma(ti, fb, deps):
        sl = fb % 2
        return p.dma("pool",
                     wqb[:, sl * 2048:(sl + 1) * 2048].rearrange(
                         "p (kt n) -> p kt n", n=128),
                     w_qkv[:, fb * 128:(fb + 1) * 128].rearrange(
                         "(kt p) n -> p kt n", p=128), deps=deps)

    def s1_gen(ti):
        t0 = ti * T1
        FB = s1t["fb"]
        SQ = s1t["sq"]
        tk_xtr = []
        tkms = s1t.get("xlast", [None] * 4)
        for tb in range(NTB1):
            tk_xs = [p.dma("pool", xb16[:, hf * 512:(hf + 1) * 512],
                           x_in[t0 + tb * 128: t0 + (tb + 1) * 128,
                                hf * 512:(hf + 1) * 512],
                           deps=[s1t["xfree"], tkms[hf]])
                     for hf in range(4)]
            tkms = []
            for g4 in range(4):
                prev = tk_xtr[-1] if tk_xtr else s1t["ps1"]
                tkm = None
                for j in range(4):
                    dblk = g4 * 4 + j
                    tkm = p.mm(ps_s1[:, j * 128:(j + 1) * 128],
                               xb16[:, dblk * 128:(dblk + 1) * 128],
                               ident_b[:, :], True, True,
                               deps=[tk_xs[g4], prev] + list(TC),
                               fence=(j == 3))
                tkms.append(tkm)
                tk_xtr.append(p.cp(
                    "dve",
                    xf[:, :].rearrange("p (b tt) -> p b tt", b=16)
                    [:, g4 * 4:(g4 + 1) * 4, tb * 128:(tb + 1) * 128],
                    ps_s1[:, :].rearrange("p (b tt) -> p b tt", b=4)[:, :, :],
                    deps=[tkm]))
            yield
        s1t["xlast"] = tkms
        tk_xf = p.nopf("dve", deps=tk_xtr)

        if ti == 0:
            FB[(0, -2)] = FB[(0, -1)] = FB[(-1, 46)] = FB[(-1, 47)] = None
            s1t["wq"] = {0: wq_dma(0, 0, [tk_wcv]), 1: wq_dma(0, 1, [tk_wcv])}
        WQ = s1t["wq"]
        pst = {}
        for fbx in range(49):
            if fbx < 48:
                fb = fbx
                sl = fb % 2
                bank = ps_q if sl == 0 else ps_s1
                qp = qpre[:, sl * (T1 + 4):(sl + 1) * (T1 + 4)]
                dg = diag4[:, sl * 512:(sl + 1) * 512]
                fprev = FB.get((ti, fb - 2), FB.get((ti - 1, fb + 46)))
                tkm = None
                for kt in range(16):
                    tkm = p.mm(bank[:, 0:SEGW],
                               wqb[:, sl * 2048 + kt * 128:
                                   sl * 2048 + (kt + 1) * 128],
                               xf[:, kt * T1: kt * T1 + SEGW],
                               kt == 0, kt == 15,
                               deps=[WQ[fb], tk_xf, fprev],
                               fence=(kt == 15))
                tk_ev = p.act(qp[:, 3: 3 + SEGW], bank[:, 0:SEGW], AF.Copy,
                              deps=[tkm])
                nfb, nti = fb + 2, ti
                if nfb >= 48:
                    nfb -= 48
                    nti += 1
                if nti < NT1:
                    WQ[nfb] = wq_dma(nti, nfb, [tkm, tk_wcv])
                tk_tl = p.cp("dve", qp[:, 0:3], tails[:, fb * 3:(fb + 1) * 3],
                             deps=[s1t["tail"][fb], fprev])
                s1t["tail"][fb] = p.cp("dve", tails[:, fb * 3:(fb + 1) * 3],
                                       qp[:, T1: T1 + 3], deps=[tk_ev, tk_tl])
                tkd = None
                for j in range(4):
                    tkd = p.ts(dg[:, j * 128:(j + 1) * 128], ident_b[:, :],
                               wcv[:, fb * 4 + j: fb * 4 + j + 1], None,
                               op.mult, deps=[tk_wcv, fprev], fence=(j == 3))
                pst[fb] = (bank, qp, dg, tk_ev, tk_tl, tkd)
            if fbx >= 1:
                fb = fbx - 1
                sl = fb % 2
                bank, qp, dg, tk_ev, tk_tl, tkd = pst.pop(fb)
                tkm = None
                for j in range(4):
                    tkm = p.mm(bank[:, 0:SEGW], dg[:, j * 128:(j + 1) * 128],
                               qp[:, j: j + SEGW], j == 0, j == 3,
                               deps=[tkd, tk_ev, tk_tl], fence=(j == 3))
                tk_cv = p.act(sq_out[:, sl * T1: sl * T1 + SEGW],
                              bank[:, 0:SEGW], AF.Silu,
                              deps=[tkm, SQ.get((ti, fb - 2),
                                                SQ.get((ti - 1, fb + 46)))])
                FB[(ti, fb)] = tk_cv
                if fb < 16:
                    blk = fb * 3 + 1
                elif fb < 32:
                    blk = (fb - 16) * 3
                else:
                    blk = (fb - 32) * 3 + 2
                c0 = t0 // 128
                SQ[(ti, fb)] = p.dma("pool",
                                     qkc_s[c0:c0 + NTB1, blk, :, :].rearrange(
                                         "cc pp tt -> pp cc tt"),
                                     sq_out[:, sl * T1:(sl + 1) * T1].rearrange(
                                         "pp (cc tt) -> pp cc tt", tt=128),
                                     deps=[tk_cv])
            yield

        if ti == 0:
            s1t["wsm"] = [p.dma("pool", wabsm[:, kt * 32:(kt + 1) * 32],
                                w_ab[kt * 128:(kt + 1) * 128, 0:32])
                          for kt in range(16)]
        tk_absm = []
        for tb in range(NTB1):
            tkm = None
            for kt in range(16):
                tkm = p.mm(ps_s1[:, 0:32],
                           xf[:, kt * T1 + tb * 128: kt * T1 + (tb + 1) * 128],
                           wabsm[:, kt * 32:(kt + 1) * 32],
                           kt == 0, kt == 15,
                           deps=s1t["wsm"] + [tk_xf, s1t["bg"],
                                              FB.get((ti, 47)),
                                              tk_absm[tb - 1] if tb >= 1 else
                                              tk_xtr[-1]],
                           fence=(kt == 15))
            tk_absm.append(p.cp("dve", sab[:, tb * 32:(tb + 1) * 32],
                                ps_s1[:, 0:32], deps=[tkm]))
            yield

        GD = s1t["gd"]

        def gsk(idx):
            return GD.get(("s", ti, idx // NTB1, idx % NTB1))

        for nch in range(8):
            ncol0 = 32 + nch * 256
            tk_wab = p.dma("pool", wabb[:, :].rearrange("p (kt n) -> p kt n", n=256),
                           w_ab[:, ncol0:ncol0 + 256].rearrange(
                               "(kt p) n -> p kt n", p=128),
                           deps=[GD.get(("mm", ti, nch - 1),
                                        GD.get(("mm", ti - 1, 7)))])
            for tb in range(NTB1):
                sl = tb % 2
                idx = nch * NTB1 + tb
                gbank = ps_q if idx % 2 == 0 else ps_s1
                gprev = (gsk(idx - 2) if idx >= 2 else
                         FB.get((ti, 46 + idx % 2)))
                tkm = None
                for kt in range(16):
                    tkm = p.mm(gbank[:, 0:256],
                               xf[:, kt * T1 + tb * 128: kt * T1 + (tb + 1) * 128],
                               wabb[:, kt * 256:(kt + 1) * 256],
                               kt == 0, kt == 15,
                               deps=[tk_wab, tk_xf, tk_absm[-1], gprev],
                               fence=(kt == 15))
                GD[("mm", ti, nch)] = tkm
                tk_si = p.act(sgate[:, sl * 256:(sl + 1) * 256],
                              gbank[:, 0:256], AF.Silu,
                              deps=[tkm, GD.get(("d", ti, nch, tb - 2),
                                                GD.get(("d", ti, nch - 1, tb + NTB1 - 2),
                                                       GD.get(("d", ti - 1, 7, tb + NTB1 - 2))))])
                tk_sd = p.dma("sp", sg_s[t0 + tb * 128: t0 + (tb + 1) * 128,
                                         ncol0 - 32: ncol0 - 32 + 256],
                              sgate[:, sl * 256:(sl + 1) * 256], deps=[tk_si])
                GD[("s", ti, nch, tb)] = tk_si
                GD[("d", ti, nch, tb)] = tk_sd
                yield

        aT = st1[:, 0:128]
        bbT = st1[:, 128:256]
        u_ = st1[:, 256:384]
        tmp_ = st1[:, 384:512]
        sp_ = st1[:, 512:640]
        gsl = st1[:, 640:768]
        lsl = st1[:, 768:896]
        tk_bg = s1t["bg"]
        tk_gd = []
        tk_gsil = GD[("s", ti, 7, NTB1 - 1)]
        for tb in range(NTB1):
            tka1 = p.tr(ps_s1[0:16, 0:128], sab[:, tb * 32: tb * 32 + 16],
                        ident_f[:, :], deps=[tk_absm[tb], tk_bg, tk_gsil])
            tc1 = p.cp("dve", aT, ps_s1[0:16, 0:128], deps=[tka1])
            tka2 = p.tr(ps_s1[0:16, 128:256], sab[:, tb * 32 + 16: tb * 32 + 32],
                        ident_f[:, :], deps=[tka1, tc1])
            tc2 = p.cp("dve", bbT, ps_s1[0:16, 128:256], deps=[tka2])
            tk_u = p.ts(u_, aT, dtb[:, 0:1], None, op.add, deps=[tc1])
            tsp = softplus(sp_, u_, tmp_, deps=[tk_u])
            tk_g = p.ts(gsl, sp_, nega[:, 0:1], None, op.mult,
                        deps=[tsp, tk_gd[-2] if len(tk_gd) >= 2 else
                              s1t.get("gd2")])
            tsp2 = softplus(sp_, bbT, tmp_, deps=[tc2, tk_g], scale=-1.0)
            tk_bg = p.ts(lsl, sp_, -1.0, LN2, op.mult, op.add,
                         deps=[tsp2, tk_gd[-1] if tk_gd else s1t.get("gd1")])
            tk_gd.append(p.dma("sp", gH_d[:, t0 + tb * 128:t0 + (tb + 1) * 128],
                               gsl, deps=[tk_g]))
            tk_gd.append(p.dma("sp", lnbH_d[:, t0 + tb * 128:t0 + (tb + 1) * 128],
                               lsl, deps=[tk_bg]))
            yield
        s1t["bg"] = tk_bg
        s1t["gd2"], s1t["gd1"] = tk_gd[-2], tk_gd[-1]
        s1t["ps1"] = tk_bg
        s1t["tile"][(ti, "gdump")] = p.nopf("sp", deps=tk_gd)
        s1t["tile"][(ti, "done")] = p.nopf(
            "sp", deps=[SQ[(ti, fb)] for fb in range(48)] +
            [GD[("d", ti, nch, tb)] for nch in range(8) for tb in range(NTB1)] +
            [s1t["tile"][(ti, "gdump")]])
        s1t["xfree"] = GD[("d", ti, 7, NTB1 - 1)]

    s1_q = [(ti, s1_gen(ti)) for ti in range(NT1)]
    s1_max = [1]
    s1_ctr = [0.0, 0]
    S1_RATE = (NT1 * 93.0) / max(NCH * 41.0, 1.0)

    def s1_drive(n=1):
        s1_ctr[0] += S1_RATE
        while s1_ctr[1] < s1_ctr[0] + 12 and s1_q:
            if s1_q[0][0] > s1_max[0]:
                return
            try:
                next(s1_q[0][1])
                s1_ctr[1] += 1
            except StopIteration:
                s1_q.pop(0)

    def s1_finish(ti):
        while s1_q and s1_q[0][0] <= ti:
            try:
                next(s1_q[0][1])
            except StopIteration:
                s1_q.pop(0)

    tk_Sw = {h: None for h in range(H)}
    Ldict = {}
    MSL = dict(gc=0, m1=1, m2=2, m3=3, m4=4, tmp=5, exG=6)

    def msl(nm):
        i = MSL[nm]
        return mst[:, i * 128:(i + 1) * 128]

    tk_s2_init = p.mset("dve", S_sb[:, :], 0.0, fence=False)
    p.mset("dve", S_bf[:, :], 0.0, fence=False)
    p.mset("dve", lA[0:57, :], 0.0, fence=False)
    p.mset("dve", lST[0:57, :], 0.0, fence=False)
    tk_lz = p.mset("dve", lST2[0:57, :], 0.0)
    tk_s2_init = p.nopf("dve", deps=[
        p.dma("pool", lA[48:51, :], ones16[0:3, :], deps=[tk_lz]),
        p.dma("pool", lST[51:54, :], ones16[0:3, :], deps=[tk_lz]),
        p.dma("pool", lST2[54:57, :], ones16[0:3, :], deps=[tk_lz])])
    for h in range(H):
        tk_Sw[h] = tk_s2_init
    tk_ind = p.dma("pool", indb[0:48, :], c_ind[:, :])
    tk_qc = {}
    tk_sg = {}

    def emit_local(cc):
        par = cc % 2
        qcs = qc[:, par * 6144:(par + 1) * 6144]
        cols = colsb[:, par * 64:(par + 1) * 64]
        gcb = gcbb[:, par * 16:(par + 1) * 16]
        prevL = Ldict.get(cc - 1, {})
        out = {}
        ti_c = cc // CPT
        if cc not in tk_qc:
            tk_qc[cc] = p.dma("sp",
                              qc[:, par * 6144:(par + 1) * 6144].rearrange(
                                  "pp (b t) -> pp b t", t=128),
                              qkc_s[cc, :, :, :].rearrange("b pp t -> pp b t"),
                              deps=[s1t["tile"][(ti_c, "done")],
                                    Ldict.get(cc - 2, {}).get("qc_free")])
        cn = cc + 1
        if cn < NCH and cn not in tk_qc and (cn // CPT, "done") in s1t["tile"]:
            tk_qc[cn] = p.dma("sp",
                              qc[:, (1 - par) * 6144:(2 - par) * 6144].rearrange(
                                  "pp (b t) -> pp b t", t=128),
                              qkc_s[cn, :, :, :].rearrange("b pp t -> pp b t"),
                              deps=[s1t["tile"][(cn // CPT, "done")],
                                    prevL.get("qc_free")])
        tk_gl = p.dma("sp", ghb[:, par * 256: par * 256 + 128],
                      gH_d[:, cc * 128:(cc + 1) * 128],
                      deps=[s1t["tile"][(ti_c, "done")],
                            Ldict.get(cc - 2, {}).get("ghb_free")])
        tk_ll = p.dma("sp", ghb[:, par * 256 + 128: par * 256 + 256],
                      lnbH_d[:, cc * 128:(cc + 1) * 128],
                      deps=[s1t["tile"][(ti_c, "done")],
                            Ldict.get(cc - 2, {}).get("ghb_free")])
        tk_gc = p.raw("dve", lambda v, d=msl("gc"), o=ones16[:, :],
                      g_=ghb[:, par * 256: par * 256 + 128]:
                      v.tensor_tensor_scan(d, o, g_, 0.0, op.mult, op.add),
                      deps=[tk_s2_init, prevL.get("mst_free"), tk_gl])
        s1_drive()
        tkm = None
        tk_prevhalf = None
        for gg in range(2):
            tk_k2 = p.act(sqgb[:, :].rearrange("p (h kq t) -> p h kq t", h=8, kq=2)
                          [:, :, 0, :],
                          qcs.rearrange("p (h z t) -> p h z t", h=H, z=3)
                          [:, gg * 8:(gg + 1) * 8, 0, :],
                          AF.Square, deps=[tk_qc[cc], tk_prevhalf,
                                           prevL.get("gram_free")])
            tk_q2 = p.act(sqgb[:, :].rearrange("p (h kq t) -> p h kq t", h=8, kq=2)
                          [:, :, 1, :],
                          qcs.rearrange("p (h z t) -> p h z t", h=H, z=3)
                          [:, gg * 8:(gg + 1) * 8, 1, :],
                          AF.Square, deps=[tk_qc[cc]])
            for hh in range(8):
                h = gg * 8 + hh
                tkm = p.mm(pSQ[0:16, 0:256],
                           oh_b[:, h * 16:(h + 1) * 16],
                           sqgb[:, hh * 256:(hh + 1) * 256],
                           h == 0, h == 15,
                           deps=[tk_k2, tk_q2, prevL.get("sqbank_free")],
                           fence=(hh == 7))
            tk_prevhalf = tkm
            s1_drive()
        tk_ln = p.act(lnssq[:, :], pSQ[0:16, 0:256], AF.Ln, bias=L2EPS,
                      deps=[tkm])
        p.stt(msl("m1"), lnssq[:, 0:128],
              -0.5, msl("gc"), op.mult, op.add, deps=[tk_ln, tk_gc], fence=False)
        tk_m2a = p.stt(msl("tmp"), lnssq[:, 0:128], -0.5,
                       ghb[:, par * 256 + 128: par * 256 + 256],
                       op.mult, op.add, deps=[tk_ln, tk_ll])
        tk_m2b = p.tt(msl("m2"), msl("tmp"), msl("gc"), op.subtract)
        out["ghb_free"] = tk_m2b
        p.stt(msl("tmp"), lnssq[:, 128:256], -0.5, msl("gc"), op.mult, op.add,
              fence=False)
        p.ts(msl("m3"), msl("tmp"), LN_HALF_DK, None, op.add, fence=False)
        tk_m = p.ts(msl("m4"), msl("m2"), msl("gc")[:, 127:128], None, op.add)
        out["mst_done"] = tk_m
        tk_pc = []
        for mi, nm in enumerate(("m1", "m2", "m3")):
            b0 = mi * 3
            p.cp("dve", mpc[:, b0 * 128:(b0 + 1) * 128], msl(nm), deps=[tk_m],
                 fence=False)
            p.tt(msl("tmp"), msl(nm), mpc[:, b0 * 128:(b0 + 1) * 128], op.subtract,
                 fence=False)
            p.cp("dve", mpc[:, (b0 + 1) * 128:(b0 + 2) * 128], msl("tmp"), fence=False)
            p.tt(msl("tmp"), msl("tmp"), mpc[:, (b0 + 1) * 128:(b0 + 2) * 128],
                 op.subtract, fence=False)
            tk_pc.append(p.cp("dve", mpc[:, (b0 + 2) * 128:(b0 + 3) * 128],
                              msl("tmp")))
        tk_pc = tk_pc[-1]
        tk_asm = []
        for j in range(3):
            tk_asm.append(p.dma("sp", lA[16 * j:16 * (j + 1), :],
                                mpc[:, j * 128:(j + 1) * 128],
                                deps=[tk_pc, prevL.get("lhs_free"), tk_ind]))
            tk_asm.append(p.dma("sp", lST[16 * j:16 * (j + 1), :],
                                mpc[:, (3 + j) * 128:(4 + j) * 128],
                                deps=[tk_pc, prevL.get("lhs_free")]))
            tk_asm.append(p.dma("pool", lST2[16 * j:16 * (j + 1), :],
                                mpc[:, (3 + j) * 128:(4 + j) * 128],
                                deps=[tk_pc, prevL.get("lhs_free")]))
        for oi, mb in enumerate((3, 0, 6)):
            for j in range(3):
                tk_asm.append(p.dma("pool", indb[48 + oi * 3 + j:
                                                 49 + oi * 3 + j, :],
                                    mpc[:, (mb + j) * 128:(mb + j + 1) * 128],
                                    deps=[tk_pc, prevL.get("lhs_free"),
                                          tk_ind]))
        if locstop == 1:
            Ldict[cc] = out
            return out
        tk_tr1 = p.tr(pSQ[:, 512:528], msl("m1"), ident_f[0:16, 0:16],
                      deps=[tk_m, prevL.get("sqbank_free2")])
        tk_tr2 = p.tr(pSQ[:, 528:544], msl("m4"), ident_f[0:16, 0:16], deps=[tk_m])
        tk_tr3 = p.tr(pSQ[:, 544:560], msl("m3"), ident_f[0:16, 0:16], deps=[tk_m])
        tk_cols = p.act(cols[:, 0:48], pSQ[:, 512:560], AF.Exp,
                        deps=[tk_tr1, tk_tr2, tk_tr3, prevL.get("cols_free")])
        out["cols"] = tk_cols
        s1_drive()
        tk_xg = p.act(msl("exG")[:, 0:1], msl("gc")[:, 127:128], AF.Exp,
                      deps=[tk_gc])
        tk_eg = p.tr(pSQ[0:1, 560:576], msl("exG")[:, 0:1], ident_f[0:16, 0:16],
                     deps=[tk_xg, tk_cols])
        tk_egc = p.cp("dve", grow[0:1, 0:16], pSQ[0:1, 560:576], deps=[tk_eg])
        tk_gcb = p.mm(pSQ[:, 576:592], ones1f[:, :], grow[0:1, 0:16], True, True,
                      deps=[tk_egc], fence=True)
        out["gcb"] = p.cp("dve", gcb[:, 0:16], pSQ[:, 576:592],
                          deps=[tk_gcb, prevL.get("gcb_free")])
        s1_drive()

        if locstop == 2:
            Ldict[cc] = out
            return out
        tk_last_ch = prevL.get("ch_last")
        tk_w7 = []
        tk_gbar = []
        tk_sq_last = None
        for g in range(2):
            h0 = g * 8

            def kqv(hh):
                b = ((h0 + hh) * 3) * 128
                return (qcs[:, b + 128:b + 256], qcs[:, b:b + 128],
                        qcs[:, b + 256:b + 384])

            tkmh = [None, None]
            for hh in range(8):
                q_, k_, v_ = kqv(hh)
                tkmh[hh // 4] = p.mm(pCH[:, hh * 128:(hh + 1) * 128], k_,
                                     ident_b[:, :], True, True,
                                     deps=[tk_qc[cc], tk_cols, out["gcb"],
                                           tk_last_ch],
                                     fence=(hh % 4 == 3))
            tk_ktil = []
            for hh in range(8):
                h = h0 + hh
                tk_ktil.append(p.ts(r0b[:, hh * 256 + 128: hh * 256 + 256],
                                    pCH[:, hh * 128:(hh + 1) * 128],
                                    cols[:, h:h + 1], -1.0, op.mult, op.mult,
                                    deps=[tkmh[hh // 4],
                                          prevL.get("r0_free") if g == 0
                                          else None]))
                tk_ktil.append(p.ts(kdecb[:, par * 2048 + h * 128:
                                          par * 2048 + (h + 1) * 128],
                                    pCH[:, hh * 128:(hh + 1) * 128],
                                    cols[:, 16 + h:16 + h + 1], None, op.mult,
                                    deps=[tkmh[hh // 4],
                                          prevL.get("kdec_free")]))
            tk_ktil = p.nopf("dve", deps=tk_ktil)
            s1_drive()
            if locstop == 3:
                Ldict[cc] = out
                return out
            tkg4 = [None] * 4
            for hh in range(8):
                q_, k_, v_ = kqv(hh)
                dst = pCH if hh < 4 else pSQ
                off = (hh % 4) * 256
                b = ((h0 + hh) * 3) * 128
                tkg4[hh // 2] = p.mm(dst[:, off:off + 256], k_,
                                     qcs[:, b:b + 256], True, True,
                                     deps=[tk_ktil, tk_qc[cc], tk_sq_last,
                                           tk_ln],
                                     fence=(hh % 2 == 1))
            tk_ge = [p.cp("dve", sqgb[:, qb * 512:(qb + 1) * 512],
                          (pCH if qb < 2 else pSQ)[:, (qb % 2) * 512:
                                                   (qb % 2 + 1) * 512],
                          deps=[tkg4[qb]]) for qb in range(4)]
            tk_gram = p.nopf("dve", deps=tk_ge)
            s1_drive()
            if locstop == 4:
                Ldict[cc] = out
                return out

            def diff_build(orient, mask_i, out_sb, sign, kq, mm_deps,
                           exp_deps):
                lhs = (lA, lST, lST2)[orient]
                tkk_h = [None, None]
                for half in range(2):
                    p.mm(pCH[:, half * 512:(half + 1) * 512], lhs[:, :],
                         indb[:, g * 1024 + half * 512:
                              g * 1024 + (half + 1) * 512],
                         True, False,
                         deps=list(mm_deps) + tk_asm + [tk_ind],
                         fence=False)
                    tkk_h[half] = p.mm(
                        pCH[:, half * 512:(half + 1) * 512], ident_b[:, :],
                        masks_b[:, mask_i * 1024 + half * 512:
                                mask_i * 1024 + (half + 1) * 512],
                        False, True, deps=[], fence=True)
                tks = []
                tke = None
                for half in range(2):
                    tke = p.act(expd[:, :], pCH[:, half * 512:(half + 1) * 512],
                                AF.Exp,
                                deps=[tkk_h[half],
                                      tks[-1] if tks else None] +
                                list(exp_deps))
                    tks.append(p.stt(
                        out_sb.rearrange("p (hh s) -> p hh s", hh=8)
                        [:, half * 4:(half + 1) * 4, :],
                        expd[:, :].rearrange("p (hh s) -> p hh s", hh=4),
                        float(sign),
                        sqgb[:, :].rearrange("p (hh kq s) -> p hh kq s", hh=8, kq=2)
                        [:, half * 4:(half + 1) * 4, kq, :],
                        op.mult, op.mult, deps=[tke, tk_gram]))
                return p.nopf("dve", deps=tks), tkk_h[1], tke

            tk_n0, tke0, txp0 = diff_build(0, 0, n0b[:, 0:1024], -1.0, 0,
                                           [tk_gram], [tk_gram])
            s1_drive()
            tk_nb0, tke1, txp1 = diff_build(1, 1, n0b[:, 1024:2048], -1.0, 0,
                                            [txp0], [tk_n0])
            s1_drive()
            tk_wb, tke2, txp2 = diff_build(2, 2,
                                           wbarb[:, par * 2048 + g * 1024:
                                                 par * 2048 + (g + 1) * 1024],
                                           1.0, 1, [txp1], [tk_nb0])
            out[("wbar", g)] = tk_wb
            out["lhs_free"] = tke2
            if locstop == 5:
                Ldict[cc] = out
                return out

            tkmh = [None, None]
            for hh in range(8):
                q_, k_, v_ = kqv(hh)
                tkmh[hh // 4] = p.mm(pCH[:, hh * 128:(hh + 1) * 128], v_,
                                     ident_b[:, :], True, True,
                                     deps=[tk_qc[cc], txp2, tk_wb],
                                     fence=(hh % 4 == 3))
            tk_r0 = [p.cp("dve",
                          r0b[:, bh * 1024:(bh + 1) * 1024]
                          .rearrange("p (hh vk t) -> p hh vk t",
                                     hh=4, vk=2)[:, :, 0, :],
                          pCH[:, bh * 512:(bh + 1) * 512]
                          .rearrange("p (hh t) -> p hh t", hh=4),
                          deps=[tkmh[bh]]) for bh in range(2)]
            tk_r0 = p.nopf("dve", deps=tk_r0)
            s1_drive()
            if locstop == 6:
                Ldict[cc] = out
                return out

            for ps in range(2):
                hh0 = ps * 4
                tk_w0 = p.nopf("dve", deps=[tk_r0, tk_w7[-1] if tk_w7 else None])
                tk_wh = [tk_w0, tk_w0]
                tk_ncur = p.nopf("act", deps=[tk_n0, tk_sq_last])
                tk_bcur = tk_nb0
                ncur = 0
                for lev in range(7):
                    tkh = [None, None]
                    for half in range(2):
                        for hh4 in (half * 2, half * 2 + 1):
                            hh = hh0 + hh4
                            lhsT = (n0b[:, 1024 + hh * 128:1024 + (hh + 1) * 128]
                                    if lev == 0 else
                                    nbar[:, ncur * 512 + hh4 * 128:
                                         ncur * 512 + (hh4 + 1) * 128])
                            rhs = (r0b[:, hh * 256:(hh + 1) * 256] if lev == 0
                                   else wchn[:, (lev % 2) * 1024 + hh4 * 256:
                                             (lev % 2) * 1024 + (hh4 + 1) * 256])
                            p.mm(pCH[:, hh4 * 256:(hh4 + 1) * 256], lhsT, rhs,
                                 hh4 % 2 == 0, False,
                                 deps=[tk_wh[half], tk_bcur], fence=False)
                        for hh4 in (half * 2, half * 2 + 1):
                            hh = hh0 + hh4
                            rhs = (r0b[:, hh * 256:(hh + 1) * 256] if lev == 0
                                   else wchn[:, (lev % 2) * 1024 + hh4 * 256:
                                             (lev % 2) * 1024 + (hh4 + 1) * 256])
                            tkh[half] = p.mm(
                                pCH[:, hh4 * 256:(hh4 + 1) * 256],
                                ident_b[:, :], rhs, False, hh4 % 2 == 1,
                                deps=[], fence=(hh4 == half * 2 + 1))
                    if lev == 6:
                        base = par * 2048 + g * 1024 + hh0 * 128
                        tvl = []
                        for h in range(2):
                            tvl.append(p.cp(
                                "dve",
                                trvb[:, base + h * 256: base + h * 256 + 256]
                                .rearrange("p (hh t) -> p hh t", hh=2),
                                pCH[:, h * 512:(h + 1) * 512].rearrange(
                                    "p (hh vk t) -> p hh vk t", hh=2, vk=2)
                                [:, :, 0, :],
                                deps=[tkh[h],
                                      Ldict.get(cc - 1, {}).get("trv_free")
                                      if h == 0 else None]))
                        tgl = []
                        for h in range(2):
                            tgl.append(p.cp(
                                "act",
                                gtrb[:, hh0 * 128 + h * 256:
                                     hh0 * 128 + h * 256 + 256]
                                .rearrange("p (hh t) -> p hh t", hh=2),
                                pCH[:, h * 512:(h + 1) * 512].rearrange(
                                    "p (hh vk t) -> p hh vk t", hh=2, vk=2)
                                [:, :, 1, :],
                                deps=[tkh[h], tvl[h]]))
                        tk_wcur = p.nopf("dve", deps=tvl + tgl)
                    else:
                        ce = "dve" if lev % 2 == 0 else "act"
                        nxs = ((lev + 1) % 2) * 1024
                        tk_wh = [p.cp(ce, wchn[:, nxs + h * 512:
                                               nxs + (h + 1) * 512],
                                      pCH[:, h * 512:(h + 1) * 512],
                                      deps=[tkh[h]]) for h in range(2)]
                    if lev < 6:
                        tks = None
                        for hh4 in range(4):
                            hh = hh0 + hh4
                            lhsT = (n0b[:, 1024 + hh * 128:1024 + (hh + 1) * 128]
                                    if lev == 0 else
                                    nbar[:, ncur * 512 + hh4 * 128:
                                         ncur * 512 + (hh4 + 1) * 128])
                            rhsn = (n0b[:, hh * 128:(hh + 1) * 128] if lev == 0
                                    else ntok[:, ncur * 512 + hh4 * 128:
                                              ncur * 512 + (hh4 + 1) * 128])
                            tks = p.mm(pSQ[:, hh4 * 128:(hh4 + 1) * 128],
                                       lhsT, rhsn, True, True,
                                       deps=[tk_ncur, tk_bcur],
                                       fence=(hh4 == 3))
                        tksA = tks
                        for hh4 in range(4):
                            hh = hh0 + hh4
                            lhsTn = (n0b[:, hh * 128:(hh + 1) * 128] if lev == 0
                                     else ntok[:, ncur * 512 + hh4 * 128:
                                               ncur * 512 + (hh4 + 1) * 128])
                            rhsb = (n0b[:, 1024 + hh * 128:1024 + (hh + 1) * 128]
                                    if lev == 0 else
                                    nbar[:, ncur * 512 + hh4 * 128:
                                         ncur * 512 + (hh4 + 1) * 128])
                            tks = p.mm(pSQ[:, 512 + hh4 * 128:
                                           512 + (hh4 + 1) * 128],
                                       lhsTn, rhsb, True, True,
                                       deps=[], fence=(hh4 == 3))
                        nnx = 1 - ncur
                        cn = "act" if lev % 2 == 0 else "dve"
                        tk_ncur = p.cp(cn, ntok[:, nnx * 512:(nnx + 1) * 512],
                                       pSQ[:, 0:512], deps=[tksA])
                        tk_bcur = p.cp(cn, nbar[:, nnx * 512:(nnx + 1) * 512],
                                       pSQ[:, 512:1024], deps=[tks])
                        tk_sq_last = tk_bcur
                        ncur = nnx
                    s1_drive()
                    if locstop == 60 + ps * 10 + lev:
                        Ldict[cc] = out
                        return out
                tk_w7.append(tk_wcur)
            if locstop == 7:
                Ldict[cc] = out
                return out

            tkmh = [None, None]
            for hh in range(8):
                tkmh[hh // 4] = p.mm(pCH[:, hh * 128:(hh + 1) * 128],
                                     gtrb[:, hh * 128:(hh + 1) * 128],
                                     ident_b[:, :], True, True,
                                     deps=tk_w7[-2:], fence=(hh % 4 == 3))
            for bh in range(2):
                tk_gbar.append(p.cp(
                    "dve",
                    gbarb[:, par * 2048 + g * 1024 + bh * 512:
                          par * 2048 + g * 1024 + (bh + 1) * 512],
                    pCH[:, bh * 512:(bh + 1) * 512],
                    deps=[tkmh[bh], prevL.get("gbar_free")]))
            tk_last_ch = tk_gbar[-1]
            s1_drive()
            if locstop == 8:
                Ldict[cc] = out
                return out

        out["w7"] = p.nopf("dve", deps=tk_w7)
        out["gbar"] = p.nopf("dve", deps=tk_gbar)
        out["gram_free"] = out[("wbar", 1)]
        out["kdec"] = tk_ktil
        out["ch_last"] = tk_last_ch
        out["sqbank_free"] = tk_sq_last
        out["sqbank_free2"] = tk_sq_last
        out["mst_free"] = out["cols"]
        Ldict[cc] = out
        return out

    def emit_serial(cc):
        par = cc % 2
        qcs = qc[:, par * 6144:(par + 1) * 6144]
        cols = colsb[:, par * 64:(par + 1) * 64]
        gcb = gcbb[:, par * 16:(par + 1) * 16]
        lc = Ldict[cc]
        out = {}
        tk_qsl = []
        tk_o = []
        tk_sup = []
        for g in range(2):
            h0 = g * 8
            tkmh = [None, None]
            for hh in range(8):
                h = h0 + hh
                tkmh[hh // 4] = p.mm(
                    pSU[:, hh * 128:(hh + 1) * 128],
                    gbarb[:, par * 2048 + h * 128:par * 2048 + (h + 1) * 128],
                    S_bf[:, h * 128:(h + 1) * 128], True, True,
                    deps=[lc["gbar"], tk_Sw[h],
                          tk_o[-1] if tk_o else None,
                          tk_sup[-1] if tk_sup else
                          Ldict.get(cc - 1, {}).get("su_free")],
                    fence=(hh % 4 == 3))
            tk_ueh = []
            for bh in range(2):
                tk_ueh.append(p.stt(
                    ub[:, bh * 512:(bh + 1) * 512]
                    .rearrange("p (hh t) -> p hh t", hh=4),
                    pSU[:, bh * 512:(bh + 1) * 512]
                    .rearrange("p (hh t) -> p hh t", hh=4), 1.0,
                    trvb[:, par * 2048 + g * 1024 + bh * 512:
                         par * 2048 + g * 1024 + (bh + 1) * 512]
                    .rearrange("p (hh t) -> p hh t", hh=4),
                    op.mult, op.add, deps=[tkmh[bh], lc["w7"]]))
            tk_ue = tk_ueh[-1]
            out[("u", g)] = tk_ue
            s1_drive()
            tkqh = [None, None]
            for hh in range(8):
                q_ = qcs[:, ((h0 + hh) * 3) * 128 + 128:
                         ((h0 + hh) * 3) * 128 + 256]
                h = h0 + hh
                tkqh[hh // 4] = p.mm(pSU[:, hh * 128:(hh + 1) * 128], q_,
                                     S_bf[:, h * 128:(h + 1) * 128], True, True,
                                     deps=[tk_Sw[h], tk_ueh[hh // 4]],
                                     fence=(hh % 4 == 3))
            tqs = []
            for hh in range(8):
                h = h0 + hh
                tqs.append(p.ts(qst[:, g * 1024 + hh * 128:
                                    g * 1024 + (hh + 1) * 128],
                                pSU[:, hh * 128:(hh + 1) * 128],
                                cols[:, 32 + h:32 + h + 1], None, op.mult,
                                deps=[tkqh[hh // 4]]))
            tk_qs = p.nopf("dve", deps=tqs)
            tk_qsl.append(tk_qs)
            s1_drive()
            tkm4h = [None, None]
            for hh in range(8):
                h = h0 + hh
                tkm4h[hh // 4] = p.mm(
                    pSU[:, hh * 128:(hh + 1) * 128],
                    kdecb[:, par * 2048 + h * 128:par * 2048 + (h + 1) * 128],
                    ub[:, hh * 128:(hh + 1) * 128], True, True,
                    deps=[tk_ueh[hh // 4], lc["kdec"], tqs[hh // 4 * 4 + 3]],
                    fence=(hh % 4 == 3))
            for hh in range(8):
                h = h0 + hh
                tk_su = p.stt(S_sb[:, h * 128:(h + 1) * 128],
                              S_sb[:, h * 128:(h + 1) * 128],
                              gcb[:, h:h + 1], pSU[:, hh * 128:(hh + 1) * 128],
                              op.mult, op.add, deps=[tkm4h[hh // 4], lc["gcb"]])
                tk_Sw[h] = p.cp("dve", S_bf[:, h * 128:(h + 1) * 128],
                                S_sb[:, h * 128:(h + 1) * 128], deps=[tk_su])
                tk_sup.append(tk_Sw[h])
            s1_drive()
            tk3h = [None, None]
            for hh in range(8):
                h = h0 + hh
                tk3h[hh // 4] = p.mm(
                    pSU[:, hh * 128:(hh + 1) * 128],
                    wbarb[:, par * 2048 + h * 128:par * 2048 + (h + 1) * 128],
                    ub[:, hh * 128:(hh + 1) * 128], True, True,
                    deps=[tk_ueh[hh // 4], lc[("wbar", g)],
                          tk_sup[g * 8 + hh // 4 * 4 + 3]],
                    fence=(hh % 4 == 3))
            for bh in range(2):
                tk_o.append(p.stt(
                    qst[:, g * 1024 + bh * 512:g * 1024 + (bh + 1) * 512],
                    qst[:, g * 1024 + bh * 512:g * 1024 + (bh + 1) * 512], 1.0,
                    pSU[:, bh * 512:(bh + 1) * 512], op.mult, op.add,
                    deps=[tk3h[bh], tqs[bh * 4 + 3]]))
            s1_drive()
        out["o"] = tk_o
        out["s_done"] = tk_sup[-1]
        lc["su_free"] = tk_o[-1]
        lc["qc_free"] = p.nopf("pe", deps=[])
        lc["trv_free"] = out[("u", 1)]
        lc["gbar_free"] = out[("u", 1)]
        lc["kdec_free"] = out["s_done"]
        lc["cols_free"] = p.nopf("dve", deps=tk_qsl + [out["s_done"]])
        lc["gcb_free"] = out["s_done"]
        return out

    def emit_opost(cc, ser):
        par = cc % 2
        sgs = sgc[:, par * D:(par + 1) * D]
        lc = Ldict[cc]
        tk_ofd = []
        if cc not in tk_sg:
            tk_sg[cc] = p.dma("sp", sgc[:, par * D:(par + 1) * D],
                              sg_s[cc * 128:(cc + 1) * 128, :],
                              deps=[s1t["tile"][(cc // CPT, "done")],
                                    Ldict.get(cc - 2, {}).get("sg_free")])
        cn = cc + 1
        if (cn < NCH and cn not in tk_sg
                and (cn // CPT, "done") in s1t["tile"]):
            tk_sg[cn] = p.dma("sp", sgc[:, (1 - par) * D:(2 - par) * D],
                              sg_s[cn * 128:(cn + 1) * 128, :],
                              deps=[s1t["tile"][(cn // CPT, "done")],
                                    Ldict.get(cc - 1, {}).get("sg_free")])
        tk_og_l = []
        for g in range(2):
            h0 = g * 8
            tk_og = p.tt(ost[:, :], qst[:, g * 1024:(g + 1) * 1024],
                         sgs[:, g * 1024:(g + 1) * 1024], op.mult,
                         deps=[ser["o"][2 * g + 1], tk_sg[cc]])
            tk_og_l.append(tk_og)
            tk_s2 = p.act(qst[:, g * 1024:(g + 1) * 1024], ost[:, :],
                          AF.Square, deps=[tk_og])
            tk_red = p.red(rms[:, g * 8:(g + 1) * 8],
                           qst[:, g * 1024:(g + 1) * 1024].rearrange(
                               "p (hh s) -> p hh s", hh=8),
                           op.add, deps=[tk_s2])
            p.act(rms[:, 16 + g * 8:16 + (g + 1) * 8], rms[:, g * 8:(g + 1) * 8],
                  AF.Ln, bias=EPS, scale=1.0 / 128.0, deps=[tk_red], fence=False)
            tk_rs = p.act(rms[:, 32 + g * 8:32 + (g + 1) * 8],
                          rms[:, 16 + g * 8:16 + (g + 1) * 8], AF.Exp, scale=-0.5)
            tk_on = []
            for hh in range(8):
                h = h0 + hh
                tk_on.append(p.stt(onb[:, hh * 128:(hh + 1) * 128],
                                   ost[:, hh * 128:(hh + 1) * 128],
                                   rms[:, 32 + g * 8 + hh:32 + g * 8 + hh + 1],
                                   wtile[:, h * 128:(h + 1) * 128],
                                   op.mult, op.mult, deps=[tk_rs, tk_og]))
            tkmh = [None, None]
            for hh in range(8):
                tkmh[hh // 4] = p.mm(pSU[:, hh * 128:(hh + 1) * 128],
                                     onb[:, hh * 128:(hh + 1) * 128],
                                     ident_b[:, :], True, True,
                                     deps=[tk_on[hh], ser["o"][3]],
                                     fence=(hh % 4 == 3))
            tk_ofe = [p.cp("dve", ofst[:, bh * 512:(bh + 1) * 512],
                           pSU[:, bh * 512:(bh + 1) * 512],
                           deps=[tkmh[bh], tk_ofd[-1] if tk_ofd else
                                 Ldict.get(cc - 1, {}).get("opost_done")])
                      for bh in range(2)]
            tk_ofe = p.nopf("dve", deps=tk_ofe)
            s1_drive()
            tk_ofd.append(p.dma("pool",
                                of_s[cc, h0:h0 + 8, :, :].rearrange(
                                    "hh pp tt -> pp hh tt"),
                                ofst[:, :].rearrange("pp (hh tt) -> pp hh tt",
                                                     tt=128),
                                deps=[tk_ofe]))
            lc["su_free"] = tk_ofe
        lc["sg_free"] = p.nopf("dve", deps=tk_og_l)
        lc["opost_done"] = p.nopf("pool", deps=tk_ofd)
        return tk_ofd

    if phases == "s1":
        s1_max[0] = NT1
        s1_finish(NT1 - 1)
        p.barrier()
        esA.close()
        es12.close()
        p.replay()
        es.close()
        return nc
    s1_finish(0)
    s1_drive(12)
    tk_sg[0] = p.dma("sp", sgc[:, 0:D], sg_s[0:128, :],
                     deps=[s1t["tile"][(0, "done")]])
    emit_local(0)
    stop_cc = NCH
    stop_ser_only = False
    if phases.startswith("cc"):
        stop_cc = min(int(phases[2:].rstrip("s")), NCH)
        stop_ser_only = phases.endswith("s")
    for cc in range(stop_cc):
        s1_max[0] = cc // CPT + 2
        ser = emit_serial(cc)
        if stop_ser_only and cc == stop_cc - 1:
            break
        if cc + 1 < NCH:
            s1_finish((cc + 1) // CPT)
            emit_local(cc + 1)
        emit_opost(cc, ser)
    s1_max[0] = NT1
    s1_finish(NT1 - 1)
    if stop_cc < NCH or stop_ser_only:
        p.barrier()
        esA.close()
        es12.close()
        p.replay()
        es.close()
        return nc
    tk_of_all = [Ldict[cc]["opost_done"] for cc in range(NCH)]
    p.barrier()
    esA.close()
    es12.close()
    if phases == "A":
        p.replay()
        es.close()
        return nc

    esf = ExitStack()
    SF = esf.enter_context
    fw = SF(nc.sbuf_tensor("fw", [128, 2 * D], F32)).ap()
    facc = SF(nc.sbuf_tensor("facc", [128, 16], F32)).ap()
    ftmp = SF(nc.sbuf_tensor("ftmp", [128, D], F32)).ap()
    ftrn = SF(nc.sbuf_tensor("ftrn", [128, D], BF16)).ap()
    ftrn2 = SF(nc.sbuf_tensor("ftrn2", [128, D], BF16)).ap()
    fsm = SF(nc.sbuf_tensor("fsm", [1, 16], F32)).ap()
    fcol = SF(nc.sbuf_tensor("fcol", [128, 1], F32)).ap()
    ps_f = SF(nc.psum_tensor("ps_f", [128, 1024], F32)).ap()

    tk_blk = []
    for rb in range(16):
        sl = rb % 2
        tk_l = p.dma("sp", fw[:, sl * D:(sl + 1) * D],
                     ffn_w[rb * 128:(rb + 1) * 128, :],
                     deps=[tk_blk[rb - 2] if rb >= 2 else None])
        tk_blk.append(p.red(facc[:, rb:rb + 1], fw[:, sl * D:(sl + 1) * D],
                            op.add, absv=True, deps=[tk_l]))
    tk_r = p.red(fcol[:, 0:1], facc[:, :], op.add, deps=tk_blk)
    tk_t = p.tr(ps_f[0:1, 0:128], fcol[:, 0:1], ident_f[:, :], deps=[tk_r])
    tk_r2 = p.red(fsm[0:1, 0:1], ps_f[0:1, 0:128], op.add, deps=[tk_t])
    p.ts(fsm[0:1, 1:2], fsm[0:1, 0:1], 1.0 / (D * D), EPS, op.mult, op.max,
         deps=[tk_r2], fence=False)
    p.raw("dve", lambda v: v.reciprocal(fsm[0:1, 2:3], fsm[0:1, 1:2]), fence=False)
    tk_al = p.ts(fsm[0:1, 3:4], fsm[0:1, 1:2], 1.0 / 127.0, None, op.mult)
    if dbg:
        p.dma("sp", alpha_d[:, :], fsm[0:1, 0:4], deps=[tk_al])
    tk_bc = p.mm(ps_f[:, 128:132], ones1f[:, :], fsm[0:1, 0:4], True, True,
                 deps=[tk_al, tk_t] + list(TC), fence=True)
    tk_at = p.cp("dve", alpha_t[:, 0:4], ps_f[:, 128:132], deps=[tk_bc])

    tk_rd = {}
    for rb in range(16):
        sl = rb % 2
        tk_l = p.dma("sp", fw[:, sl * D:(sl + 1) * D],
                     ffn_w[rb * 128:(rb + 1) * 128, :],
                     deps=[tk_at, tk_rd.get(rb - 2)])
        tk1 = p.act(ftmp[:, :], fw[:, sl * D:(sl + 1) * D], AF.Copy,
                    bias=MAGIC, scale=alpha_t[:, 2:3],
                    deps=[tk_l, tk_at, tk_rd.get(("ts", rb - 1))])
        tk_rd[rb] = tk1
        tk2 = p.act(ftmp[:, :], ftmp[:, :], AF.Copy, bias=-MAGIC, deps=[tk1])
        tk3 = p.ts(ftrn[:, :], ftmp[:, :], 1.0, -1.0, op.min, op.max,
                   deps=[tk2, tk_rd.get(("tr", rb - 1))])
        tk_rd[("ts", rb)] = tk3
        tk_cps = []
        for ib in range(16):
            poff = (ib % 2) * 512
            tkt = p.mm(ps_f[:, poff:poff + 128],
                       ftrn[:, ib * 128:(ib + 1) * 128], ident_b[:, :], True, True,
                       deps=[tk3, tk_cps[ib - 2] if ib >= 2 else None],
                       fence=True)
            tk_cps.append(p.cp("dve", ftrn2[:, ib * 128:(ib + 1) * 128],
                               ps_f[:, poff:poff + 128],
                               deps=[tkt, tk_rd.get(("dma", rb - 1, ib))]))
            tk_rd[("tr", rb)] = tkt
            tk_rd[("dma", rb, ib)] = p.dma(
                "pool", tern_s[ib * 128:(ib + 1) * 128, rb * 128:(rb + 1) * 128],
                ftrn2[:, ib * 128:(ib + 1) * 128], deps=[tk_cps[-1]])
    p.barrier()
    esf.close()
    if phases == "ffn":
        p.replay()
        es.close()
        return nc

    _stage3(nc, p, es, T, T3, NT3, x_in, w_o, of_s, tern_s, y_out,
            ident_b, ident_f, btile, alpha_t)

    p.replay()
    es.close()
    return nc


def _stage3(nc, p, es, T, T3, NT3, x_in, w_o, of_s, tern_s, y_out,
            ident_b, ident_f, btile, alpha_t):
    SG = es.enter_context
    wo_sb = SG(nc.sbuf_tensor("wo_sb", [128, 16 * D], BF16)).ap()
    tern_sb = SG(nc.sbuf_tensor("tern_sb", [128, 16 * D], BF16)).ap()
    ofsb = SG(nc.sbuf_tensor("ofsb", [128, 2 * D], BF16)).ap()
    xin = SG(nc.sbuf_tensor("xin", [128, D], F32)).ap()
    scr = SG(nc.sbuf_tensor("scr", [128, D], F32)).ap()
    dsc = SG(nc.sbuf_tensor("dsc", [128, 512], F32)).ap()
    yb = SG(nc.sbuf_tensor("yb", [128, 2 * D], F32)).ap()
    qx = SG(nc.sbuf_tensor("qx", [128, 2 * D], BF16)).ap()
    qxf = SG(nc.sbuf_tensor("qxf", [128, D], BF16)).ap()
    zb = SG(nc.sbuf_tensor("zb", [128, D], F32)).ap()
    outb = SG(nc.sbuf_tensor("outb", [128, D], F32)).ap()
    stats = SG(nc.sbuf_tensor("stats", [128, 48], F32)).ap()
    ps2 = [SG(nc.psum_tensor("psA", [128, 2048], F32)).ap(),
           SG(nc.psum_tensor("psB", [128, 2048], F32)).ap()]

    tk_wo = [p.dma("pool", wo_sb[:, kt * D:(kt + 1) * D],
                   w_o[kt * 128:(kt + 1) * 128, :]) for kt in range(16)]
    tk_tern = [p.dma("sp", tern_sb[:, kt * D:(kt + 1) * D],
                     tern_s[kt * 128:(kt + 1) * 128, :]) for kt in range(16)]
    tk_out = []
    prev = {}
    ofld = {}
    NTB = T // 128

    def emit_attn(tb):
        sl = tb % 2
        ps = ps2[sl]
        st = stats[:, sl * 24:(sl + 1) * 24]
        yv = yb[:, sl * D:(sl + 1) * D]
        qv = qx[:, sl * D:(sl + 1) * D]
        if tb not in ofld:
            ofld[tb] = p.dma("sp", ofsb[:, sl * D:(sl + 1) * D].rearrange(
                                 "pp (h tt) -> pp h tt", tt=128),
                             of_s[tb, :, :, :].rearrange("h pp tt -> pp h tt"),
                             deps=[prev.get(("attnpe", tb - 2))])
        tk_of = ofld[tb]
        if tb + 1 < NTB:
            ofld[tb + 1] = p.dma(
                "sp", ofsb[:, (1 - sl) * D:(2 - sl) * D].rearrange(
                    "pp (h tt) -> pp h tt", tt=128),
                of_s[tb + 1, :, :, :].rearrange("h pp tt -> pp h tt"),
                deps=[prev.get(("attnpe", tb - 1))])
        tk_x = p.dma("sp", xin[:, :], x_in[tb * 128:(tb + 1) * 128, :],
                     deps=[prev.get(("s1", tb - 1))])
        tkm = None
        for oc in range(4):
            for kt in range(16):
                tkm = p.mm(ps[:, oc * 512:(oc + 1) * 512],
                           ofsb[:, sl * D + kt * 128: sl * D + (kt + 1) * 128],
                           wo_sb[:, kt * D + oc * 512: kt * D + (oc + 1) * 512],
                           kt == 0, kt == 15,
                           deps=[tk_of, prev.get(("z", tb - 2, oc)),
                                 prev.get(("qxf", tb - 2))] + tk_wo,
                           fence=(kt == 15))
        prev[("attnpe", tb)] = tkm
        tk_s1 = p.stt(yv[:, :], xin[:, :], 1.0, ps[:, :], op.mult, op.add,
                      deps=[tkm, tk_x, prev.get(("s2", tb - 1))],
                      accum=st[:, 0:1])
        prev[("s1", tb)] = tk_s1
        hD = D // 2
        tk_s2a = p.act(scr[:, 0:hD], yv[:, 0:hD], AF.Square, deps=[tk_s1],
                       accum=st[:, 1:2])
        tk_sq1 = p.stt(dsc[:, :], yv[:, hD:hD + 512], 1.0,
                       yv[:, hD:hD + 512], op.mult, op.mult,
                       deps=[tk_s1], accum=st[:, 19:20])
        tk_s2b = p.stt(dsc[:, :], yv[:, hD + 512:D], 1.0, yv[:, hD + 512:D],
                       op.mult, op.mult, deps=[tk_sq1], accum=st[:, 21:22])
        prev[("s2", tb)] = tk_s2b
        p.tt(st[:, 19:20], st[:, 19:20], st[:, 21:22], op.add,
             deps=[tk_s2b], fence=False)
        p.tt(st[:, 1:2], st[:, 1:2], st[:, 19:20], op.add,
             deps=[tk_s2a], fence=False)
        p.ts(st[:, 2:3], st[:, 0:1], 1.0 / D, None, op.mult, fence=False)
        p.tt(st[:, 3:4], st[:, 2:3], st[:, 2:3], op.mult, fence=False)
        tk_var = p.stt(st[:, 4:5], st[:, 1:2], 1.0 / D, st[:, 3:4],
                       op.mult, op.subtract)
        p.act(st[:, 5:6], st[:, 4:5], AF.Ln, bias=EPS, deps=[tk_var],
              fence=False)
        tk_st = p.act(st[:, 6:7], st[:, 5:6], AF.Exp, scale=-0.5)
        tk_y = p.ts(yv[:, :], yv[:, :], st[:, 2:3], st[:, 6:7],
                    op.subtract, op.mult, deps=[tk_st, tk_s2a])
        tk_am = p.red(st[:, 7:8], yv[:, :], op.max, absv=True, deps=[tk_y])
        p.ts(st[:, 8:9], st[:, 7:8], EPS, None, op.max, deps=[tk_am],
             fence=False)
        p.raw("dve", lambda v, a=st: v.reciprocal(a[:, 9:10], a[:, 8:9]),
              fence=False)
        p.ts(st[:, 10:11], st[:, 9:10], 127.0, None, op.mult, fence=False)
        tk_so = p.tt(st[:, 11:12], st[:, 8:9], alpha_t[:, 3:4], op.mult)
        prev[("so", tb)] = tk_so
        tk_q1 = p.act(scr[:, :], yv[:, :], AF.Copy, bias=MAGIC,
                      scale=st[:, 10:11], deps=[tk_so])
        prev[("q2", tb)] = p.act(qv[:, :], scr[:, :], AF.Copy, bias=-MAGIC,
                                 deps=[tk_q1])

    def emit_post(tb):
        sl = tb % 2
        ps = ps2[sl]
        st = stats[:, sl * 24:(sl + 1) * 24]
        yv = yb[:, sl * D:(sl + 1) * D]
        qv = qx[:, sl * D:(sl + 1) * D]
        tk_tr = []
        for g4 in range(4):
            poff = g4 * 512
            tkm = None
            for j in range(4):
                dblk = g4 * 4 + j
                tkm = p.mm(ps[:, poff + j * 128: poff + (j + 1) * 128],
                           qv[:, dblk * 128:(dblk + 1) * 128], ident_b[:, :],
                           True, True,
                           deps=[prev[("q2", tb)], prev[("s1", tb)],
                                 tk_tr[g4 - 2] if g4 >= 2 else None],
                           fence=(j == 3))
            tk_tr.append(p.cp("dve", qxf[:, g4 * 512:(g4 + 1) * 512],
                              ps[:, poff:poff + 512], deps=[tkm]))
        tk_qxf = p.nopf("dve", deps=tk_tr)
        prev[("qxf", tb)] = tk_qxf
        tk_f = []
        for oc in range(4):
            tkm = None
            for kt in range(16):
                tkm = p.mm(ps[:, oc * 512:(oc + 1) * 512],
                           qxf[:, kt * 128:(kt + 1) * 128],
                           tern_sb[:, kt * D + oc * 512: kt * D + (oc + 1) * 512],
                           kt == 0, kt == 15,
                           deps=[tk_qxf, prev.get(("s1", tb + 1)),
                                 prev.get(("zr", tb - 1))] + tk_tern,
                           fence=(kt == 15))
            tk_f.append(tkm)
        tk_z = []
        for oc in range(4):
            tk_z.append(p.stt(zb[:, oc * 512:(oc + 1) * 512],
                              ps[:, oc * 512:(oc + 1) * 512], st[:, 11:12],
                              btile[:, oc * 512:(oc + 1) * 512],
                              op.mult, op.add,
                              deps=[tk_f[oc], prev[("so", tb)],
                                    prev.get(("ozb", tb - 1))]))
            prev[("z", tb, oc)] = tk_z[-1]
        tk_zr = p.tt(zb[:, :], zb[:, :], yv[:, :], op.add, deps=tk_z)
        prev[("zr", tb)] = tk_zr
        hD = D // 2
        tk_c1 = p.act(scr[:, 0:hD], zb[:, 0:hD], AF.Copy, deps=[tk_zr],
                      accum=st[:, 12:13])
        tk_c1b = p.ts(dsc[:, :], zb[:, hD:hD + 512], 1.0, 0.0,
                      op.mult, op.add, deps=[tk_zr], accum=st[:, 22:23])
        tk_c1c = p.ts(dsc[:, :], zb[:, hD + 512:D], 1.0, 0.0,
                      op.mult, op.add, deps=[tk_c1b], accum=st[:, 23:24])
        tk_c2 = p.act(scr[:, 0:hD], zb[:, 0:hD], AF.Square, deps=[tk_c1],
                      accum=st[:, 13:14])
        tk_c2b = p.stt(dsc[:, :], zb[:, hD:hD + 512], 1.0,
                       zb[:, hD:hD + 512], op.mult, op.mult,
                       deps=[tk_c1c], accum=st[:, 19:20])
        tk_c2c = p.stt(dsc[:, :], zb[:, hD + 512:D], 1.0, zb[:, hD + 512:D],
                       op.mult, op.mult, deps=[tk_c2b], accum=st[:, 21:22])
        prev[("c2", tb)] = p.nopf("dve", deps=[tk_c2, tk_c2c])
        p.tt(st[:, 22:23], st[:, 22:23], st[:, 23:24], op.add, fence=False)
        p.tt(st[:, 12:13], st[:, 12:13], st[:, 22:23], op.add,
             deps=[tk_c1], fence=False)
        p.tt(st[:, 19:20], st[:, 19:20], st[:, 21:22], op.add, fence=False)
        p.tt(st[:, 13:14], st[:, 13:14], st[:, 19:20], op.add,
             deps=[tk_c2], fence=False)
        p.ts(st[:, 14:15], st[:, 12:13], 1.0 / D, None, op.mult, fence=False)
        p.tt(st[:, 15:16], st[:, 14:15], st[:, 14:15], op.mult, fence=False)
        tk_v2 = p.stt(st[:, 16:17], st[:, 13:14], 1.0 / D, st[:, 15:16],
                      op.mult, op.subtract)
        p.act(st[:, 17:18], st[:, 16:17], AF.Ln, bias=EPS, deps=[tk_v2],
              fence=False)
        tk_st2 = p.act(st[:, 18:19], st[:, 17:18], AF.Exp, scale=-0.5)
        tk_o = p.ts(outb[:, :], zb[:, :], st[:, 14:15], st[:, 18:19],
                    op.subtract, op.mult,
                    deps=[tk_st2, tk_c2,
                          tk_out[-1] if tk_out else None])
        prev[("ozb", tb)] = tk_o
        tk_out.append(p.dma("sp", y_out[tb * 128:(tb + 1) * 128, :],
                            outb[:, :], deps=[tk_o]))

    for tb in range(NTB):
        emit_attn(tb)
        if tb >= 1:
            emit_post(tb - 1)
    emit_post(NTB - 1)

    p.wait_self_dma("sp", tk_out[-1])


def make_aux():
    ident = np.eye(128, dtype=np.float32)
    mA = np.where(np.tril(np.ones((128, 128)), -1) > 0, 0.0, NEG).astype(np.float32)
    mNb = np.where(np.triu(np.ones((128, 128)), 1) > 0, 0.0, NEG).astype(np.float32)
    mW = np.where(np.triu(np.ones((128, 128)), 0) > 0, 0.0, NEG).astype(np.float32)
    masks = np.concatenate([np.tile(m, (1, 8)) for m in (mA, mNb, mW)], axis=1)
    ind = np.zeros((48, D), np.float32)
    for j in range(3):
        for h in range(H):
            ind[j * 16 + h, h * 128:(h + 1) * 128] = 1.0
    oh = np.zeros((128, H * H), np.float32)
    for h in range(H):
        oh[:, h * 16 + h] = 1.0
    return ident, masks, ind, oh


def make_inputs(xb, W_qkv, W_ab, conv_w, A_log, dt_bias, o_norm_w, W_o, fw, fb):
    ident, masks, ind, oh = make_aux()
    wt = np.ascontiguousarray(
        np.broadcast_to(np.tile(np.asarray(o_norm_w, np.float32), H), (128, D)))
    bt = np.ascontiguousarray(
        np.broadcast_to(np.asarray(fb, np.float32), (128, D)))
    return dict(
        x=np.ascontiguousarray(xb, np.float32),
        w_qkv=np.ascontiguousarray(W_qkv, np.float32),
        w_ab=np.ascontiguousarray(W_ab, np.float32),
        conv_w=np.ascontiguousarray(conv_w, np.float32),
        a_log=np.asarray(A_log, np.float32).reshape(H, 1),
        dt_bias=np.asarray(dt_bias, np.float32).reshape(H, 1),
        w_o=np.ascontiguousarray(W_o, np.float32),
        ffn_w=np.ascontiguousarray(fw, np.float32),
        c_ident=ident, c_masks=masks, c_ind=ind, c_oh=oh,
        c_wtile=wt, c_btile=bt,
    )


def kernel(L0_attn_hidden_states, L0_attn_W_qkv, L0_attn_W_ab, L0_attn_conv_w,
           L0_attn_A_log, L0_attn_dt_bias, L0_attn_o_norm_w, L0_attn_W_o,
           L0_ffn_weight, L0_ffn_bias):
    x = np.ascontiguousarray(L0_attn_hidden_states, dtype=np.float32)
    B, T, _ = x.shape
    nc = build_program(T=T)
    in_maps = [make_inputs(x[b], L0_attn_W_qkv, L0_attn_W_ab, L0_attn_conv_w,
                           L0_attn_A_log, L0_attn_dt_bias, L0_attn_o_norm_w,
                           L0_attn_W_o, L0_ffn_weight, L0_ffn_bias)
               for b in range(B)]
    res = run_bass_kernel_spmd(nc, in_maps, core_ids=list(range(B)))
    return np.stack([res.results[b]["y"] for b in range(B)], axis=0).astype(np.float32)


# version 14
# speedup vs baseline: 1.1867x; 1.1867x over previous
import numpy as np
from contextlib import ExitStack

import concourse.bass as bass
import concourse.mybir as mybir
from concourse.bass_utils import run_bass_kernel_spmd
from concourse.alu_op_type import AluOpType as op

F32 = mybir.dt.float32
BF16 = mybir.dt.bfloat16
AF = mybir.ActivationFunctionType
AX = mybir.AxisListType.X

D = 2048
H = 16
C = 128
QKV_N = 3 * D
MAGIC = float(3 * 2 ** 22)
NEG = -60000.0
EPS = 1e-5
L2EPS = 1e-6
LN_HALF_DK = float(-0.5 * np.log(128.0))
LN2 = float(np.log(2.0))

ENGINES = ("pe", "act", "dve", "sp", "pool")
BLOCK_NAME = dict(pe="tensor", act="scalar", dve="vector", sp="sync", pool="gpsimd")


class P:

    NLANES = 4
    CHANNELS = ("pe", "act", "dve", "sp", "pool") + tuple(
        f"{e}_d{l}" for e in ("sp", "pool") for l in range(4))

    def __init__(self, nc, es):
        self.nc = nc
        self.sem = {ch: es.enter_context(nc.semaphore(f"sem_{ch}"))
                    for ch in self.CHANNELS}
        self.q = {e: [] for e in ENGINES}
        self.cnt = {ch: 0 for ch in self.CHANNELS}
        self.lane = {"sp": 0, "pool": 0}
        self.acc = {e: ([], []) for e in ENGINES}

    @staticmethod
    def _eng_of(ch):
        return ch.split("_")[0]

    @staticmethod
    def _ext(ap):
        if ap is None or isinstance(ap, (int, float)):
            return None
        try:
            lo = ap.offset
            hi = lo + 1
            for step, count in list(ap.ap)[1:]:
                hi += step * (count - 1)
            return (ap.tensor.name, lo, hi)
        except Exception:
            return (getattr(getattr(ap, "tensor", None), "name", "?"), 0, 1 << 60)

    @staticmethod
    def _hits(x, y):
        if x[0] == "__any__" or y[0] == "__any__":
            return True
        return x[0] == y[0] and x[1] < y[2] and y[1] < x[2]

    def _conflict(self, e, writes, reads):
        tw, tr = self.acc[e]
        for w in writes:
            if w is not None and any(self._hits(x, w) for x in tw + tr):
                return True
        for r in reads:
            if r is not None and any(self._hits(x, r) for x in tw):
                return True
        return False

    def _emit(self, e, fn, deps, inc, ch=None, drain=None, w=(), r=()):
        ch = ch or e
        if drain is None and e in ("act", "dve"):
            if w or r:
                writes = [self._ext(a) for a in w]
                reads = [self._ext(a) for a in r]
            else:
                writes = [("__any__", 0, 1 << 60)]
                reads = []
            drain = self._conflict(e, writes, reads) if (w or r) else True
            if drain:
                self.acc[e] = ([], [])
            self.acc[e][0].extend(x for x in writes if x)
            self.acc[e][1].extend(x for x in reads if x)
        if drain:
            self.q[e].append((lambda g: g.drain(), (), 0, ch))
        deps = tuple(d for d in deps
                     if d is not None and (self._eng_of(d[0]) != e or "_d" in d[0]))
        self.q[e].append((fn, deps, inc, ch))
        if inc:
            self.cnt[ch] += inc
            return (ch, self.cnt[ch])
        return None

    def raw(self, e, fn, deps=(), fence=True):
        return self._emit(e, fn, deps, 1 if fence else 0)

    def nopf(self, e, deps=()):
        if e in ("pe", "act", "dve"):
            return self._emit(e, lambda g: g.drain(), deps, 1, drain=False)
        return self._emit(e, lambda g: g.nop(), deps, 1, drain=False)

    def wait_self_dma(self, e, tok):
        sem = self.sem[tok[0]]
        v = tok[1]
        self.q[e].append((lambda g, sem=sem, v=v: g.wait_ge(sem, v) or None, (), 0, e))

    def dma_quiesce(self, e):
        deps = [(f"{e}_d{l}", self.cnt[f"{e}_d{l}"]) for l in range(self.NLANES)]
        return self.nopf(e, deps=deps)

    def barrier(self):
        t1 = [self.dma_quiesce(e) if e in ("sp", "pool") else self.nopf(e)
              for e in ENGINES]
        return [self.nopf(e, deps=t1) for e in ENGINES]

    def dma(self, e, outp, inp, deps=()):
        l = self.lane[e]
        self.lane[e] = (l + 1) % self.NLANES
        ch = f"{e}_d{l}"
        gate = (ch, self.cnt[ch])
        return self._emit(e, lambda g, o=outp, i=inp: g.dma_start(out=o, in_=i),
                          list(deps) + [gate], 16, ch=ch)

    def mm(self, dst, lhs, rhs, start, stop, deps=(), fence=False):
        return self._emit("pe", lambda t, d=dst, l=lhs, r=rhs, a=start, b=stop:
                          t.matmul(d, l, r, start=a, stop=b), deps,
                          1 if fence else 0)

    def tr(self, dst, src, ident, deps=(), fence=True):
        return self._emit("pe", lambda t, d=dst, s=src, i=ident:
                          t.transpose(d, s, i), deps, 1 if fence else 0)

    def cp(self, e, dst, src, deps=(), fence=True):
        return self._emit(e, (lambda v, d=dst, s=src: v.tensor_copy(d, s)) if e == "dve"
                          else (lambda s_, d=dst, s=src: s_.activation(
                              d, s, AF.Copy, bias=0.0, scale=1.0)),
                          deps, 1 if fence else 0, w=(dst,), r=(src,))

    def ts(self, dst, src, s1, s2, o1, o2=None, deps=(), fence=True, accum=None):
        def fn(v, d=dst, s=src, s1=s1, s2=s2, o1=o1, o2=o2, ac=accum):
            if o2 is None:
                return v.tensor_scalar(d, s, s1, None, o1, accum_out=ac) \
                    if ac is not None else v.tensor_scalar(d, s, s1, None, o1)
            return v.tensor_scalar(d, s, s1, s2, o1, o2, accum_out=ac) \
                if ac is not None else v.tensor_scalar(d, s, s1, s2, o1, o2)
        aps = lambda z: z if hasattr(z, "offset") else None
        return self._emit("dve", fn, deps, 1 if fence else 0,
                          w=(dst, accum), r=(src, aps(s1), aps(s2)))

    def tt(self, dst, a, b, o, deps=(), fence=True):
        return self._emit("dve", lambda v, d=dst, a=a, b=b, o=o:
                          v.tensor_tensor(d, a, b, o), deps, 1 if fence else 0,
                          w=(dst,), r=(a, b))

    def stt(self, dst, in0, sc, in1, o1, o2, deps=(), fence=True, accum=None):
        def fn(v, d=dst, i0=in0, sc=sc, i1=in1, o1=o1, o2=o2, ac=accum):
            if ac is not None:
                return v.scalar_tensor_tensor(d, i0, sc, i1, o1, o2, accum_out=ac)
            return v.scalar_tensor_tensor(d, i0, sc, i1, o1, o2)
        aps = lambda z: z if hasattr(z, "offset") else None
        return self._emit("dve", fn, deps, 1 if fence else 0,
                          w=(dst, accum), r=(in0, in1, aps(sc)))

    def act(self, dst, src, func, bias=0.0, scale=1.0, deps=(), fence=True,
            accum=None):
        def fn(s_, d=dst, s=src, f=func, b=bias, sc=scale, ac=accum):
            if ac is not None:
                return s_.activation(d, s, f, bias=b, scale=sc, accum_out=ac)
            return s_.activation(d, s, f, bias=b, scale=sc)
        aps = lambda z: z if hasattr(z, "offset") else None
        return self._emit("act", fn, deps, 1 if fence else 0,
                          w=(dst, accum), r=(src, aps(bias), aps(scale)))

    def red(self, dst, src, o, absv=False, deps=(), fence=True):
        return self._emit("dve", lambda v, d=dst, s=src, o=o, a=absv:
                          v.tensor_reduce(d, s, AX, o, apply_absolute_value=a),
                          deps, 1 if fence else 0, w=(dst,), r=(src,))

    def mset(self, e, ap, val, deps=(), fence=True):
        return self._emit(e, lambda g, a=ap, v=val: g.memset(a, v), deps,
                          1 if fence else 0, w=(ap,))

    def replay(self):
        with self.nc.Block() as block:
            for e in ENGINES:
                deco = getattr(block, BLOCK_NAME[e])
                items = self.q[e]
                sems = self.sem

                def body(eng, items=items):
                    last = {}
                    for fn, deps, inc, ch in items:
                        for (pch, v) in deps:
                            if last.get(pch, 0) < v:
                                eng.wait_ge(sems[pch], v)
                                last[pch] = v
                        inst = fn(eng)
                        if inc:
                            inst.then_inc(sems[ch], inc)
                deco(body)


def build_program(T=4096, dbg=False, phases="all"):
    locstop = 99
    if "/" in phases:
        phases, _loc = phases.split("/")
        locstop = int(_loc)
    nc = bass.Bass("TRN2", target_bir_lowering=False)
    NCH = T // C
    T1 = min(512, T)
    NT1 = T // T1
    CPT = T1 // 128
    T3 = min(512, T)
    NT3 = T // T3
    kdbg = "ExternalOutput" if dbg else "Internal"

    x_in = nc.dram_tensor("x", [T, D], F32, kind="ExternalInput").ap()
    w_qkv = nc.dram_tensor("w_qkv", [D, QKV_N], F32, kind="ExternalInput").ap()
    w_ab = nc.dram_tensor("w_ab", [D, 2 * H + D], F32, kind="ExternalInput").ap()
    conv_w = nc.dram_tensor("conv_w", [QKV_N, 4], F32, kind="ExternalInput").ap()
    a_log = nc.dram_tensor("a_log", [H, 1], F32, kind="ExternalInput").ap()
    dt_bias = nc.dram_tensor("dt_bias", [H, 1], F32, kind="ExternalInput").ap()
    w_o = nc.dram_tensor("w_o", [D, D], F32, kind="ExternalInput").ap()
    ffn_w = nc.dram_tensor("ffn_w", [D, D], F32, kind="ExternalInput").ap()
    c_ident = nc.dram_tensor("c_ident", [128, 128], F32, kind="ExternalInput").ap()
    c_masks = nc.dram_tensor("c_masks", [128, 3 * 1024], F32, kind="ExternalInput").ap()
    c_ind = nc.dram_tensor("c_ind", [48, D], F32, kind="ExternalInput").ap()
    c_oh = nc.dram_tensor("c_oh", [128, H * H], F32, kind="ExternalInput").ap()
    c_wtile = nc.dram_tensor("c_wtile", [128, D], F32, kind="ExternalInput").ap()
    c_btile = nc.dram_tensor("c_btile", [128, D], F32, kind="ExternalInput").ap()
    y_out = nc.dram_tensor("y", [T, D], F32, kind="ExternalOutput").ap()

    qkc_s = nc.dram_tensor("qkc_s", [NCH, 48, 128, 128], BF16, kind=kdbg).ap()
    sg_s = nc.dram_tensor("sg_s", [T, D], BF16, kind=kdbg).ap()
    of_s = nc.dram_tensor("of_s", [NCH, H, 128, 128], BF16, kind=kdbg).ap()
    tern_s = nc.dram_tensor("tern_s", [D, D], BF16, kind=kdbg).ap()
    gH_d = nc.dram_tensor("gH_d", [H, T], F32, kind=kdbg).ap()
    lnbH_d = nc.dram_tensor("lnbH_d", [H, T], F32, kind=kdbg).ap()
    if dbg:
        alpha_d = nc.dram_tensor("alpha_d", [1, 4], F32, kind=kdbg).ap()

    es = ExitStack()
    SG = es.enter_context
    p = P(nc, es)

    ident_b = SG(nc.sbuf_tensor("ident_b", [128, 128], BF16)).ap()
    ident_f = SG(nc.sbuf_tensor("ident_f", [128, 128], F32)).ap()
    btile = SG(nc.sbuf_tensor("btile", [128, D], F32)).ap()
    nega = SG(nc.sbuf_tensor("nega", [H, 2], F32)).ap()
    dtb = SG(nc.sbuf_tensor("dtb", [H, 1], F32)).ap()
    ones1f = SG(nc.sbuf_tensor("ones1f", [1, 128], F32)).ap()
    alpha_t = SG(nc.sbuf_tensor("alpha_t", [128, 4], F32)).ap()
    csts = SG(nc.sbuf_tensor("csts", [128, 4], F32)).ap()
    es12 = ExitStack()
    SG12 = es12.enter_context
    masks_b = SG12(nc.sbuf_tensor("masks_b", [128, 3 * 1024], BF16)).ap()
    oh_b = SG12(nc.sbuf_tensor("oh_b", [128, H * H], BF16)).ap()
    wtile = SG12(nc.sbuf_tensor("wtile", [128, D], F32)).ap()
    ones16 = SG12(nc.sbuf_tensor("ones16", [H, 128], F32)).ap()

    tc = []
    tc.append(p.dma("pool", ident_b[:, :], c_ident[:, :]))
    tc.append(p.dma("sp", ident_f[:, :], c_ident[:, :]))
    tc.append(p.dma("pool", masks_b[:, :], c_masks[:, :]))
    tc.append(p.dma("pool", oh_b[:, :], c_oh[:, :]))
    tc.append(p.dma("sp", wtile[:, :], c_wtile[:, :]))
    tc.append(p.dma("sp", btile[:, :], c_btile[:, :]))
    tc.append(p.dma("sp", dtb[:, :], dt_bias[:, :]))
    tk_a = p.dma("sp", nega[:, 1:2], a_log[:, :])
    tk_na = p.act(nega[:, 0:1], nega[:, 1:2], AF.Exp, deps=[tk_a])
    tc.append(p.ts(nega[:, 0:1], nega[:, 0:1], -1.0, None, op.mult, deps=[tk_na]))
    p.mset("dve", ones16[:, :], 1.0)
    tc.append(p.mset("dve", ones1f[:, :], 1.0))
    p.mset("dve", csts[:, 0:1], L2EPS, fence=False)
    tc.append(p.mset("dve", csts[:, 1:2], EPS))
    nc.const_aps.register(F32, L2EPS, csts[:, 0:1])
    nc.const_aps.register(F32, EPS, csts[:, 1:2])
    TC = tuple(tc)

    esA = ExitStack()
    SA = esA.enter_context
    xf = SA(nc.sbuf_tensor("xf", [128, 16 * T1], BF16)).ap()
    xb16 = SA(nc.sbuf_tensor("xb16", [128, D], BF16)).ap()
    wqb = SA(nc.sbuf_tensor("wqb", [128, 2 * 2048], BF16)).ap()
    wabb = SA(nc.sbuf_tensor("wabb", [128, 16 * 256], BF16)).ap()
    wabsm = SA(nc.sbuf_tensor("wabsm", [128, 16 * 32], BF16)).ap()
    qpre = SA(nc.sbuf_tensor("qpre", [128, 2 * (T1 + 4)], BF16)).ap()
    tails = SA(nc.sbuf_tensor("tails", [128, 3 * 48], BF16)).ap()
    sq_out = SA(nc.sbuf_tensor("sq_out", [128, 2 * T1], BF16)).ap()
    sgate = SA(nc.sbuf_tensor("sgate", [128, 2 * 256], BF16)).ap()
    sab = SA(nc.sbuf_tensor("sab", [128, 32 * (T1 // 128)], F32)).ap()
    wcv = SA(nc.sbuf_tensor("wcv", [128, 192], F32)).ap()
    diag4 = SA(nc.sbuf_tensor("diag4", [128, 2 * 512], BF16)).ap()
    st1 = SA(nc.sbuf_tensor("st1", [H, 7 * 128], F32)).ap()
    ps_q = SA(nc.psum_tensor("ps_q", [128, 512], F32)).ap()
    ps_s1 = SA(nc.psum_tensor("ps_s1", [128, 512], F32)).ap()
    qc = SA(nc.sbuf_tensor("qc", [128, 2 * 6144], BF16)).ap()
    sgc = SA(nc.sbuf_tensor("sgc", [128, 2 * D], BF16)).ap()
    S_sb = SA(nc.sbuf_tensor("S_sb", [128, H * 128], F32)).ap()
    S_bf = SA(nc.sbuf_tensor("S_bf", [128, H * 128], BF16)).ap()
    sqgb = SA(nc.sbuf_tensor("sqgb", [128, 2048], BF16)).ap()
    indb = SA(nc.sbuf_tensor("indb", [57, D], BF16)).ap()
    lA = SA(nc.sbuf_tensor("lA", [57, 128], BF16)).ap()
    lST = SA(nc.sbuf_tensor("lST", [57, 128], BF16)).ap()
    lST2 = SA(nc.sbuf_tensor("lST2", [57, 128], BF16)).ap()
    mst = SA(nc.sbuf_tensor("mst", [H, 8 * 128], F32)).ap()
    mpc = SA(nc.sbuf_tensor("mpc", [H, 9 * 128], BF16)).ap()
    lnssq = SA(nc.sbuf_tensor("lnssq", [H, 256], F32)).ap()
    colsb = SA(nc.sbuf_tensor("colsb", [128, 2 * 64], F32)).ap()
    gcbb = SA(nc.sbuf_tensor("gcbb", [128, 2 * 16], F32)).ap()
    expd = SA(nc.sbuf_tensor("expd", [128, 512], F32)).ap()
    n0b = SA(nc.sbuf_tensor("n0b", [128, 2048], BF16)).ap()
    kdecb = SA(nc.sbuf_tensor("kdecb", [128, 2 * 2048], BF16)).ap()
    r0b = SA(nc.sbuf_tensor("r0b", [128, 2048], BF16)).ap()
    ntok = SA(nc.sbuf_tensor("ntok", [128, 2 * 512], BF16)).ap()
    nbar = SA(nc.sbuf_tensor("nbar", [128, 2 * 512], BF16)).ap()
    wchn = SA(nc.sbuf_tensor("wchn", [128, 2 * 1024], BF16)).ap()
    trvb = SA(nc.sbuf_tensor("trvb", [128, 2 * 2048], BF16)).ap()
    gtrb = SA(nc.sbuf_tensor("gtrb", [128, 1024], BF16)).ap()
    gbarb = SA(nc.sbuf_tensor("gbarb", [128, 2 * 2048], BF16)).ap()
    wbarb = SA(nc.sbuf_tensor("wbarb", [128, 2 * 2048], BF16)).ap()
    ub = SA(nc.sbuf_tensor("ub", [128, 1024], BF16)).ap()
    qst = SA(nc.sbuf_tensor("qst", [128, 2048], F32)).ap()
    ost = SA(nc.sbuf_tensor("ost", [128, 1024], F32)).ap()
    onb = SA(nc.sbuf_tensor("onb", [128, 1024], BF16)).ap()
    ofst = SA(nc.sbuf_tensor("ofst", [128, 1024], BF16)).ap()
    rms = SA(nc.sbuf_tensor("rms", [128, 48], F32)).ap()
    grow = SA(nc.sbuf_tensor("grow", [1, 16], F32)).ap()
    ghb = SA(nc.sbuf_tensor("ghb", [H, 2 * 256], F32)).ap()
    pCH = SA(nc.psum_tensor("pCH", [128, 1024], F32)).ap()
    pSQ = SA(nc.psum_tensor("pSQ", [128, 1024], F32)).ap()
    pSU = SA(nc.psum_tensor("pSU", [128, 1024], F32)).ap()

    tk_wcv = p.dma("pool", wcv[:, :].rearrange("p (fb j) -> p fb j", j=4),
                   conv_w[:, :].rearrange("(fb p) j -> p fb j", p=128))
    tk_t0 = p.mset("dve", tails[:, :], 0.0)
    s1t = {"tail": {fb: tk_t0 for fb in range(48)}, "fb": {}, "sq": {},
           "gd": {}, "bg": None, "xfree": None, "ps1": None, "tile": {}}
    NTB1 = T1 // 128
    SEGW = min(512, T1)
    NSEG = T1 // SEGW

    def softplus(out_ap, in_ap, tmp_ap, deps, scale=1.0):
        p.act(tmp_ap, in_ap, AF.Abs, scale=scale, deps=deps, fence=False)
        p.act(tmp_ap, tmp_ap, AF.Exp, scale=-1.0, fence=False)
        p.act(tmp_ap, tmp_ap, AF.Ln, bias=1.0, fence=False)
        t1 = p.act(out_ap, in_ap, AF.Relu, scale=scale, deps=deps)
        return p.tt(out_ap, out_ap, tmp_ap, op.add, deps=[t1])

    def wq_dma(ti, fb, deps):
        sl = fb % 2
        return p.dma("pool",
                     wqb[:, sl * 2048:(sl + 1) * 2048].rearrange(
                         "p (kt n) -> p kt n", n=128),
                     w_qkv[:, fb * 128:(fb + 1) * 128].rearrange(
                         "(kt p) n -> p kt n", p=128), deps=deps)

    def s1_gen(ti):
        t0 = ti * T1
        FB = s1t["fb"]
        SQ = s1t["sq"]
        tk_xtr = []
        tkms = s1t.get("xlast", [None] * 4)
        for tb in range(NTB1):
            tk_xs = [p.dma("pool", xb16[:, hf * 512:(hf + 1) * 512],
                           x_in[t0 + tb * 128: t0 + (tb + 1) * 128,
                                hf * 512:(hf + 1) * 512],
                           deps=[s1t["xfree"], tkms[hf]])
                     for hf in range(4)]
            tkms = []
            for g4 in range(4):
                prev = tk_xtr[-1] if tk_xtr else s1t["ps1"]
                tkm = None
                for j in range(4):
                    dblk = g4 * 4 + j
                    tkm = p.mm(ps_s1[:, j * 128:(j + 1) * 128],
                               xb16[:, dblk * 128:(dblk + 1) * 128],
                               ident_b[:, :], True, True,
                               deps=[tk_xs[g4], prev] + list(TC),
                               fence=(j == 3))
                tkms.append(tkm)
                tk_xtr.append(p.cp(
                    "dve",
                    xf[:, :].rearrange("p (b tt) -> p b tt", b=16)
                    [:, g4 * 4:(g4 + 1) * 4, tb * 128:(tb + 1) * 128],
                    ps_s1[:, :].rearrange("p (b tt) -> p b tt", b=4)[:, :, :],
                    deps=[tkm]))
            yield
        s1t["xlast"] = tkms
        tk_xf = p.nopf("dve", deps=tk_xtr)

        if ti == 0:
            FB[(0, -2)] = FB[(0, -1)] = FB[(-1, 46)] = FB[(-1, 47)] = None
            s1t["wq"] = {0: wq_dma(0, 0, [tk_wcv]), 1: wq_dma(0, 1, [tk_wcv])}
        WQ = s1t["wq"]
        pst = {}
        for fbx in range(49):
            if fbx < 48:
                fb = fbx
                sl = fb % 2
                bank = ps_q if sl == 0 else ps_s1
                qp = qpre[:, sl * (T1 + 4):(sl + 1) * (T1 + 4)]
                dg = diag4[:, sl * 512:(sl + 1) * 512]
                fprev = FB.get((ti, fb - 2), FB.get((ti - 1, fb + 46)))
                tkm = None
                for kt in range(16):
                    tkm = p.mm(bank[:, 0:SEGW],
                               wqb[:, sl * 2048 + kt * 128:
                                   sl * 2048 + (kt + 1) * 128],
                               xf[:, kt * T1: kt * T1 + SEGW],
                               kt == 0, kt == 15,
                               deps=[WQ[fb], tk_xf, fprev],
                               fence=(kt == 15))
                tk_ev = p.act(qp[:, 3: 3 + SEGW], bank[:, 0:SEGW], AF.Copy,
                              deps=[tkm])
                nfb, nti = fb + 2, ti
                if nfb >= 48:
                    nfb -= 48
                    nti += 1
                if nti < NT1:
                    WQ[nfb] = wq_dma(nti, nfb, [tkm, tk_wcv])
                tk_tl = p.cp("dve", qp[:, 0:3], tails[:, fb * 3:(fb + 1) * 3],
                             deps=[s1t["tail"][fb], fprev])
                s1t["tail"][fb] = p.cp("dve", tails[:, fb * 3:(fb + 1) * 3],
                                       qp[:, T1: T1 + 3], deps=[tk_ev, tk_tl])
                tkd = None
                for j in range(4):
                    tkd = p.ts(dg[:, j * 128:(j + 1) * 128], ident_b[:, :],
                               wcv[:, fb * 4 + j: fb * 4 + j + 1], None,
                               op.mult, deps=[tk_wcv, fprev], fence=(j == 3))
                pst[fb] = (bank, qp, dg, tk_ev, tk_tl, tkd)
            if fbx >= 1:
                fb = fbx - 1
                sl = fb % 2
                bank, qp, dg, tk_ev, tk_tl, tkd = pst.pop(fb)
                tkm = None
                for j in range(4):
                    tkm = p.mm(bank[:, 0:SEGW], dg[:, j * 128:(j + 1) * 128],
                               qp[:, j: j + SEGW], j == 0, j == 3,
                               deps=[tkd, tk_ev, tk_tl], fence=(j == 3))
                tk_cv = p.act(sq_out[:, sl * T1: sl * T1 + SEGW],
                              bank[:, 0:SEGW], AF.Silu,
                              deps=[tkm, SQ.get((ti, fb - 2),
                                                SQ.get((ti - 1, fb + 46)))])
                FB[(ti, fb)] = tk_cv
                if fb < 16:
                    blk = fb * 3 + 1
                elif fb < 32:
                    blk = (fb - 16) * 3
                else:
                    blk = (fb - 32) * 3 + 2
                c0 = t0 // 128
                SQ[(ti, fb)] = p.dma("pool",
                                     qkc_s[c0:c0 + NTB1, blk, :, :].rearrange(
                                         "cc pp tt -> pp cc tt"),
                                     sq_out[:, sl * T1:(sl + 1) * T1].rearrange(
                                         "pp (cc tt) -> pp cc tt", tt=128),
                                     deps=[tk_cv])
            yield

        if ti == 0:
            s1t["wsm"] = [p.dma("pool", wabsm[:, kt * 32:(kt + 1) * 32],
                                w_ab[kt * 128:(kt + 1) * 128, 0:32])
                          for kt in range(16)]
        tk_absm = []
        for tb in range(NTB1):
            tkm = None
            for kt in range(16):
                tkm = p.mm(ps_s1[:, 0:32],
                           xf[:, kt * T1 + tb * 128: kt * T1 + (tb + 1) * 128],
                           wabsm[:, kt * 32:(kt + 1) * 32],
                           kt == 0, kt == 15,
                           deps=s1t["wsm"] + [tk_xf, s1t["bg"],
                                              FB.get((ti, 47)),
                                              tk_absm[tb - 1] if tb >= 1 else
                                              tk_xtr[-1]],
                           fence=(kt == 15))
            tk_absm.append(p.cp("dve", sab[:, tb * 32:(tb + 1) * 32],
                                ps_s1[:, 0:32], deps=[tkm]))
            yield

        GD = s1t["gd"]

        def gsk(idx):
            return GD.get(("s", ti, idx // NTB1, idx % NTB1))

        for nch in range(8):
            ncol0 = 32 + nch * 256
            tk_wab = p.dma("pool", wabb[:, :].rearrange("p (kt n) -> p kt n", n=256),
                           w_ab[:, ncol0:ncol0 + 256].rearrange(
                               "(kt p) n -> p kt n", p=128),
                           deps=[GD.get(("mm", ti, nch - 1),
                                        GD.get(("mm", ti - 1, 7)))])
            for tb in range(NTB1):
                sl = tb % 2
                idx = nch * NTB1 + tb
                gbank = ps_q if idx % 2 == 0 else ps_s1
                gprev = (gsk(idx - 2) if idx >= 2 else
                         FB.get((ti, 46 + idx % 2)))
                tkm = None
                for kt in range(16):
                    tkm = p.mm(gbank[:, 0:256],
                               xf[:, kt * T1 + tb * 128: kt * T1 + (tb + 1) * 128],
                               wabb[:, kt * 256:(kt + 1) * 256],
                               kt == 0, kt == 15,
                               deps=[tk_wab, tk_xf, tk_absm[-1], gprev],
                               fence=(kt == 15))
                GD[("mm", ti, nch)] = tkm
                tk_si = p.act(sgate[:, sl * 256:(sl + 1) * 256],
                              gbank[:, 0:256], AF.Silu,
                              deps=[tkm, GD.get(("d", ti, nch, tb - 2),
                                                GD.get(("d", ti, nch - 1, tb + NTB1 - 2),
                                                       GD.get(("d", ti - 1, 7, tb + NTB1 - 2))))])
                tk_sd = p.dma("sp", sg_s[t0 + tb * 128: t0 + (tb + 1) * 128,
                                         ncol0 - 32: ncol0 - 32 + 256],
                              sgate[:, sl * 256:(sl + 1) * 256], deps=[tk_si])
                GD[("s", ti, nch, tb)] = tk_si
                GD[("d", ti, nch, tb)] = tk_sd
                yield

        aT = st1[:, 0:128]
        bbT = st1[:, 128:256]
        u_ = st1[:, 256:384]
        tmp_ = st1[:, 384:512]
        sp_ = st1[:, 512:640]
        gsl = st1[:, 640:768]
        lsl = st1[:, 768:896]
        tk_bg = s1t["bg"]
        tk_gd = []
        tk_gsil = GD[("s", ti, 7, NTB1 - 1)]
        for tb in range(NTB1):
            tka1 = p.tr(ps_s1[0:16, 0:128], sab[:, tb * 32: tb * 32 + 16],
                        ident_f[:, :], deps=[tk_absm[tb], tk_bg, tk_gsil])
            tc1 = p.cp("dve", aT, ps_s1[0:16, 0:128], deps=[tka1])
            tka2 = p.tr(ps_s1[0:16, 128:256], sab[:, tb * 32 + 16: tb * 32 + 32],
                        ident_f[:, :], deps=[tka1, tc1])
            tc2 = p.cp("dve", bbT, ps_s1[0:16, 128:256], deps=[tka2])
            tk_u = p.ts(u_, aT, dtb[:, 0:1], None, op.add, deps=[tc1])
            tsp = softplus(sp_, u_, tmp_, deps=[tk_u])
            tk_g = p.ts(gsl, sp_, nega[:, 0:1], None, op.mult,
                        deps=[tsp, tk_gd[-2] if len(tk_gd) >= 2 else
                              s1t.get("gd2")])
            tsp2 = softplus(sp_, bbT, tmp_, deps=[tc2, tk_g], scale=-1.0)
            tk_bg = p.ts(lsl, sp_, -1.0, LN2, op.mult, op.add,
                         deps=[tsp2, tk_gd[-1] if tk_gd else s1t.get("gd1")])
            tk_gd.append(p.dma("sp", gH_d[:, t0 + tb * 128:t0 + (tb + 1) * 128],
                               gsl, deps=[tk_g]))
            tk_gd.append(p.dma("sp", lnbH_d[:, t0 + tb * 128:t0 + (tb + 1) * 128],
                               lsl, deps=[tk_bg]))
            yield
        s1t["bg"] = tk_bg
        s1t["gd2"], s1t["gd1"] = tk_gd[-2], tk_gd[-1]
        s1t["ps1"] = tk_bg
        s1t["tile"][(ti, "gdump")] = p.nopf("sp", deps=tk_gd)
        s1t["tile"][(ti, "done")] = p.nopf(
            "sp", deps=[SQ[(ti, fb)] for fb in range(48)] +
            [GD[("d", ti, nch, tb)] for nch in range(8) for tb in range(NTB1)] +
            [s1t["tile"][(ti, "gdump")]])
        s1t["xfree"] = GD[("d", ti, 7, NTB1 - 1)]

    s1_q = [(ti, s1_gen(ti)) for ti in range(NT1)]
    s1_max = [1]
    def s1_drive(n=1):
        while n > 0 and s1_q:
            if s1_q[0][0] > s1_max[0]:
                return
            try:
                next(s1_q[0][1])
                n -= 1
            except StopIteration:
                s1_q.pop(0)

    def s1_finish(ti):
        while s1_q and s1_q[0][0] <= ti:
            try:
                next(s1_q[0][1])
            except StopIteration:
                s1_q.pop(0)

    tk_Sw = {h: None for h in range(H)}
    Ldict = {}
    MSL = dict(gc=0, m1=1, m2=2, m3=3, m4=4, tmp=5, exG=6)

    def msl(nm):
        i = MSL[nm]
        return mst[:, i * 128:(i + 1) * 128]

    tk_s2_init = p.mset("dve", S_sb[:, :], 0.0, fence=False)
    p.mset("dve", S_bf[:, :], 0.0, fence=False)
    p.mset("dve", lA[0:57, :], 0.0, fence=False)
    p.mset("dve", lST[0:57, :], 0.0, fence=False)
    tk_lz = p.mset("dve", lST2[0:57, :], 0.0)
    tk_s2_init = p.nopf("dve", deps=[
        p.dma("pool", lA[48:51, :], ones16[0:3, :], deps=[tk_lz]),
        p.dma("pool", lST[51:54, :], ones16[0:3, :], deps=[tk_lz]),
        p.dma("pool", lST2[54:57, :], ones16[0:3, :], deps=[tk_lz])])
    for h in range(H):
        tk_Sw[h] = tk_s2_init
    tk_ind = p.dma("pool", indb[0:48, :], c_ind[:, :])
    tk_qc = {}
    tk_sg = {}

    def emit_local(cc):
        par = cc % 2
        qcs = qc[:, par * 6144:(par + 1) * 6144]
        cols = colsb[:, par * 64:(par + 1) * 64]
        gcb = gcbb[:, par * 16:(par + 1) * 16]
        prevL = Ldict.get(cc - 1, {})
        out = {}
        ti_c = cc // CPT
        if cc not in tk_qc:
            tk_qc[cc] = p.dma("sp",
                              qc[:, par * 6144:(par + 1) * 6144].rearrange(
                                  "pp (b t) -> pp b t", t=128),
                              qkc_s[cc, :, :, :].rearrange("b pp t -> pp b t"),
                              deps=[s1t["tile"][(ti_c, "done")],
                                    Ldict.get(cc - 2, {}).get("qc_free")])
        cn = cc + 1
        if cn < NCH and cn not in tk_qc and (cn // CPT, "done") in s1t["tile"]:
            tk_qc[cn] = p.dma("sp",
                              qc[:, (1 - par) * 6144:(2 - par) * 6144].rearrange(
                                  "pp (b t) -> pp b t", t=128),
                              qkc_s[cn, :, :, :].rearrange("b pp t -> pp b t"),
                              deps=[s1t["tile"][(cn // CPT, "done")],
                                    prevL.get("qc_free")])
        tk_gl = p.dma("sp", ghb[:, par * 256: par * 256 + 128],
                      gH_d[:, cc * 128:(cc + 1) * 128],
                      deps=[s1t["tile"][(ti_c, "done")],
                            Ldict.get(cc - 2, {}).get("ghb_free")])
        tk_ll = p.dma("sp", ghb[:, par * 256 + 128: par * 256 + 256],
                      lnbH_d[:, cc * 128:(cc + 1) * 128],
                      deps=[s1t["tile"][(ti_c, "done")],
                            Ldict.get(cc - 2, {}).get("ghb_free")])
        tk_gc = p.raw("dve", lambda v, d=msl("gc"), o=ones16[:, :],
                      g_=ghb[:, par * 256: par * 256 + 128]:
                      v.tensor_tensor_scan(d, o, g_, 0.0, op.mult, op.add),
                      deps=[tk_s2_init, prevL.get("mst_free"), tk_gl])
        s1_drive()
        tkm = None
        tk_prevhalf = None
        for gg in range(2):
            tk_k2 = p.act(sqgb[:, :].rearrange("p (h kq t) -> p h kq t", h=8, kq=2)
                          [:, :, 0, :],
                          qcs.rearrange("p (h z t) -> p h z t", h=H, z=3)
                          [:, gg * 8:(gg + 1) * 8, 0, :],
                          AF.Square, deps=[tk_qc[cc], tk_prevhalf,
                                           prevL.get("gram_free")])
            tk_q2 = p.act(sqgb[:, :].rearrange("p (h kq t) -> p h kq t", h=8, kq=2)
                          [:, :, 1, :],
                          qcs.rearrange("p (h z t) -> p h z t", h=H, z=3)
                          [:, gg * 8:(gg + 1) * 8, 1, :],
                          AF.Square, deps=[tk_qc[cc]])
            for hh in range(8):
                h = gg * 8 + hh
                tkm = p.mm(pSQ[0:16, 0:256],
                           oh_b[:, h * 16:(h + 1) * 16],
                           sqgb[:, hh * 256:(hh + 1) * 256],
                           h == 0, h == 15,
                           deps=[tk_k2, tk_q2, prevL.get("sqbank_free")],
                           fence=(hh == 7))
            tk_prevhalf = tkm
            s1_drive()
        tk_ln = p.act(lnssq[:, :], pSQ[0:16, 0:256], AF.Ln, bias=L2EPS,
                      deps=[tkm])
        p.stt(msl("m1"), lnssq[:, 0:128],
              -0.5, msl("gc"), op.mult, op.add, deps=[tk_ln, tk_gc], fence=False)
        tk_m2a = p.stt(msl("tmp"), lnssq[:, 0:128], -0.5,
                       ghb[:, par * 256 + 128: par * 256 + 256],
                       op.mult, op.add, deps=[tk_ln, tk_ll])
        tk_m2b = p.tt(msl("m2"), msl("tmp"), msl("gc"), op.subtract)
        out["ghb_free"] = tk_m2b
        p.stt(msl("tmp"), lnssq[:, 128:256], -0.5, msl("gc"), op.mult, op.add,
              fence=False)
        p.ts(msl("m3"), msl("tmp"), LN_HALF_DK, None, op.add, fence=False)
        tk_m = p.ts(msl("m4"), msl("m2"), msl("gc")[:, 127:128], None, op.add)
        out["mst_done"] = tk_m
        tk_pc = []
        for mi, nm in enumerate(("m1", "m2", "m3")):
            b0 = mi * 3
            p.cp("dve", mpc[:, b0 * 128:(b0 + 1) * 128], msl(nm), deps=[tk_m],
                 fence=False)
            p.tt(msl("tmp"), msl(nm), mpc[:, b0 * 128:(b0 + 1) * 128], op.subtract,
                 fence=False)
            p.cp("dve", mpc[:, (b0 + 1) * 128:(b0 + 2) * 128], msl("tmp"), fence=False)
            p.tt(msl("tmp"), msl("tmp"), mpc[:, (b0 + 1) * 128:(b0 + 2) * 128],
                 op.subtract, fence=False)
            tk_pc.append(p.cp("dve", mpc[:, (b0 + 2) * 128:(b0 + 3) * 128],
                              msl("tmp")))
        tk_pc = tk_pc[-1]
        tk_asm = []
        for j in range(3):
            tk_asm.append(p.dma("sp", lA[16 * j:16 * (j + 1), :],
                                mpc[:, j * 128:(j + 1) * 128],
                                deps=[tk_pc, prevL.get("lhs_free"), tk_ind]))
            tk_asm.append(p.dma("sp", lST[16 * j:16 * (j + 1), :],
                                mpc[:, (3 + j) * 128:(4 + j) * 128],
                                deps=[tk_pc, prevL.get("lhs_free")]))
            tk_asm.append(p.dma("pool", lST2[16 * j:16 * (j + 1), :],
                                mpc[:, (3 + j) * 128:(4 + j) * 128],
                                deps=[tk_pc, prevL.get("lhs_free")]))
        for oi, mb in enumerate((3, 0, 6)):
            for j in range(3):
                tk_asm.append(p.dma("pool", indb[48 + oi * 3 + j:
                                                 49 + oi * 3 + j, :],
                                    mpc[:, (mb + j) * 128:(mb + j + 1) * 128],
                                    deps=[tk_pc, prevL.get("lhs_free"),
                                          tk_ind]))
        if locstop == 1:
            Ldict[cc] = out
            return out
        tk_tr1 = p.tr(pSQ[:, 512:528], msl("m1"), ident_f[0:16, 0:16],
                      deps=[tk_m, prevL.get("sqbank_free2")])
        tk_tr2 = p.tr(pSQ[:, 528:544], msl("m4"), ident_f[0:16, 0:16], deps=[tk_m])
        tk_tr3 = p.tr(pSQ[:, 544:560], msl("m3"), ident_f[0:16, 0:16], deps=[tk_m])
        tk_cols = p.act(cols[:, 0:48], pSQ[:, 512:560], AF.Exp,
                        deps=[tk_tr1, tk_tr2, tk_tr3, prevL.get("cols_free")])
        out["cols"] = tk_cols
        s1_drive()
        tk_xg = p.act(msl("exG")[:, 0:1], msl("gc")[:, 127:128], AF.Exp,
                      deps=[tk_gc])
        tk_eg = p.tr(pSQ[0:1, 560:576], msl("exG")[:, 0:1], ident_f[0:16, 0:16],
                     deps=[tk_xg, tk_cols])
        tk_egc = p.cp("dve", grow[0:1, 0:16], pSQ[0:1, 560:576], deps=[tk_eg])
        tk_gcb = p.mm(pSQ[:, 576:592], ones1f[:, :], grow[0:1, 0:16], True, True,
                      deps=[tk_egc], fence=True)
        out["gcb"] = p.cp("dve", gcb[:, 0:16], pSQ[:, 576:592],
                          deps=[tk_gcb, prevL.get("gcb_free")])
        s1_drive()

        if locstop == 2:
            Ldict[cc] = out
            return out
        tk_last_ch = prevL.get("ch_last")
        tk_w7 = []
        tk_gbar = []
        tk_sq_last = None
        for g in range(2):
            h0 = g * 8

            def kqv(hh):
                b = ((h0 + hh) * 3) * 128
                return (qcs[:, b + 128:b + 256], qcs[:, b:b + 128],
                        qcs[:, b + 256:b + 384])

            tkmh = [None, None]
            for hh in range(8):
                q_, k_, v_ = kqv(hh)
                tkmh[hh // 4] = p.mm(pCH[:, hh * 128:(hh + 1) * 128], k_,
                                     ident_b[:, :], True, True,
                                     deps=[tk_qc[cc], tk_cols, out["gcb"],
                                           tk_last_ch],
                                     fence=(hh % 4 == 3))
            tk_ktil = []
            for hh in range(8):
                h = h0 + hh
                tk_ktil.append(p.ts(r0b[:, hh * 256 + 128: hh * 256 + 256],
                                    pCH[:, hh * 128:(hh + 1) * 128],
                                    cols[:, h:h + 1], -1.0, op.mult, op.mult,
                                    deps=[tkmh[hh // 4],
                                          prevL.get("r0_free") if g == 0
                                          else None]))
                tk_ktil.append(p.ts(kdecb[:, par * 2048 + h * 128:
                                          par * 2048 + (h + 1) * 128],
                                    pCH[:, hh * 128:(hh + 1) * 128],
                                    cols[:, 16 + h:16 + h + 1], None, op.mult,
                                    deps=[tkmh[hh // 4],
                                          prevL.get("kdec_free")]))
            tk_ktil = p.nopf("dve", deps=tk_ktil)
            s1_drive()
            if locstop == 3:
                Ldict[cc] = out
                return out
            tkg4 = [None] * 4
            for hh in range(8):
                q_, k_, v_ = kqv(hh)
                dst = pCH if hh < 4 else pSQ
                off = (hh % 4) * 256
                b = ((h0 + hh) * 3) * 128
                tkg4[hh // 2] = p.mm(dst[:, off:off + 256], k_,
                                     qcs[:, b:b + 256], True, True,
                                     deps=[tk_ktil, tk_qc[cc], tk_sq_last,
                                           tk_ln],
                                     fence=(hh % 2 == 1))
            tk_ge = [p.cp("dve", sqgb[:, qb * 512:(qb + 1) * 512],
                          (pCH if qb < 2 else pSQ)[:, (qb % 2) * 512:
                                                   (qb % 2 + 1) * 512],
                          deps=[tkg4[qb]]) for qb in range(4)]
            tk_gram = p.nopf("dve", deps=tk_ge)
            s1_drive()
            if locstop == 4:
                Ldict[cc] = out
                return out

            def diff_build(orient, mask_i, out_sb, sign, kq, mm_deps,
                           exp_deps):
                lhs = (lA, lST, lST2)[orient]
                tkk_h = [None, None]
                for half in range(2):
                    p.mm(pCH[:, half * 512:(half + 1) * 512], lhs[:, :],
                         indb[:, g * 1024 + half * 512:
                              g * 1024 + (half + 1) * 512],
                         True, False,
                         deps=list(mm_deps) + tk_asm + [tk_ind],
                         fence=False)
                    tkk_h[half] = p.mm(
                        pCH[:, half * 512:(half + 1) * 512], ident_b[:, :],
                        masks_b[:, mask_i * 1024 + half * 512:
                                mask_i * 1024 + (half + 1) * 512],
                        False, True, deps=[], fence=True)
                tks = []
                tke = None
                for half in range(2):
                    tke = p.act(expd[:, :], pCH[:, half * 512:(half + 1) * 512],
                                AF.Exp,
                                deps=[tkk_h[half],
                                      tks[-1] if tks else None] +
                                list(exp_deps))
                    tks.append(p.stt(
                        out_sb.rearrange("p (hh s) -> p hh s", hh=8)
                        [:, half * 4:(half + 1) * 4, :],
                        expd[:, :].rearrange("p (hh s) -> p hh s", hh=4),
                        float(sign),
                        sqgb[:, :].rearrange("p (hh kq s) -> p hh kq s", hh=8, kq=2)
                        [:, half * 4:(half + 1) * 4, kq, :],
                        op.mult, op.mult, deps=[tke, tk_gram]))
                return p.nopf("dve", deps=tks), tkk_h[1], tke

            tk_n0, tke0, txp0 = diff_build(0, 0, n0b[:, 0:1024], -1.0, 0,
                                           [tk_gram], [tk_gram])
            s1_drive()
            tk_nb0, tke1, txp1 = diff_build(1, 1, n0b[:, 1024:2048], -1.0, 0,
                                            [txp0], [tk_n0])
            s1_drive()
            tk_wb, tke2, txp2 = diff_build(2, 2,
                                           wbarb[:, par * 2048 + g * 1024:
                                                 par * 2048 + (g + 1) * 1024],
                                           1.0, 1, [txp1], [tk_nb0])
            out[("wbar", g)] = tk_wb
            out["lhs_free"] = tke2
            if locstop == 5:
                Ldict[cc] = out
                return out

            tkmh = [None, None]
            for hh in range(8):
                q_, k_, v_ = kqv(hh)
                tkmh[hh // 4] = p.mm(pCH[:, hh * 128:(hh + 1) * 128], v_,
                                     ident_b[:, :], True, True,
                                     deps=[tk_qc[cc], txp2, tk_wb],
                                     fence=(hh % 4 == 3))
            tk_r0 = [p.cp("dve",
                          r0b[:, bh * 1024:(bh + 1) * 1024]
                          .rearrange("p (hh vk t) -> p hh vk t",
                                     hh=4, vk=2)[:, :, 0, :],
                          pCH[:, bh * 512:(bh + 1) * 512]
                          .rearrange("p (hh t) -> p hh t", hh=4),
                          deps=[tkmh[bh]]) for bh in range(2)]
            tk_r0 = p.nopf("dve", deps=tk_r0)
            s1_drive()
            if locstop == 6:
                Ldict[cc] = out
                return out

            for ps in range(2):
                hh0 = ps * 4
                tk_w0 = p.nopf("dve", deps=[tk_r0, tk_w7[-1] if tk_w7 else None])
                tk_wh = [tk_w0, tk_w0]
                tk_ncur = p.nopf("act", deps=[tk_n0, tk_sq_last])
                tk_bcur = tk_nb0
                ncur = 0
                for lev in range(7):
                    tkh = [None, None]
                    for half in range(2):
                        for hh4 in (half * 2, half * 2 + 1):
                            hh = hh0 + hh4
                            lhsT = (n0b[:, 1024 + hh * 128:1024 + (hh + 1) * 128]
                                    if lev == 0 else
                                    nbar[:, ncur * 512 + hh4 * 128:
                                         ncur * 512 + (hh4 + 1) * 128])
                            rhs = (r0b[:, hh * 256:(hh + 1) * 256] if lev == 0
                                   else wchn[:, (lev % 2) * 1024 + hh4 * 256:
                                             (lev % 2) * 1024 + (hh4 + 1) * 256])
                            p.mm(pCH[:, hh4 * 256:(hh4 + 1) * 256], lhsT, rhs,
                                 hh4 % 2 == 0, False,
                                 deps=[tk_wh[half], tk_bcur], fence=False)
                        for hh4 in (half * 2, half * 2 + 1):
                            hh = hh0 + hh4
                            rhs = (r0b[:, hh * 256:(hh + 1) * 256] if lev == 0
                                   else wchn[:, (lev % 2) * 1024 + hh4 * 256:
                                             (lev % 2) * 1024 + (hh4 + 1) * 256])
                            tkh[half] = p.mm(
                                pCH[:, hh4 * 256:(hh4 + 1) * 256],
                                ident_b[:, :], rhs, False, hh4 % 2 == 1,
                                deps=[], fence=(hh4 == half * 2 + 1))
                    if lev == 6:
                        base = par * 2048 + g * 1024 + hh0 * 128
                        tvl = []
                        for h in range(2):
                            tvl.append(p.cp(
                                "dve",
                                trvb[:, base + h * 256: base + h * 256 + 256]
                                .rearrange("p (hh t) -> p hh t", hh=2),
                                pCH[:, h * 512:(h + 1) * 512].rearrange(
                                    "p (hh vk t) -> p hh vk t", hh=2, vk=2)
                                [:, :, 0, :],
                                deps=[tkh[h],
                                      Ldict.get(cc - 1, {}).get("trv_free")
                                      if h == 0 else None]))
                        tgl = []
                        for h in range(2):
                            tgl.append(p.cp(
                                "act",
                                gtrb[:, hh0 * 128 + h * 256:
                                     hh0 * 128 + h * 256 + 256]
                                .rearrange("p (hh t) -> p hh t", hh=2),
                                pCH[:, h * 512:(h + 1) * 512].rearrange(
                                    "p (hh vk t) -> p hh vk t", hh=2, vk=2)
                                [:, :, 1, :],
                                deps=[tkh[h], tvl[h]]))
                        tk_wcur = p.nopf("dve", deps=tvl + tgl)
                    else:
                        ce = "dve" if lev % 2 == 0 else "act"
                        nxs = ((lev + 1) % 2) * 1024
                        tk_wh = [p.cp(ce, wchn[:, nxs + h * 512:
                                               nxs + (h + 1) * 512],
                                      pCH[:, h * 512:(h + 1) * 512],
                                      deps=[tkh[h]]) for h in range(2)]
                    if lev < 6:
                        tks = None
                        for hh4 in range(4):
                            hh = hh0 + hh4
                            lhsT = (n0b[:, 1024 + hh * 128:1024 + (hh + 1) * 128]
                                    if lev == 0 else
                                    nbar[:, ncur * 512 + hh4 * 128:
                                         ncur * 512 + (hh4 + 1) * 128])
                            rhsn = (n0b[:, hh * 128:(hh + 1) * 128] if lev == 0
                                    else ntok[:, ncur * 512 + hh4 * 128:
                                              ncur * 512 + (hh4 + 1) * 128])
                            tks = p.mm(pSQ[:, hh4 * 128:(hh4 + 1) * 128],
                                       lhsT, rhsn, True, True,
                                       deps=[tk_ncur, tk_bcur],
                                       fence=(hh4 == 3))
                        tksA = tks
                        for hh4 in range(4):
                            hh = hh0 + hh4
                            lhsTn = (n0b[:, hh * 128:(hh + 1) * 128] if lev == 0
                                     else ntok[:, ncur * 512 + hh4 * 128:
                                               ncur * 512 + (hh4 + 1) * 128])
                            rhsb = (n0b[:, 1024 + hh * 128:1024 + (hh + 1) * 128]
                                    if lev == 0 else
                                    nbar[:, ncur * 512 + hh4 * 128:
                                         ncur * 512 + (hh4 + 1) * 128])
                            tks = p.mm(pSQ[:, 512 + hh4 * 128:
                                           512 + (hh4 + 1) * 128],
                                       lhsTn, rhsb, True, True,
                                       deps=[], fence=(hh4 == 3))
                        nnx = 1 - ncur
                        cn = "act" if lev % 2 == 0 else "dve"
                        tk_ncur = p.cp(cn, ntok[:, nnx * 512:(nnx + 1) * 512],
                                       pSQ[:, 0:512], deps=[tksA])
                        tk_bcur = p.cp(cn, nbar[:, nnx * 512:(nnx + 1) * 512],
                                       pSQ[:, 512:1024], deps=[tks])
                        tk_sq_last = tk_bcur
                        ncur = nnx
                    s1_drive()
                    if locstop == 60 + ps * 10 + lev:
                        Ldict[cc] = out
                        return out
                tk_w7.append(tk_wcur)
            if locstop == 7:
                Ldict[cc] = out
                return out

            tkmh = [None, None]
            for hh in range(8):
                tkmh[hh // 4] = p.mm(pCH[:, hh * 128:(hh + 1) * 128],
                                     gtrb[:, hh * 128:(hh + 1) * 128],
                                     ident_b[:, :], True, True,
                                     deps=tk_w7[-2:], fence=(hh % 4 == 3))
            for bh in range(2):
                tk_gbar.append(p.cp(
                    "dve",
                    gbarb[:, par * 2048 + g * 1024 + bh * 512:
                          par * 2048 + g * 1024 + (bh + 1) * 512],
                    pCH[:, bh * 512:(bh + 1) * 512],
                    deps=[tkmh[bh], prevL.get("gbar_free")]))
            tk_last_ch = tk_gbar[-1]
            s1_drive()
            if locstop == 8:
                Ldict[cc] = out
                return out

        out["w7"] = p.nopf("dve", deps=tk_w7)
        out["gbar"] = p.nopf("dve", deps=tk_gbar)
        out["gram_free"] = out[("wbar", 1)]
        out["kdec"] = tk_ktil
        out["ch_last"] = tk_last_ch
        out["sqbank_free"] = tk_sq_last
        out["sqbank_free2"] = tk_sq_last
        out["mst_free"] = out["cols"]
        Ldict[cc] = out
        return out

    def emit_serial(cc):
        par = cc % 2
        qcs = qc[:, par * 6144:(par + 1) * 6144]
        cols = colsb[:, par * 64:(par + 1) * 64]
        gcb = gcbb[:, par * 16:(par + 1) * 16]
        lc = Ldict[cc]
        out = {}
        tk_qsl = []
        tk_o = []
        tk_sup = []
        for g in range(2):
            h0 = g * 8
            tkmh = [None, None]
            for hh in range(8):
                h = h0 + hh
                tkmh[hh // 4] = p.mm(
                    pSU[:, hh * 128:(hh + 1) * 128],
                    gbarb[:, par * 2048 + h * 128:par * 2048 + (h + 1) * 128],
                    S_bf[:, h * 128:(h + 1) * 128], True, True,
                    deps=[lc["gbar"], tk_Sw[h],
                          tk_o[-1] if tk_o else None,
                          tk_sup[-1] if tk_sup else
                          Ldict.get(cc - 1, {}).get("su_free")],
                    fence=(hh % 4 == 3))
            tk_ueh = []
            for bh in range(2):
                tk_ueh.append(p.stt(
                    ub[:, bh * 512:(bh + 1) * 512]
                    .rearrange("p (hh t) -> p hh t", hh=4),
                    pSU[:, bh * 512:(bh + 1) * 512]
                    .rearrange("p (hh t) -> p hh t", hh=4), 1.0,
                    trvb[:, par * 2048 + g * 1024 + bh * 512:
                         par * 2048 + g * 1024 + (bh + 1) * 512]
                    .rearrange("p (hh t) -> p hh t", hh=4),
                    op.mult, op.add, deps=[tkmh[bh], lc["w7"]]))
            tk_ue = tk_ueh[-1]
            out[("u", g)] = tk_ue
            s1_drive()
            tkqh = [None, None]
            for hh in range(8):
                q_ = qcs[:, ((h0 + hh) * 3) * 128 + 128:
                         ((h0 + hh) * 3) * 128 + 256]
                h = h0 + hh
                tkqh[hh // 4] = p.mm(pSU[:, hh * 128:(hh + 1) * 128], q_,
                                     S_bf[:, h * 128:(h + 1) * 128], True, True,
                                     deps=[tk_Sw[h], tk_ueh[hh // 4]],
                                     fence=(hh % 4 == 3))
            tqs = []
            for hh in range(8):
                h = h0 + hh
                tqs.append(p.ts(qst[:, g * 1024 + hh * 128:
                                    g * 1024 + (hh + 1) * 128],
                                pSU[:, hh * 128:(hh + 1) * 128],
                                cols[:, 32 + h:32 + h + 1], None, op.mult,
                                deps=[tkqh[hh // 4]]))
            tk_qs = p.nopf("dve", deps=tqs)
            tk_qsl.append(tk_qs)
            s1_drive()
            tkm4h = [None, None]
            for hh in range(8):
                h = h0 + hh
                tkm4h[hh // 4] = p.mm(
                    pSU[:, hh * 128:(hh + 1) * 128],
                    kdecb[:, par * 2048 + h * 128:par * 2048 + (h + 1) * 128],
                    ub[:, hh * 128:(hh + 1) * 128], True, True,
                    deps=[tk_ueh[hh // 4], lc["kdec"], tqs[hh // 4 * 4 + 3]],
                    fence=(hh % 4 == 3))
            for hh in range(8):
                h = h0 + hh
                tk_su = p.stt(S_sb[:, h * 128:(h + 1) * 128],
                              S_sb[:, h * 128:(h + 1) * 128],
                              gcb[:, h:h + 1], pSU[:, hh * 128:(hh + 1) * 128],
                              op.mult, op.add, deps=[tkm4h[hh // 4], lc["gcb"]])
                tk_Sw[h] = p.cp("dve", S_bf[:, h * 128:(h + 1) * 128],
                                S_sb[:, h * 128:(h + 1) * 128], deps=[tk_su])
                tk_sup.append(tk_Sw[h])
            s1_drive()
            tk3h = [None, None]
            for hh in range(8):
                h = h0 + hh
                tk3h[hh // 4] = p.mm(
                    pSU[:, hh * 128:(hh + 1) * 128],
                    wbarb[:, par * 2048 + h * 128:par * 2048 + (h + 1) * 128],
                    ub[:, hh * 128:(hh + 1) * 128], True, True,
                    deps=[tk_ueh[hh // 4], lc[("wbar", g)],
                          tk_sup[g * 8 + hh // 4 * 4 + 3]],
                    fence=(hh % 4 == 3))
            for bh in range(2):
                tk_o.append(p.stt(
                    qst[:, g * 1024 + bh * 512:g * 1024 + (bh + 1) * 512],
                    qst[:, g * 1024 + bh * 512:g * 1024 + (bh + 1) * 512], 1.0,
                    pSU[:, bh * 512:(bh + 1) * 512], op.mult, op.add,
                    deps=[tk3h[bh], tqs[bh * 4 + 3]]))
            s1_drive()
        out["o"] = tk_o
        out["s_done"] = tk_sup[-1]
        lc["su_free"] = tk_o[-1]
        lc["qc_free"] = p.nopf("pe", deps=[])
        lc["trv_free"] = out[("u", 1)]
        lc["gbar_free"] = out[("u", 1)]
        lc["kdec_free"] = out["s_done"]
        lc["cols_free"] = p.nopf("dve", deps=tk_qsl + [out["s_done"]])
        lc["gcb_free"] = out["s_done"]
        return out

    def emit_opost(cc, ser):
        par = cc % 2
        sgs = sgc[:, par * D:(par + 1) * D]
        lc = Ldict[cc]
        tk_ofd = []
        if cc not in tk_sg:
            tk_sg[cc] = p.dma("sp", sgc[:, par * D:(par + 1) * D],
                              sg_s[cc * 128:(cc + 1) * 128, :],
                              deps=[s1t["tile"][(cc // CPT, "done")],
                                    Ldict.get(cc - 2, {}).get("sg_free")])
        cn = cc + 1
        if (cn < NCH and cn not in tk_sg
                and (cn // CPT, "done") in s1t["tile"]):
            tk_sg[cn] = p.dma("sp", sgc[:, (1 - par) * D:(2 - par) * D],
                              sg_s[cn * 128:(cn + 1) * 128, :],
                              deps=[s1t["tile"][(cn // CPT, "done")],
                                    Ldict.get(cc - 1, {}).get("sg_free")])
        tk_og_l = []
        for g in range(2):
            h0 = g * 8
            tk_og = p.tt(ost[:, :], qst[:, g * 1024:(g + 1) * 1024],
                         sgs[:, g * 1024:(g + 1) * 1024], op.mult,
                         deps=[ser["o"][2 * g + 1], tk_sg[cc]])
            tk_og_l.append(tk_og)
            tk_s2 = p.act(qst[:, g * 1024:(g + 1) * 1024], ost[:, :],
                          AF.Square, deps=[tk_og])
            tk_red = p.red(rms[:, g * 8:(g + 1) * 8],
                           qst[:, g * 1024:(g + 1) * 1024].rearrange(
                               "p (hh s) -> p hh s", hh=8),
                           op.add, deps=[tk_s2])
            p.act(rms[:, 16 + g * 8:16 + (g + 1) * 8], rms[:, g * 8:(g + 1) * 8],
                  AF.Ln, bias=EPS, scale=1.0 / 128.0, deps=[tk_red], fence=False)
            tk_rs = p.act(rms[:, 32 + g * 8:32 + (g + 1) * 8],
                          rms[:, 16 + g * 8:16 + (g + 1) * 8], AF.Exp, scale=-0.5)
            tk_on = []
            for hh in range(8):
                h = h0 + hh
                tk_on.append(p.stt(onb[:, hh * 128:(hh + 1) * 128],
                                   ost[:, hh * 128:(hh + 1) * 128],
                                   rms[:, 32 + g * 8 + hh:32 + g * 8 + hh + 1],
                                   wtile[:, h * 128:(h + 1) * 128],
                                   op.mult, op.mult, deps=[tk_rs, tk_og]))
            tkmh = [None, None]
            for hh in range(8):
                tkmh[hh // 4] = p.mm(pSU[:, hh * 128:(hh + 1) * 128],
                                     onb[:, hh * 128:(hh + 1) * 128],
                                     ident_b[:, :], True, True,
                                     deps=[tk_on[hh], ser["o"][3]],
                                     fence=(hh % 4 == 3))
            tk_ofe = [p.cp("dve", ofst[:, bh * 512:(bh + 1) * 512],
                           pSU[:, bh * 512:(bh + 1) * 512],
                           deps=[tkmh[bh], tk_ofd[-1] if tk_ofd else
                                 Ldict.get(cc - 1, {}).get("opost_done")])
                      for bh in range(2)]
            tk_ofe = p.nopf("dve", deps=tk_ofe)
            s1_drive()
            tk_ofd.append(p.dma("pool",
                                of_s[cc, h0:h0 + 8, :, :].rearrange(
                                    "hh pp tt -> pp hh tt"),
                                ofst[:, :].rearrange("pp (hh tt) -> pp hh tt",
                                                     tt=128),
                                deps=[tk_ofe]))
            lc["su_free"] = tk_ofe
        lc["sg_free"] = p.nopf("dve", deps=tk_og_l)
        lc["opost_done"] = p.nopf("pool", deps=tk_ofd)
        return tk_ofd

    if phases == "s1":
        s1_max[0] = NT1
        s1_finish(NT1 - 1)
        p.barrier()
        esA.close()
        es12.close()
        p.replay()
        es.close()
        return nc
    s1_finish(0)
    s1_drive(12)
    tk_sg[0] = p.dma("sp", sgc[:, 0:D], sg_s[0:128, :],
                     deps=[s1t["tile"][(0, "done")]])
    emit_local(0)
    stop_cc = NCH
    stop_ser_only = False
    if phases.startswith("cc"):
        stop_cc = min(int(phases[2:].rstrip("s")), NCH)
        stop_ser_only = phases.endswith("s")
    for cc in range(stop_cc):
        s1_max[0] = cc // CPT + 2
        ser = emit_serial(cc)
        if stop_ser_only and cc == stop_cc - 1:
            break
        if cc + 1 < NCH:
            s1_finish((cc + 1) // CPT)
            emit_local(cc + 1)
        emit_opost(cc, ser)
    s1_max[0] = NT1
    s1_finish(NT1 - 1)
    if stop_cc < NCH or stop_ser_only:
        p.barrier()
        esA.close()
        es12.close()
        p.replay()
        es.close()
        return nc
    tk_of_all = [Ldict[cc]["opost_done"] for cc in range(NCH)]
    p.barrier()
    esA.close()
    es12.close()
    if phases == "A":
        p.replay()
        es.close()
        return nc

    esf = ExitStack()
    SF = esf.enter_context
    fw = SF(nc.sbuf_tensor("fw", [128, 2 * D], F32)).ap()
    facc = SF(nc.sbuf_tensor("facc", [128, 16], F32)).ap()
    ftmp = SF(nc.sbuf_tensor("ftmp", [128, D], F32)).ap()
    ftrn = SF(nc.sbuf_tensor("ftrn", [128, D], BF16)).ap()
    ftrn2 = SF(nc.sbuf_tensor("ftrn2", [128, D], BF16)).ap()
    fsm = SF(nc.sbuf_tensor("fsm", [1, 16], F32)).ap()
    fcol = SF(nc.sbuf_tensor("fcol", [128, 1], F32)).ap()
    ps_f = SF(nc.psum_tensor("ps_f", [128, 1024], F32)).ap()

    tk_blk = []
    for rb in range(16):
        sl = rb % 2
        tk_l = p.dma("sp", fw[:, sl * D:(sl + 1) * D],
                     ffn_w[rb * 128:(rb + 1) * 128, :],
                     deps=[tk_blk[rb - 2] if rb >= 2 else None])
        tk_blk.append(p.red(facc[:, rb:rb + 1], fw[:, sl * D:(sl + 1) * D],
                            op.add, absv=True, deps=[tk_l]))
    tk_r = p.red(fcol[:, 0:1], facc[:, :], op.add, deps=tk_blk)
    tk_t = p.tr(ps_f[0:1, 0:128], fcol[:, 0:1], ident_f[:, :], deps=[tk_r])
    tk_r2 = p.red(fsm[0:1, 0:1], ps_f[0:1, 0:128], op.add, deps=[tk_t])
    p.ts(fsm[0:1, 1:2], fsm[0:1, 0:1], 1.0 / (D * D), EPS, op.mult, op.max,
         deps=[tk_r2], fence=False)
    p.raw("dve", lambda v: v.reciprocal(fsm[0:1, 2:3], fsm[0:1, 1:2]), fence=False)
    tk_al = p.ts(fsm[0:1, 3:4], fsm[0:1, 1:2], 1.0 / 127.0, None, op.mult)
    if dbg:
        p.dma("sp", alpha_d[:, :], fsm[0:1, 0:4], deps=[tk_al])
    tk_bc = p.mm(ps_f[:, 128:132], ones1f[:, :], fsm[0:1, 0:4], True, True,
                 deps=[tk_al, tk_t] + list(TC), fence=True)
    tk_at = p.cp("dve", alpha_t[:, 0:4], ps_f[:, 128:132], deps=[tk_bc])

    tk_rd = {}
    for rb in range(16):
        sl = rb % 2
        tk_l = p.dma("sp", fw[:, sl * D:(sl + 1) * D],
                     ffn_w[rb * 128:(rb + 1) * 128, :],
                     deps=[tk_at, tk_rd.get(rb - 2)])
        tk1 = p.act(ftmp[:, :], fw[:, sl * D:(sl + 1) * D], AF.Copy,
                    bias=MAGIC, scale=alpha_t[:, 2:3],
                    deps=[tk_l, tk_at, tk_rd.get(("ts", rb - 1))])
        tk_rd[rb] = tk1
        tk2 = p.act(ftmp[:, :], ftmp[:, :], AF.Copy, bias=-MAGIC, deps=[tk1])
        tk3 = p.ts(ftrn[:, :], ftmp[:, :], 1.0, -1.0, op.min, op.max,
                   deps=[tk2, tk_rd.get(("tr", rb - 1))])
        tk_rd[("ts", rb)] = tk3
        tk_cps = []
        for ib in range(16):
            poff = (ib % 2) * 512
            tkt = p.mm(ps_f[:, poff:poff + 128],
                       ftrn[:, ib * 128:(ib + 1) * 128], ident_b[:, :], True, True,
                       deps=[tk3, tk_cps[ib - 2] if ib >= 2 else None],
                       fence=True)
            tk_cps.append(p.cp("dve", ftrn2[:, ib * 128:(ib + 1) * 128],
                               ps_f[:, poff:poff + 128],
                               deps=[tkt, tk_rd.get(("dma", rb - 1, ib))]))
            tk_rd[("tr", rb)] = tkt
            tk_rd[("dma", rb, ib)] = p.dma(
                "pool", tern_s[ib * 128:(ib + 1) * 128, rb * 128:(rb + 1) * 128],
                ftrn2[:, ib * 128:(ib + 1) * 128], deps=[tk_cps[-1]])
    p.barrier()
    esf.close()
    if phases == "ffn":
        p.replay()
        es.close()
        return nc

    _stage3(nc, p, es, T, T3, NT3, x_in, w_o, of_s, tern_s, y_out,
            ident_b, ident_f, btile, alpha_t)

    p.replay()
    es.close()
    return nc


def _stage3(nc, p, es, T, T3, NT3, x_in, w_o, of_s, tern_s, y_out,
            ident_b, ident_f, btile, alpha_t):
    SG = es.enter_context
    wo_sb = SG(nc.sbuf_tensor("wo_sb", [128, 16 * D], BF16)).ap()
    tern_sb = SG(nc.sbuf_tensor("tern_sb", [128, 16 * D], BF16)).ap()
    ofsb = SG(nc.sbuf_tensor("ofsb", [128, 2 * D], BF16)).ap()
    xin = SG(nc.sbuf_tensor("xin", [128, D], F32)).ap()
    scr = SG(nc.sbuf_tensor("scr", [128, D], F32)).ap()
    dsc = SG(nc.sbuf_tensor("dsc", [128, 512], F32)).ap()
    yb = SG(nc.sbuf_tensor("yb", [128, 2 * D], F32)).ap()
    qx = SG(nc.sbuf_tensor("qx", [128, 2 * D], BF16)).ap()
    qxf = SG(nc.sbuf_tensor("qxf", [128, D], BF16)).ap()
    zb = SG(nc.sbuf_tensor("zb", [128, D], F32)).ap()
    outb = SG(nc.sbuf_tensor("outb", [128, D], F32)).ap()
    stats = SG(nc.sbuf_tensor("stats", [128, 48], F32)).ap()
    ps2 = [SG(nc.psum_tensor("psA", [128, 2048], F32)).ap(),
           SG(nc.psum_tensor("psB", [128, 2048], F32)).ap()]

    tk_wo = [p.dma("pool", wo_sb[:, kt * D:(kt + 1) * D],
                   w_o[kt * 128:(kt + 1) * 128, :]) for kt in range(16)]
    tk_tern = [p.dma("sp", tern_sb[:, kt * D:(kt + 1) * D],
                     tern_s[kt * 128:(kt + 1) * 128, :]) for kt in range(16)]
    tk_out = []
    prev = {}
    ofld = {}
    NTB = T // 128

    def emit_attn(tb):
        sl = tb % 2
        ps = ps2[sl]
        st = stats[:, sl * 24:(sl + 1) * 24]
        yv = yb[:, sl * D:(sl + 1) * D]
        qv = qx[:, sl * D:(sl + 1) * D]
        if tb not in ofld:
            ofld[tb] = p.dma("sp", ofsb[:, sl * D:(sl + 1) * D].rearrange(
                                 "pp (h tt) -> pp h tt", tt=128),
                             of_s[tb, :, :, :].rearrange("h pp tt -> pp h tt"),
                             deps=[prev.get(("attnpe", tb - 2))])
        tk_of = ofld[tb]
        if tb + 1 < NTB:
            ofld[tb + 1] = p.dma(
                "sp", ofsb[:, (1 - sl) * D:(2 - sl) * D].rearrange(
                    "pp (h tt) -> pp h tt", tt=128),
                of_s[tb + 1, :, :, :].rearrange("h pp tt -> pp h tt"),
                deps=[prev.get(("attnpe", tb - 1))])
        tk_x = p.dma("sp", xin[:, :], x_in[tb * 128:(tb + 1) * 128, :],
                     deps=[prev.get(("s1", tb - 1))])
        tkm = None
        for oc in range(4):
            for kt in range(16):
                tkm = p.mm(ps[:, oc * 512:(oc + 1) * 512],
                           ofsb[:, sl * D + kt * 128: sl * D + (kt + 1) * 128],
                           wo_sb[:, kt * D + oc * 512: kt * D + (oc + 1) * 512],
                           kt == 0, kt == 15,
                           deps=[tk_of, prev.get(("z", tb - 2, oc)),
                                 prev.get(("qxf", tb - 2))] + tk_wo,
                           fence=(kt == 15))
        prev[("attnpe", tb)] = tkm
        tk_s1 = p.stt(yv[:, :], xin[:, :], 1.0, ps[:, :], op.mult, op.add,
                      deps=[tkm, tk_x, prev.get(("s2", tb - 1))],
                      accum=st[:, 0:1])
        prev[("s1", tb)] = tk_s1
        hD = D // 2
        tk_s2a = p.act(scr[:, 0:hD], yv[:, 0:hD], AF.Square, deps=[tk_s1],
                       accum=st[:, 1:2])
        tk_sq1 = p.stt(dsc[:, :], yv[:, hD:hD + 512], 1.0,
                       yv[:, hD:hD + 512], op.mult, op.mult,
                       deps=[tk_s1], accum=st[:, 19:20])
        tk_s2b = p.stt(dsc[:, :], yv[:, hD + 512:D], 1.0, yv[:, hD + 512:D],
                       op.mult, op.mult, deps=[tk_sq1], accum=st[:, 21:22])
        prev[("s2", tb)] = tk_s2b
        p.tt(st[:, 19:20], st[:, 19:20], st[:, 21:22], op.add,
             deps=[tk_s2b], fence=False)
        p.tt(st[:, 1:2], st[:, 1:2], st[:, 19:20], op.add,
             deps=[tk_s2a], fence=False)
        p.ts(st[:, 2:3], st[:, 0:1], 1.0 / D, None, op.mult, fence=False)
        p.tt(st[:, 3:4], st[:, 2:3], st[:, 2:3], op.mult, fence=False)
        tk_var = p.stt(st[:, 4:5], st[:, 1:2], 1.0 / D, st[:, 3:4],
                       op.mult, op.subtract)
        p.act(st[:, 5:6], st[:, 4:5], AF.Ln, bias=EPS, deps=[tk_var],
              fence=False)
        tk_st = p.act(st[:, 6:7], st[:, 5:6], AF.Exp, scale=-0.5)
        tk_y = p.ts(yv[:, :], yv[:, :], st[:, 2:3], st[:, 6:7],
                    op.subtract, op.mult, deps=[tk_st, tk_s2a])
        tk_am = p.red(st[:, 7:8], yv[:, :], op.max, absv=True, deps=[tk_y])
        p.ts(st[:, 8:9], st[:, 7:8], EPS, None, op.max, deps=[tk_am],
             fence=False)
        p.raw("dve", lambda v, a=st: v.reciprocal(a[:, 9:10], a[:, 8:9]),
              fence=False)
        p.ts(st[:, 10:11], st[:, 9:10], 127.0, None, op.mult, fence=False)
        tk_so = p.tt(st[:, 11:12], st[:, 8:9], alpha_t[:, 3:4], op.mult)
        prev[("so", tb)] = tk_so
        tk_q1 = p.act(scr[:, :], yv[:, :], AF.Copy, bias=MAGIC,
                      scale=st[:, 10:11], deps=[tk_so])
        prev[("q2", tb)] = p.act(qv[:, :], scr[:, :], AF.Copy, bias=-MAGIC,
                                 deps=[tk_q1])

    def emit_post(tb):
        sl = tb % 2
        ps = ps2[sl]
        st = stats[:, sl * 24:(sl + 1) * 24]
        yv = yb[:, sl * D:(sl + 1) * D]
        qv = qx[:, sl * D:(sl + 1) * D]
        tk_tr = []
        for g4 in range(4):
            poff = g4 * 512
            tkm = None
            for j in range(4):
                dblk = g4 * 4 + j
                tkm = p.mm(ps[:, poff + j * 128: poff + (j + 1) * 128],
                           qv[:, dblk * 128:(dblk + 1) * 128], ident_b[:, :],
                           True, True,
                           deps=[prev[("q2", tb)], prev[("s1", tb)],
                                 tk_tr[g4 - 2] if g4 >= 2 else None],
                           fence=(j == 3))
            tk_tr.append(p.cp("dve", qxf[:, g4 * 512:(g4 + 1) * 512],
                              ps[:, poff:poff + 512], deps=[tkm]))
        tk_qxf = p.nopf("dve", deps=tk_tr)
        prev[("qxf", tb)] = tk_qxf
        tk_f = []
        for oc in range(4):
            tkm = None
            for kt in range(16):
                tkm = p.mm(ps[:, oc * 512:(oc + 1) * 512],
                           qxf[:, kt * 128:(kt + 1) * 128],
                           tern_sb[:, kt * D + oc * 512: kt * D + (oc + 1) * 512],
                           kt == 0, kt == 15,
                           deps=[tk_qxf, prev.get(("s1", tb + 1)),
                                 prev.get(("zr", tb - 1))] + tk_tern,
                           fence=(kt == 15))
            tk_f.append(tkm)
        tk_z = []
        for oc in range(4):
            tk_z.append(p.stt(zb[:, oc * 512:(oc + 1) * 512],
                              ps[:, oc * 512:(oc + 1) * 512], st[:, 11:12],
                              btile[:, oc * 512:(oc + 1) * 512],
                              op.mult, op.add,
                              deps=[tk_f[oc], prev[("so", tb)],
                                    prev.get(("ozb", tb - 1))]))
            prev[("z", tb, oc)] = tk_z[-1]
        tk_zr = p.tt(zb[:, :], zb[:, :], yv[:, :], op.add, deps=tk_z)
        prev[("zr", tb)] = tk_zr
        hD = D // 2
        tk_c1 = p.act(scr[:, 0:hD], zb[:, 0:hD], AF.Copy, deps=[tk_zr],
                      accum=st[:, 12:13])
        tk_c1b = p.ts(dsc[:, :], zb[:, hD:hD + 512], 1.0, 0.0,
                      op.mult, op.add, deps=[tk_zr], accum=st[:, 22:23])
        tk_c1c = p.ts(dsc[:, :], zb[:, hD + 512:D], 1.0, 0.0,
                      op.mult, op.add, deps=[tk_c1b], accum=st[:, 23:24])
        tk_c2 = p.act(scr[:, 0:hD], zb[:, 0:hD], AF.Square, deps=[tk_c1],
                      accum=st[:, 13:14])
        tk_c2b = p.stt(dsc[:, :], zb[:, hD:hD + 512], 1.0,
                       zb[:, hD:hD + 512], op.mult, op.mult,
                       deps=[tk_c1c], accum=st[:, 19:20])
        tk_c2c = p.stt(dsc[:, :], zb[:, hD + 512:D], 1.0, zb[:, hD + 512:D],
                       op.mult, op.mult, deps=[tk_c2b], accum=st[:, 21:22])
        prev[("c2", tb)] = p.nopf("dve", deps=[tk_c2, tk_c2c])
        p.tt(st[:, 22:23], st[:, 22:23], st[:, 23:24], op.add, fence=False)
        p.tt(st[:, 12:13], st[:, 12:13], st[:, 22:23], op.add,
             deps=[tk_c1], fence=False)
        p.tt(st[:, 19:20], st[:, 19:20], st[:, 21:22], op.add, fence=False)
        p.tt(st[:, 13:14], st[:, 13:14], st[:, 19:20], op.add,
             deps=[tk_c2], fence=False)
        p.ts(st[:, 14:15], st[:, 12:13], 1.0 / D, None, op.mult, fence=False)
        p.tt(st[:, 15:16], st[:, 14:15], st[:, 14:15], op.mult, fence=False)
        tk_v2 = p.stt(st[:, 16:17], st[:, 13:14], 1.0 / D, st[:, 15:16],
                      op.mult, op.subtract)
        p.act(st[:, 17:18], st[:, 16:17], AF.Ln, bias=EPS, deps=[tk_v2],
              fence=False)
        tk_st2 = p.act(st[:, 18:19], st[:, 17:18], AF.Exp, scale=-0.5)
        tk_o = p.ts(outb[:, :], zb[:, :], st[:, 14:15], st[:, 18:19],
                    op.subtract, op.mult,
                    deps=[tk_st2, tk_c2,
                          tk_out[-1] if tk_out else None])
        prev[("ozb", tb)] = tk_o
        tk_out.append(p.dma("sp", y_out[tb * 128:(tb + 1) * 128, :],
                            outb[:, :], deps=[tk_o]))

    for tb in range(NTB):
        emit_attn(tb)
        if tb >= 1:
            emit_post(tb - 1)
    emit_post(NTB - 1)

    p.wait_self_dma("sp", tk_out[-1])


def make_aux():
    ident = np.eye(128, dtype=np.float32)
    mA = np.where(np.tril(np.ones((128, 128)), -1) > 0, 0.0, NEG).astype(np.float32)
    mNb = np.where(np.triu(np.ones((128, 128)), 1) > 0, 0.0, NEG).astype(np.float32)
    mW = np.where(np.triu(np.ones((128, 128)), 0) > 0, 0.0, NEG).astype(np.float32)
    masks = np.concatenate([np.tile(m, (1, 8)) for m in (mA, mNb, mW)], axis=1)
    ind = np.zeros((48, D), np.float32)
    for j in range(3):
        for h in range(H):
            ind[j * 16 + h, h * 128:(h + 1) * 128] = 1.0
    oh = np.zeros((128, H * H), np.float32)
    for h in range(H):
        oh[:, h * 16 + h] = 1.0
    return ident, masks, ind, oh


def make_inputs(xb, W_qkv, W_ab, conv_w, A_log, dt_bias, o_norm_w, W_o, fw, fb):
    ident, masks, ind, oh = make_aux()
    wt = np.ascontiguousarray(
        np.broadcast_to(np.tile(np.asarray(o_norm_w, np.float32), H), (128, D)))
    bt = np.ascontiguousarray(
        np.broadcast_to(np.asarray(fb, np.float32), (128, D)))
    return dict(
        x=np.ascontiguousarray(xb, np.float32),
        w_qkv=np.ascontiguousarray(W_qkv, np.float32),
        w_ab=np.ascontiguousarray(W_ab, np.float32),
        conv_w=np.ascontiguousarray(conv_w, np.float32),
        a_log=np.asarray(A_log, np.float32).reshape(H, 1),
        dt_bias=np.asarray(dt_bias, np.float32).reshape(H, 1),
        w_o=np.ascontiguousarray(W_o, np.float32),
        ffn_w=np.ascontiguousarray(fw, np.float32),
        c_ident=ident, c_masks=masks, c_ind=ind, c_oh=oh,
        c_wtile=wt, c_btile=bt,
    )


def kernel(L0_attn_hidden_states, L0_attn_W_qkv, L0_attn_W_ab, L0_attn_conv_w,
           L0_attn_A_log, L0_attn_dt_bias, L0_attn_o_norm_w, L0_attn_W_o,
           L0_ffn_weight, L0_ffn_bias):
    x = np.ascontiguousarray(L0_attn_hidden_states, dtype=np.float32)
    B, T, _ = x.shape
    nc = build_program(T=T)
    in_maps = [make_inputs(x[b], L0_attn_W_qkv, L0_attn_W_ab, L0_attn_conv_w,
                           L0_attn_A_log, L0_attn_dt_bias, L0_attn_o_norm_w,
                           L0_attn_W_o, L0_ffn_weight, L0_ffn_bias)
               for b in range(B)]
    res = run_bass_kernel_spmd(nc, in_maps, core_ids=list(range(B)))
    return np.stack([res.results[b]["y"] for b in range(B)], axis=0).astype(np.float32)


# version 15
# speedup vs baseline: 1.2130x; 1.0221x over previous
import numpy as np
from contextlib import ExitStack

import concourse.bass as bass
import concourse.mybir as mybir
from concourse.bass_utils import run_bass_kernel_spmd
from concourse.alu_op_type import AluOpType as op

F32 = mybir.dt.float32
BF16 = mybir.dt.bfloat16
AF = mybir.ActivationFunctionType
AX = mybir.AxisListType.X

D = 2048
H = 16
C = 128
QKV_N = 3 * D
MAGIC = float(3 * 2 ** 22)
NEG = -60000.0
EPS = 1e-5
L2EPS = 1e-6
LN_HALF_DK = float(-0.5 * np.log(128.0))
LN2 = float(np.log(2.0))

ENGINES = ("pe", "act", "dve", "sp", "pool")
BLOCK_NAME = dict(pe="tensor", act="scalar", dve="vector", sp="sync", pool="gpsimd")


class P:

    NLANES = 4
    CHANNELS = ("pe", "act", "dve", "sp", "pool") + tuple(
        f"{e}_d{l}" for e in ("sp", "pool") for l in range(4))

    def __init__(self, nc, es):
        self.nc = nc
        self.sem = {ch: es.enter_context(nc.semaphore(f"sem_{ch}"))
                    for ch in self.CHANNELS}
        self.q = {e: [] for e in ENGINES}
        self.cnt = {ch: 0 for ch in self.CHANNELS}
        self.lane = {"sp": 0, "pool": 0}
        self.acc = {e: ([], []) for e in ENGINES}

    @staticmethod
    def _eng_of(ch):
        return ch.split("_")[0]

    @staticmethod
    def _ext(ap):
        if ap is None or isinstance(ap, (int, float)):
            return None
        try:
            lo = ap.offset
            hi = lo + 1
            for step, count in list(ap.ap)[1:]:
                hi += step * (count - 1)
            return (ap.tensor.name, lo, hi)
        except Exception:
            return (getattr(getattr(ap, "tensor", None), "name", "?"), 0, 1 << 60)

    @staticmethod
    def _hits(x, y):
        if x[0] == "__any__" or y[0] == "__any__":
            return True
        return x[0] == y[0] and x[1] < y[2] and y[1] < x[2]

    def _conflict(self, e, writes, reads):
        tw, tr = self.acc[e]
        for w in writes:
            if w is not None and any(self._hits(x, w) for x in tw + tr):
                return True
        for r in reads:
            if r is not None and any(self._hits(x, r) for x in tw):
                return True
        return False

    def _emit(self, e, fn, deps, inc, ch=None, drain=None, w=(), r=()):
        ch = ch or e
        if drain is None and e in ("act", "dve"):
            if w or r:
                writes = [self._ext(a) for a in w]
                reads = [self._ext(a) for a in r]
            else:
                writes = [("__any__", 0, 1 << 60)]
                reads = []
            drain = self._conflict(e, writes, reads) if (w or r) else True
            if drain:
                self.acc[e] = ([], [])
            self.acc[e][0].extend(x for x in writes if x)
            self.acc[e][1].extend(x for x in reads if x)
        if drain:
            self.q[e].append((lambda g: g.drain(), (), 0, ch))
        deps = tuple(d for d in deps
                     if d is not None and (self._eng_of(d[0]) != e or "_d" in d[0]))
        self.q[e].append((fn, deps, inc, ch))
        if inc:
            self.cnt[ch] += inc
            return (ch, self.cnt[ch])
        return None

    def raw(self, e, fn, deps=(), fence=True):
        return self._emit(e, fn, deps, 1 if fence else 0)

    def nopf(self, e, deps=()):
        if e in ("pe", "act", "dve"):
            return self._emit(e, lambda g: g.drain(), deps, 1, drain=False)
        return self._emit(e, lambda g: g.nop(), deps, 1, drain=False)

    def wait_self_dma(self, e, tok):
        sem = self.sem[tok[0]]
        v = tok[1]
        self.q[e].append((lambda g, sem=sem, v=v: g.wait_ge(sem, v) or None, (), 0, e))

    def dma_quiesce(self, e):
        deps = [(f"{e}_d{l}", self.cnt[f"{e}_d{l}"]) for l in range(self.NLANES)]
        return self.nopf(e, deps=deps)

    def barrier(self):
        t1 = [self.dma_quiesce(e) if e in ("sp", "pool") else self.nopf(e)
              for e in ENGINES]
        return [self.nopf(e, deps=t1) for e in ENGINES]

    def dma(self, e, outp, inp, deps=()):
        l = self.lane[e]
        self.lane[e] = (l + 1) % self.NLANES
        ch = f"{e}_d{l}"
        gate = (ch, self.cnt[ch])
        return self._emit(e, lambda g, o=outp, i=inp: g.dma_start(out=o, in_=i),
                          list(deps) + [gate], 16, ch=ch)

    def mm(self, dst, lhs, rhs, start, stop, deps=(), fence=False):
        return self._emit("pe", lambda t, d=dst, l=lhs, r=rhs, a=start, b=stop:
                          t.matmul(d, l, r, start=a, stop=b), deps,
                          1 if fence else 0)

    def tr(self, dst, src, ident, deps=(), fence=True):
        return self._emit("pe", lambda t, d=dst, s=src, i=ident:
                          t.transpose(d, s, i), deps, 1 if fence else 0)

    def cp(self, e, dst, src, deps=(), fence=True):
        return self._emit(e, (lambda v, d=dst, s=src: v.tensor_copy(d, s)) if e == "dve"
                          else (lambda s_, d=dst, s=src: s_.activation(
                              d, s, AF.Copy, bias=0.0, scale=1.0)),
                          deps, 1 if fence else 0, w=(dst,), r=(src,))

    def ts(self, dst, src, s1, s2, o1, o2=None, deps=(), fence=True, accum=None):
        def fn(v, d=dst, s=src, s1=s1, s2=s2, o1=o1, o2=o2, ac=accum):
            if o2 is None:
                return v.tensor_scalar(d, s, s1, None, o1, accum_out=ac) \
                    if ac is not None else v.tensor_scalar(d, s, s1, None, o1)
            return v.tensor_scalar(d, s, s1, s2, o1, o2, accum_out=ac) \
                if ac is not None else v.tensor_scalar(d, s, s1, s2, o1, o2)
        aps = lambda z: z if hasattr(z, "offset") else None
        return self._emit("dve", fn, deps, 1 if fence else 0,
                          w=(dst, accum), r=(src, aps(s1), aps(s2)))

    def tt(self, dst, a, b, o, deps=(), fence=True):
        return self._emit("dve", lambda v, d=dst, a=a, b=b, o=o:
                          v.tensor_tensor(d, a, b, o), deps, 1 if fence else 0,
                          w=(dst,), r=(a, b))

    def stt(self, dst, in0, sc, in1, o1, o2, deps=(), fence=True, accum=None):
        def fn(v, d=dst, i0=in0, sc=sc, i1=in1, o1=o1, o2=o2, ac=accum):
            if ac is not None:
                return v.scalar_tensor_tensor(d, i0, sc, i1, o1, o2, accum_out=ac)
            return v.scalar_tensor_tensor(d, i0, sc, i1, o1, o2)
        aps = lambda z: z if hasattr(z, "offset") else None
        return self._emit("dve", fn, deps, 1 if fence else 0,
                          w=(dst, accum), r=(in0, in1, aps(sc)))

    def act(self, dst, src, func, bias=0.0, scale=1.0, deps=(), fence=True,
            accum=None):
        def fn(s_, d=dst, s=src, f=func, b=bias, sc=scale, ac=accum):
            if ac is not None:
                return s_.activation(d, s, f, bias=b, scale=sc, accum_out=ac)
            return s_.activation(d, s, f, bias=b, scale=sc)
        aps = lambda z: z if hasattr(z, "offset") else None
        return self._emit("act", fn, deps, 1 if fence else 0,
                          w=(dst, accum), r=(src, aps(bias), aps(scale)))

    def red(self, dst, src, o, absv=False, deps=(), fence=True):
        return self._emit("dve", lambda v, d=dst, s=src, o=o, a=absv:
                          v.tensor_reduce(d, s, AX, o, apply_absolute_value=a),
                          deps, 1 if fence else 0, w=(dst,), r=(src,))

    def mset(self, e, ap, val, deps=(), fence=True):
        return self._emit(e, lambda g, a=ap, v=val: g.memset(a, v), deps,
                          1 if fence else 0, w=(ap,))

    def replay(self):
        with self.nc.Block() as block:
            for e in ENGINES:
                deco = getattr(block, BLOCK_NAME[e])
                items = self.q[e]
                sems = self.sem

                def body(eng, items=items):
                    last = {}
                    for fn, deps, inc, ch in items:
                        for (pch, v) in deps:
                            if last.get(pch, 0) < v:
                                eng.wait_ge(sems[pch], v)
                                last[pch] = v
                        inst = fn(eng)
                        if inc:
                            inst.then_inc(sems[ch], inc)
                deco(body)


def build_program(T=4096, dbg=False, phases="all"):
    locstop = 99
    if "/" in phases:
        phases, _loc = phases.split("/")
        locstop = int(_loc)
    nc = bass.Bass("TRN2", target_bir_lowering=False)
    NCH = T // C
    T1 = min(512, T)
    NT1 = T // T1
    CPT = T1 // 128
    T3 = min(512, T)
    NT3 = T // T3
    kdbg = "ExternalOutput" if dbg else "Internal"

    x_in = nc.dram_tensor("x", [T, D], F32, kind="ExternalInput").ap()
    w_qkv = nc.dram_tensor("w_qkv", [D, QKV_N], F32, kind="ExternalInput").ap()
    w_ab = nc.dram_tensor("w_ab", [D, 2 * H + D], F32, kind="ExternalInput").ap()
    conv_w = nc.dram_tensor("conv_w", [QKV_N, 4], F32, kind="ExternalInput").ap()
    a_log = nc.dram_tensor("a_log", [H, 1], F32, kind="ExternalInput").ap()
    dt_bias = nc.dram_tensor("dt_bias", [H, 1], F32, kind="ExternalInput").ap()
    w_o = nc.dram_tensor("w_o", [D, D], F32, kind="ExternalInput").ap()
    ffn_w = nc.dram_tensor("ffn_w", [D, D], F32, kind="ExternalInput").ap()
    c_ident = nc.dram_tensor("c_ident", [128, 128], F32, kind="ExternalInput").ap()
    c_masks = nc.dram_tensor("c_masks", [128, 3 * 1024], F32, kind="ExternalInput").ap()
    c_ind = nc.dram_tensor("c_ind", [48, D], F32, kind="ExternalInput").ap()
    c_oh = nc.dram_tensor("c_oh", [128, H * H], F32, kind="ExternalInput").ap()
    c_wtile = nc.dram_tensor("c_wtile", [128, D], F32, kind="ExternalInput").ap()
    c_btile = nc.dram_tensor("c_btile", [128, D], F32, kind="ExternalInput").ap()
    y_out = nc.dram_tensor("y", [T, D], F32, kind="ExternalOutput").ap()

    qkc_s = nc.dram_tensor("qkc_s", [NCH, 48, 128, 128], BF16, kind=kdbg).ap()
    sg_s = nc.dram_tensor("sg_s", [T, D], BF16, kind=kdbg).ap()
    of_s = nc.dram_tensor("of_s", [NCH, H, 128, 128], BF16, kind=kdbg).ap()
    tern_s = nc.dram_tensor("tern_s", [D, D], BF16, kind=kdbg).ap()
    gH_d = nc.dram_tensor("gH_d", [H, T], F32, kind=kdbg).ap()
    lnbH_d = nc.dram_tensor("lnbH_d", [H, T], F32, kind=kdbg).ap()
    if dbg:
        alpha_d = nc.dram_tensor("alpha_d", [1, 4], F32, kind=kdbg).ap()

    es = ExitStack()
    SG = es.enter_context
    p = P(nc, es)

    ident_b = SG(nc.sbuf_tensor("ident_b", [128, 128], BF16)).ap()
    ident_f = SG(nc.sbuf_tensor("ident_f", [128, 128], F32)).ap()
    btile = SG(nc.sbuf_tensor("btile", [128, D], F32)).ap()
    nega = SG(nc.sbuf_tensor("nega", [H, 2], F32)).ap()
    dtb = SG(nc.sbuf_tensor("dtb", [H, 1], F32)).ap()
    ones1f = SG(nc.sbuf_tensor("ones1f", [1, 128], F32)).ap()
    alpha_t = SG(nc.sbuf_tensor("alpha_t", [128, 4], F32)).ap()
    csts = SG(nc.sbuf_tensor("csts", [128, 4], F32)).ap()
    es12 = ExitStack()
    SG12 = es12.enter_context
    masks_b = SG12(nc.sbuf_tensor("masks_b", [128, 3 * 1024], BF16)).ap()
    oh_b = SG12(nc.sbuf_tensor("oh_b", [128, H * H], BF16)).ap()
    wtile = SG12(nc.sbuf_tensor("wtile", [128, D], F32)).ap()
    ones16 = SG12(nc.sbuf_tensor("ones16", [H, 128], F32)).ap()

    tc = []
    tc.append(p.dma("pool", ident_b[:, :], c_ident[:, :]))
    tc.append(p.dma("sp", ident_f[:, :], c_ident[:, :]))
    tc.append(p.dma("pool", masks_b[:, :], c_masks[:, :]))
    tc.append(p.dma("pool", oh_b[:, :], c_oh[:, :]))
    tc.append(p.dma("sp", wtile[:, :], c_wtile[:, :]))
    tc.append(p.dma("sp", btile[:, :], c_btile[:, :]))
    tc.append(p.dma("sp", dtb[:, :], dt_bias[:, :]))
    tk_a = p.dma("sp", nega[:, 1:2], a_log[:, :])
    tk_na = p.act(nega[:, 0:1], nega[:, 1:2], AF.Exp, deps=[tk_a])
    tc.append(p.ts(nega[:, 0:1], nega[:, 0:1], -1.0, None, op.mult, deps=[tk_na]))
    p.mset("dve", ones16[:, :], 1.0)
    tc.append(p.mset("dve", ones1f[:, :], 1.0))
    p.mset("dve", csts[:, 0:1], L2EPS, fence=False)
    tc.append(p.mset("dve", csts[:, 1:2], EPS))
    nc.const_aps.register(F32, L2EPS, csts[:, 0:1])
    nc.const_aps.register(F32, EPS, csts[:, 1:2])
    TC = tuple(tc)

    esA = ExitStack()
    SA = esA.enter_context
    xf = SA(nc.sbuf_tensor("xf", [128, 16 * T1], BF16)).ap()
    xb16 = SA(nc.sbuf_tensor("xb16", [128, D], BF16)).ap()
    wqb = SA(nc.sbuf_tensor("wqb", [128, 2 * 2048], BF16)).ap()
    wabb = SA(nc.sbuf_tensor("wabb", [128, 16 * 256], BF16)).ap()
    wabsm = SA(nc.sbuf_tensor("wabsm", [128, 16 * 32], BF16)).ap()
    qpre = SA(nc.sbuf_tensor("qpre", [128, 2 * (T1 + 4)], BF16)).ap()
    tails = SA(nc.sbuf_tensor("tails", [128, 3 * 48], BF16)).ap()
    sq_out = SA(nc.sbuf_tensor("sq_out", [128, 2 * T1], BF16)).ap()
    sgate = SA(nc.sbuf_tensor("sgate", [128, 2 * 256], BF16)).ap()
    sab = SA(nc.sbuf_tensor("sab", [128, 32 * (T1 // 128)], F32)).ap()
    wcv = SA(nc.sbuf_tensor("wcv", [128, 192], F32)).ap()
    diag4 = SA(nc.sbuf_tensor("diag4", [128, 2 * 512], BF16)).ap()
    st1 = SA(nc.sbuf_tensor("st1", [H, 7 * 128], F32)).ap()
    ps_q = SA(nc.psum_tensor("ps_q", [128, 512], F32)).ap()
    ps_s1 = SA(nc.psum_tensor("ps_s1", [128, 512], F32)).ap()
    qc = SA(nc.sbuf_tensor("qc", [128, 2 * 6144], BF16)).ap()
    sgc = SA(nc.sbuf_tensor("sgc", [128, 2 * D], BF16)).ap()
    S_sb = SA(nc.sbuf_tensor("S_sb", [128, H * 128], F32)).ap()
    S_bf = SA(nc.sbuf_tensor("S_bf", [128, H * 128], BF16)).ap()
    sqgb = SA(nc.sbuf_tensor("sqgb", [128, 2048], BF16)).ap()
    indb = SA(nc.sbuf_tensor("indb", [57, D], BF16)).ap()
    lA = SA(nc.sbuf_tensor("lA", [57, 128], BF16)).ap()
    lST = SA(nc.sbuf_tensor("lST", [57, 128], BF16)).ap()
    lST2 = SA(nc.sbuf_tensor("lST2", [57, 128], BF16)).ap()
    mst = SA(nc.sbuf_tensor("mst", [H, 8 * 128], F32)).ap()
    mpc = SA(nc.sbuf_tensor("mpc", [H, 9 * 128], BF16)).ap()
    lnssq = SA(nc.sbuf_tensor("lnssq", [H, 256], F32)).ap()
    colsb = SA(nc.sbuf_tensor("colsb", [128, 2 * 64], F32)).ap()
    gcbb = SA(nc.sbuf_tensor("gcbb", [128, 2 * 16], F32)).ap()
    expd = SA(nc.sbuf_tensor("expd", [128, 512], F32)).ap()
    n0b = SA(nc.sbuf_tensor("n0b", [128, 2048], BF16)).ap()
    kdecb = SA(nc.sbuf_tensor("kdecb", [128, 2 * 2048], BF16)).ap()
    r0b = SA(nc.sbuf_tensor("r0b", [128, 2048], BF16)).ap()
    ntok = SA(nc.sbuf_tensor("ntok", [128, 2 * 512], BF16)).ap()
    nbar = SA(nc.sbuf_tensor("nbar", [128, 2 * 512], BF16)).ap()
    wchn = SA(nc.sbuf_tensor("wchn", [128, 2 * 1024], BF16)).ap()
    trvb = SA(nc.sbuf_tensor("trvb", [128, 2 * 2048], BF16)).ap()
    gtrb = SA(nc.sbuf_tensor("gtrb", [128, 1024], BF16)).ap()
    gbarb = SA(nc.sbuf_tensor("gbarb", [128, 2 * 2048], BF16)).ap()
    wbarb = SA(nc.sbuf_tensor("wbarb", [128, 2 * 2048], BF16)).ap()
    ub = SA(nc.sbuf_tensor("ub", [128, 1024], BF16)).ap()
    qst = SA(nc.sbuf_tensor("qst", [128, 2048], F32)).ap()
    ost = SA(nc.sbuf_tensor("ost", [128, 1024], F32)).ap()
    onb = SA(nc.sbuf_tensor("onb", [128, 1024], BF16)).ap()
    ofst = SA(nc.sbuf_tensor("ofst", [128, 1024], BF16)).ap()
    rms = SA(nc.sbuf_tensor("rms", [128, 48], F32)).ap()
    grow = SA(nc.sbuf_tensor("grow", [1, 16], F32)).ap()
    ghb = SA(nc.sbuf_tensor("ghb", [H, 2 * 256], F32)).ap()
    pCH = SA(nc.psum_tensor("pCH", [128, 1024], F32)).ap()
    pSQ = SA(nc.psum_tensor("pSQ", [128, 1024], F32)).ap()
    pSU = SA(nc.psum_tensor("pSU", [128, 1024], F32)).ap()

    tk_wcv = p.dma("pool", wcv[:, :].rearrange("p (fb j) -> p fb j", j=4),
                   conv_w[:, :].rearrange("(fb p) j -> p fb j", p=128))
    tk_t0 = p.mset("dve", tails[:, :], 0.0)
    s1t = {"tail": {fb: tk_t0 for fb in range(48)}, "fb": {}, "sq": {},
           "gd": {}, "bg": None, "xfree": None, "ps1": None, "tile": {}}
    NTB1 = T1 // 128
    SEGW = min(512, T1)
    NSEG = T1 // SEGW

    def softplus(out_ap, in_ap, tmp_ap, deps, scale=1.0):
        p.act(tmp_ap, in_ap, AF.Abs, scale=scale, deps=deps, fence=False)
        p.act(tmp_ap, tmp_ap, AF.Exp, scale=-1.0, fence=False)
        p.act(tmp_ap, tmp_ap, AF.Ln, bias=1.0, fence=False)
        t1 = p.act(out_ap, in_ap, AF.Relu, scale=scale, deps=deps)
        return p.tt(out_ap, out_ap, tmp_ap, op.add, deps=[t1])

    def wq_dma(ti, fb, deps):
        sl = fb % 2
        return p.dma("pool",
                     wqb[:, sl * 2048:(sl + 1) * 2048].rearrange(
                         "p (kt n) -> p kt n", n=128),
                     w_qkv[:, fb * 128:(fb + 1) * 128].rearrange(
                         "(kt p) n -> p kt n", p=128), deps=deps)

    def s1_gen(ti):
        t0 = ti * T1
        FB = s1t["fb"]
        SQ = s1t["sq"]
        tk_xtr = []
        tkms = s1t.get("xlast", [None] * 4)
        for tb in range(NTB1):
            tk_xs = [p.dma("pool", xb16[:, hf * 512:(hf + 1) * 512],
                           x_in[t0 + tb * 128: t0 + (tb + 1) * 128,
                                hf * 512:(hf + 1) * 512],
                           deps=[s1t["xfree"], tkms[hf]])
                     for hf in range(4)]
            tkms = []
            for g4 in range(4):
                prev = tk_xtr[-1] if tk_xtr else s1t["ps1"]
                tkm = None
                for j in range(4):
                    dblk = g4 * 4 + j
                    tkm = p.mm(ps_s1[:, j * 128:(j + 1) * 128],
                               xb16[:, dblk * 128:(dblk + 1) * 128],
                               ident_b[:, :], True, True,
                               deps=[tk_xs[g4], prev] + list(TC),
                               fence=(j == 3))
                tkms.append(tkm)
                tk_xtr.append(p.cp(
                    "dve",
                    xf[:, :].rearrange("p (b tt) -> p b tt", b=16)
                    [:, g4 * 4:(g4 + 1) * 4, tb * 128:(tb + 1) * 128],
                    ps_s1[:, :].rearrange("p (b tt) -> p b tt", b=4)[:, :, :],
                    deps=[tkm]))
            yield
        s1t["xlast"] = tkms
        tk_xf = p.nopf("dve", deps=tk_xtr)

        if ti == 0:
            FB[(0, -2)] = FB[(0, -1)] = FB[(-1, 46)] = FB[(-1, 47)] = None
            s1t["wq"] = {0: wq_dma(0, 0, [tk_wcv]), 1: wq_dma(0, 1, [tk_wcv])}
        WQ = s1t["wq"]
        pst = {}
        for fbx in range(49):
            if fbx < 48:
                fb = fbx
                sl = fb % 2
                bank = ps_q if sl == 0 else ps_s1
                qp = qpre[:, sl * (T1 + 4):(sl + 1) * (T1 + 4)]
                dg = diag4[:, sl * 512:(sl + 1) * 512]
                fprev = FB.get((ti, fb - 2), FB.get((ti - 1, fb + 46)))
                tkm = None
                for kt in range(16):
                    tkm = p.mm(bank[:, 0:SEGW],
                               wqb[:, sl * 2048 + kt * 128:
                                   sl * 2048 + (kt + 1) * 128],
                               xf[:, kt * T1: kt * T1 + SEGW],
                               kt == 0, kt == 15,
                               deps=[WQ[fb], tk_xf, fprev],
                               fence=(kt == 15))
                tk_ev = p.act(qp[:, 3: 3 + SEGW], bank[:, 0:SEGW], AF.Copy,
                              deps=[tkm])
                nfb, nti = fb + 2, ti
                if nfb >= 48:
                    nfb -= 48
                    nti += 1
                if nti < NT1:
                    WQ[nfb] = wq_dma(nti, nfb, [tkm, tk_wcv])
                tk_tl = p.cp("dve", qp[:, 0:3], tails[:, fb * 3:(fb + 1) * 3],
                             deps=[s1t["tail"][fb], fprev])
                s1t["tail"][fb] = p.cp("dve", tails[:, fb * 3:(fb + 1) * 3],
                                       qp[:, T1: T1 + 3], deps=[tk_ev, tk_tl])
                tkd = None
                for j in range(4):
                    tkd = p.ts(dg[:, j * 128:(j + 1) * 128], ident_b[:, :],
                               wcv[:, fb * 4 + j: fb * 4 + j + 1], None,
                               op.mult, deps=[tk_wcv, fprev], fence=(j == 3))
                pst[fb] = (bank, qp, dg, tk_ev, tk_tl, tkd)
            if fbx >= 1:
                fb = fbx - 1
                sl = fb % 2
                bank, qp, dg, tk_ev, tk_tl, tkd = pst.pop(fb)
                tkm = None
                for j in range(4):
                    tkm = p.mm(bank[:, 0:SEGW], dg[:, j * 128:(j + 1) * 128],
                               qp[:, j: j + SEGW], j == 0, j == 3,
                               deps=[tkd, tk_ev, tk_tl], fence=(j == 3))
                tk_cv = p.act(sq_out[:, sl * T1: sl * T1 + SEGW],
                              bank[:, 0:SEGW], AF.Silu,
                              deps=[tkm, SQ.get((ti, fb - 2),
                                                SQ.get((ti - 1, fb + 46)))])
                FB[(ti, fb)] = tk_cv
                if fb < 16:
                    blk = fb * 3 + 1
                elif fb < 32:
                    blk = (fb - 16) * 3
                else:
                    blk = (fb - 32) * 3 + 2
                c0 = t0 // 128
                SQ[(ti, fb)] = p.dma("pool",
                                     qkc_s[c0:c0 + NTB1, blk, :, :].rearrange(
                                         "cc pp tt -> pp cc tt"),
                                     sq_out[:, sl * T1:(sl + 1) * T1].rearrange(
                                         "pp (cc tt) -> pp cc tt", tt=128),
                                     deps=[tk_cv])
            yield

        if ti == 0:
            s1t["wsm"] = [p.dma("pool", wabsm[:, kt * 32:(kt + 1) * 32],
                                w_ab[kt * 128:(kt + 1) * 128, 0:32])
                          for kt in range(16)]
        tk_absm = []
        for tb in range(NTB1):
            tkm = None
            for kt in range(16):
                tkm = p.mm(ps_s1[:, 0:32],
                           xf[:, kt * T1 + tb * 128: kt * T1 + (tb + 1) * 128],
                           wabsm[:, kt * 32:(kt + 1) * 32],
                           kt == 0, kt == 15,
                           deps=s1t["wsm"] + [tk_xf, s1t["bg"],
                                              FB.get((ti, 47)),
                                              tk_absm[tb - 1] if tb >= 1 else
                                              tk_xtr[-1]],
                           fence=(kt == 15))
            tk_absm.append(p.cp("dve", sab[:, tb * 32:(tb + 1) * 32],
                                ps_s1[:, 0:32], deps=[tkm]))
            yield

        GD = s1t["gd"]

        def gsk(idx):
            return GD.get(("s", ti, idx // NTB1, idx % NTB1))

        for nch in range(8):
            ncol0 = 32 + nch * 256
            tk_wab = p.dma("pool", wabb[:, :].rearrange("p (kt n) -> p kt n", n=256),
                           w_ab[:, ncol0:ncol0 + 256].rearrange(
                               "(kt p) n -> p kt n", p=128),
                           deps=[GD.get(("mm", ti, nch - 1),
                                        GD.get(("mm", ti - 1, 7)))])
            for tb in range(NTB1):
                sl = tb % 2
                idx = nch * NTB1 + tb
                gbank = ps_q if idx % 2 == 0 else ps_s1
                gprev = (gsk(idx - 2) if idx >= 2 else
                         FB.get((ti, 46 + idx % 2)))
                tkm = None
                for kt in range(16):
                    tkm = p.mm(gbank[:, 0:256],
                               xf[:, kt * T1 + tb * 128: kt * T1 + (tb + 1) * 128],
                               wabb[:, kt * 256:(kt + 1) * 256],
                               kt == 0, kt == 15,
                               deps=[tk_wab, tk_xf, tk_absm[-1], gprev],
                               fence=(kt == 15))
                GD[("mm", ti, nch)] = tkm
                tk_si = p.act(sgate[:, sl * 256:(sl + 1) * 256],
                              gbank[:, 0:256], AF.Silu,
                              deps=[tkm, GD.get(("d", ti, nch, tb - 2),
                                                GD.get(("d", ti, nch - 1, tb + NTB1 - 2),
                                                       GD.get(("d", ti - 1, 7, tb + NTB1 - 2))))])
                tk_sd = p.dma("sp", sg_s[t0 + tb * 128: t0 + (tb + 1) * 128,
                                         ncol0 - 32: ncol0 - 32 + 256],
                              sgate[:, sl * 256:(sl + 1) * 256], deps=[tk_si])
                GD[("s", ti, nch, tb)] = tk_si
                GD[("d", ti, nch, tb)] = tk_sd
                yield

        aT = st1[:, 0:128]
        bbT = st1[:, 128:256]
        u_ = st1[:, 256:384]
        tmp_ = st1[:, 384:512]
        sp_ = st1[:, 512:640]
        gsl = st1[:, 640:768]
        lsl = st1[:, 768:896]
        tk_bg = s1t["bg"]
        tk_gd = []
        tk_gsil = GD[("s", ti, 7, NTB1 - 1)]
        for tb in range(NTB1):
            tka1 = p.tr(ps_s1[0:16, 0:128], sab[:, tb * 32: tb * 32 + 16],
                        ident_f[:, :], deps=[tk_absm[tb], tk_bg, tk_gsil])
            tc1 = p.cp("dve", aT, ps_s1[0:16, 0:128], deps=[tka1])
            tka2 = p.tr(ps_s1[0:16, 128:256], sab[:, tb * 32 + 16: tb * 32 + 32],
                        ident_f[:, :], deps=[tka1, tc1])
            tc2 = p.cp("dve", bbT, ps_s1[0:16, 128:256], deps=[tka2])
            tk_u = p.ts(u_, aT, dtb[:, 0:1], None, op.add, deps=[tc1])
            tsp = softplus(sp_, u_, tmp_, deps=[tk_u])
            tk_g = p.ts(gsl, sp_, nega[:, 0:1], None, op.mult,
                        deps=[tsp, tk_gd[-2] if len(tk_gd) >= 2 else
                              s1t.get("gd2")])
            tsp2 = softplus(sp_, bbT, tmp_, deps=[tc2, tk_g], scale=-1.0)
            tk_bg = p.ts(lsl, sp_, -1.0, LN2, op.mult, op.add,
                         deps=[tsp2, tk_gd[-1] if tk_gd else s1t.get("gd1")])
            tk_gd.append(p.dma("sp", gH_d[:, t0 + tb * 128:t0 + (tb + 1) * 128],
                               gsl, deps=[tk_g]))
            tk_gd.append(p.dma("sp", lnbH_d[:, t0 + tb * 128:t0 + (tb + 1) * 128],
                               lsl, deps=[tk_bg]))
            yield
        s1t["bg"] = tk_bg
        s1t["gd2"], s1t["gd1"] = tk_gd[-2], tk_gd[-1]
        s1t["ps1"] = tk_bg
        s1t["tile"][(ti, "gdump")] = p.nopf("sp", deps=tk_gd)
        s1t["tile"][(ti, "done")] = p.nopf(
            "sp", deps=[SQ[(ti, fb)] for fb in range(48)] +
            [GD[("d", ti, nch, tb)] for nch in range(8) for tb in range(NTB1)] +
            [s1t["tile"][(ti, "gdump")]])
        s1t["xfree"] = GD[("d", ti, 7, NTB1 - 1)]

    s1_q = [(ti, s1_gen(ti)) for ti in range(NT1)]
    s1_max = [1]
    def s1_drive(n=1):
        while n > 0 and s1_q:
            if s1_q[0][0] > s1_max[0]:
                return
            try:
                next(s1_q[0][1])
                n -= 1
            except StopIteration:
                s1_q.pop(0)

    def s1_finish(ti):
        while s1_q and s1_q[0][0] <= ti:
            try:
                next(s1_q[0][1])
            except StopIteration:
                s1_q.pop(0)

    tk_Sw = {h: None for h in range(H)}
    Ldict = {}
    MSL = dict(gc=0, m1=1, m2=2, m3=3, m4=4, tmp=5, exG=6)

    def msl(nm):
        i = MSL[nm]
        return mst[:, i * 128:(i + 1) * 128]

    tk_s2_init = p.mset("dve", S_sb[:, :], 0.0, fence=False)
    p.mset("dve", S_bf[:, :], 0.0, fence=False)
    p.mset("dve", lA[0:57, :], 0.0, fence=False)
    p.mset("dve", lST[0:57, :], 0.0, fence=False)
    tk_lz = p.mset("dve", lST2[0:57, :], 0.0)
    tk_s2_init = p.nopf("dve", deps=[
        p.dma("pool", lA[48:51, :], ones16[0:3, :], deps=[tk_lz]),
        p.dma("pool", lST[51:54, :], ones16[0:3, :], deps=[tk_lz]),
        p.dma("pool", lST2[54:57, :], ones16[0:3, :], deps=[tk_lz])])
    for h in range(H):
        tk_Sw[h] = tk_s2_init
    tk_ind = p.dma("pool", indb[0:48, :], c_ind[:, :])
    tk_qc = {}
    tk_sg = {}

    def emit_local(cc):
        par = cc % 2
        qcs = qc[:, par * 6144:(par + 1) * 6144]
        cols = colsb[:, par * 64:(par + 1) * 64]
        gcb = gcbb[:, par * 16:(par + 1) * 16]
        prevL = Ldict.get(cc - 1, {})
        out = {}
        ti_c = cc // CPT
        if cc not in tk_qc:
            tk_qc[cc] = p.dma("sp",
                              qc[:, par * 6144:(par + 1) * 6144].rearrange(
                                  "pp (b t) -> pp b t", t=128),
                              qkc_s[cc, :, :, :].rearrange("b pp t -> pp b t"),
                              deps=[s1t["tile"][(ti_c, "done")],
                                    Ldict.get(cc - 2, {}).get("qc_free")])
        cn = cc + 1
        if cn < NCH and cn not in tk_qc and (cn // CPT, "done") in s1t["tile"]:
            tk_qc[cn] = p.dma("sp",
                              qc[:, (1 - par) * 6144:(2 - par) * 6144].rearrange(
                                  "pp (b t) -> pp b t", t=128),
                              qkc_s[cn, :, :, :].rearrange("b pp t -> pp b t"),
                              deps=[s1t["tile"][(cn // CPT, "done")],
                                    prevL.get("qc_free")])
        tk_gl = p.dma("sp", ghb[:, par * 256: par * 256 + 128],
                      gH_d[:, cc * 128:(cc + 1) * 128],
                      deps=[s1t["tile"][(ti_c, "done")],
                            Ldict.get(cc - 2, {}).get("ghb_free")])
        tk_ll = p.dma("sp", ghb[:, par * 256 + 128: par * 256 + 256],
                      lnbH_d[:, cc * 128:(cc + 1) * 128],
                      deps=[s1t["tile"][(ti_c, "done")],
                            Ldict.get(cc - 2, {}).get("ghb_free")])
        tk_gc = p.raw("dve", lambda v, d=msl("gc"), o=ones16[:, :],
                      g_=ghb[:, par * 256: par * 256 + 128]:
                      v.tensor_tensor_scan(d, o, g_, 0.0, op.mult, op.add),
                      deps=[tk_s2_init, prevL.get("mst_free"), tk_gl])
        s1_drive()
        tkm = None
        tk_prevhalf = None
        for gg in range(2):
            tk_k2 = p.act(sqgb[:, :].rearrange("p (h kq t) -> p h kq t", h=8, kq=2)
                          [:, :, 0, :],
                          qcs.rearrange("p (h z t) -> p h z t", h=H, z=3)
                          [:, gg * 8:(gg + 1) * 8, 0, :],
                          AF.Square, deps=[tk_qc[cc], tk_prevhalf,
                                           prevL.get("gram_free")])
            tk_q2 = p.act(sqgb[:, :].rearrange("p (h kq t) -> p h kq t", h=8, kq=2)
                          [:, :, 1, :],
                          qcs.rearrange("p (h z t) -> p h z t", h=H, z=3)
                          [:, gg * 8:(gg + 1) * 8, 1, :],
                          AF.Square, deps=[tk_qc[cc]])
            for hh in range(8):
                h = gg * 8 + hh
                tkm = p.mm(pSQ[0:16, 0:256],
                           oh_b[:, h * 16:(h + 1) * 16],
                           sqgb[:, hh * 256:(hh + 1) * 256],
                           h == 0, h == 15,
                           deps=[tk_k2, tk_q2, prevL.get("sqbank_free")],
                           fence=(hh == 7))
            tk_prevhalf = tkm
            s1_drive()
        tk_ln = p.act(lnssq[:, :], pSQ[0:16, 0:256], AF.Ln, bias=L2EPS,
                      deps=[tkm])
        p.stt(msl("m1"), lnssq[:, 0:128],
              -0.5, msl("gc"), op.mult, op.add, deps=[tk_ln, tk_gc], fence=False)
        tk_m2a = p.stt(msl("tmp"), lnssq[:, 0:128], -0.5,
                       ghb[:, par * 256 + 128: par * 256 + 256],
                       op.mult, op.add, deps=[tk_ln, tk_ll])
        tk_m2b = p.tt(msl("m2"), msl("tmp"), msl("gc"), op.subtract)
        out["ghb_free"] = tk_m2b
        p.stt(msl("tmp"), lnssq[:, 128:256], -0.5, msl("gc"), op.mult, op.add,
              fence=False)
        p.ts(msl("m3"), msl("tmp"), LN_HALF_DK, None, op.add, fence=False)
        tk_m = p.ts(msl("m4"), msl("m2"), msl("gc")[:, 127:128], None, op.add)
        out["mst_done"] = tk_m
        tk_pc = []
        for mi, nm in enumerate(("m1", "m2", "m3")):
            b0 = mi * 3
            p.cp("dve", mpc[:, b0 * 128:(b0 + 1) * 128], msl(nm), deps=[tk_m],
                 fence=False)
            p.tt(msl("tmp"), msl(nm), mpc[:, b0 * 128:(b0 + 1) * 128], op.subtract,
                 fence=False)
            p.cp("dve", mpc[:, (b0 + 1) * 128:(b0 + 2) * 128], msl("tmp"), fence=False)
            p.tt(msl("tmp"), msl("tmp"), mpc[:, (b0 + 1) * 128:(b0 + 2) * 128],
                 op.subtract, fence=False)
            tk_pc.append(p.cp("dve", mpc[:, (b0 + 2) * 128:(b0 + 3) * 128],
                              msl("tmp")))
        tk_pc = tk_pc[-1]
        tk_asm = []
        for j in range(3):
            tk_asm.append(p.dma("sp", lA[16 * j:16 * (j + 1), :],
                                mpc[:, j * 128:(j + 1) * 128],
                                deps=[tk_pc, prevL.get("lhs_free"), tk_ind]))
            tk_asm.append(p.dma("sp", lST[16 * j:16 * (j + 1), :],
                                mpc[:, (3 + j) * 128:(4 + j) * 128],
                                deps=[tk_pc, prevL.get("lhs_free")]))
            tk_asm.append(p.dma("pool", lST2[16 * j:16 * (j + 1), :],
                                mpc[:, (3 + j) * 128:(4 + j) * 128],
                                deps=[tk_pc, prevL.get("lhs_free")]))
        for oi, mb in enumerate((3, 0, 6)):
            for j in range(3):
                tk_asm.append(p.dma("pool", indb[48 + oi * 3 + j:
                                                 49 + oi * 3 + j, :],
                                    mpc[:, (mb + j) * 128:(mb + j + 1) * 128],
                                    deps=[tk_pc, prevL.get("lhs_free"),
                                          tk_ind]))
        if locstop == 1:
            Ldict[cc] = out
            return out
        tk_tr1 = p.tr(pSQ[:, 512:528], msl("m1"), ident_f[0:16, 0:16],
                      deps=[tk_m, prevL.get("sqbank_free2")])
        tk_tr2 = p.tr(pSQ[:, 528:544], msl("m4"), ident_f[0:16, 0:16], deps=[tk_m])
        tk_tr3 = p.tr(pSQ[:, 544:560], msl("m3"), ident_f[0:16, 0:16], deps=[tk_m])
        tk_cols = p.act(cols[:, 0:48], pSQ[:, 512:560], AF.Exp,
                        deps=[tk_tr1, tk_tr2, tk_tr3, prevL.get("cols_free")])
        out["cols"] = tk_cols
        s1_drive()
        tk_xg = p.act(msl("exG")[:, 0:1], msl("gc")[:, 127:128], AF.Exp,
                      deps=[tk_gc])
        tk_eg = p.tr(pSQ[0:1, 560:576], msl("exG")[:, 0:1], ident_f[0:16, 0:16],
                     deps=[tk_xg, tk_cols])
        tk_egc = p.cp("dve", grow[0:1, 0:16], pSQ[0:1, 560:576], deps=[tk_eg])
        tk_gcb = p.mm(pSQ[:, 576:592], ones1f[:, :], grow[0:1, 0:16], True, True,
                      deps=[tk_egc], fence=True)
        out["gcb"] = p.cp("dve", gcb[:, 0:16], pSQ[:, 576:592],
                          deps=[tk_gcb, prevL.get("gcb_free")])
        s1_drive()

        if locstop == 2:
            Ldict[cc] = out
            return out
        tk_last_ch = prevL.get("ch_last")
        tk_w7 = []
        tk_gbar = []
        tk_sq_last = None
        for g in range(2):
            h0 = g * 8

            def kqv(hh):
                b = ((h0 + hh) * 3) * 128
                return (qcs[:, b + 128:b + 256], qcs[:, b:b + 128],
                        qcs[:, b + 256:b + 384])

            tkmh = [None, None]
            for hh in range(8):
                q_, k_, v_ = kqv(hh)
                tkmh[hh // 4] = p.mm(pCH[:, hh * 128:(hh + 1) * 128], k_,
                                     ident_b[:, :], True, True,
                                     deps=[tk_qc[cc], tk_cols, out["gcb"],
                                           tk_last_ch],
                                     fence=(hh % 4 == 3))
            tk_ktil = []
            for hh in range(8):
                h = h0 + hh
                tk_ktil.append(p.ts(r0b[:, hh * 256 + 128: hh * 256 + 256],
                                    pCH[:, hh * 128:(hh + 1) * 128],
                                    cols[:, h:h + 1], -1.0, op.mult, op.mult,
                                    deps=[tkmh[hh // 4],
                                          prevL.get("r0_free") if g == 0
                                          else None]))
                tk_ktil.append(p.ts(kdecb[:, par * 2048 + h * 128:
                                          par * 2048 + (h + 1) * 128],
                                    pCH[:, hh * 128:(hh + 1) * 128],
                                    cols[:, 16 + h:16 + h + 1], None, op.mult,
                                    deps=[tkmh[hh // 4],
                                          prevL.get("kdec_free")]))
            tk_ktil = p.nopf("dve", deps=tk_ktil)
            s1_drive()
            if locstop == 3:
                Ldict[cc] = out
                return out
            tkg4 = [None] * 4
            for hh in range(8):
                q_, k_, v_ = kqv(hh)
                dst = pCH if hh < 4 else pSQ
                off = (hh % 4) * 256
                b = ((h0 + hh) * 3) * 128
                tkg4[hh // 2] = p.mm(dst[:, off:off + 256], k_,
                                     qcs[:, b:b + 256], True, True,
                                     deps=[tk_ktil, tk_qc[cc], tk_sq_last,
                                           tk_ln],
                                     fence=(hh % 2 == 1))
            tk_ge = [p.cp("dve", sqgb[:, qb * 512:(qb + 1) * 512],
                          (pCH if qb < 2 else pSQ)[:, (qb % 2) * 512:
                                                   (qb % 2 + 1) * 512],
                          deps=[tkg4[qb]]) for qb in range(4)]
            tk_gram = p.nopf("dve", deps=tk_ge)
            s1_drive()
            if locstop == 4:
                Ldict[cc] = out
                return out

            def diff_build(orient, mask_i, out_sb, sign, kq, mm_deps,
                           exp_deps):
                lhs = (lA, lST, lST2)[orient]
                tkk_h = [None, None]
                for half in range(2):
                    p.mm(pCH[:, half * 512:(half + 1) * 512], lhs[:, :],
                         indb[:, g * 1024 + half * 512:
                              g * 1024 + (half + 1) * 512],
                         True, False,
                         deps=list(mm_deps) + tk_asm + [tk_ind],
                         fence=False)
                    tkk_h[half] = p.mm(
                        pCH[:, half * 512:(half + 1) * 512], ident_b[:, :],
                        masks_b[:, mask_i * 1024 + half * 512:
                                mask_i * 1024 + (half + 1) * 512],
                        False, True, deps=[], fence=True)
                tks = []
                tke = None
                for half in range(2):
                    tke = p.act(expd[:, :], pCH[:, half * 512:(half + 1) * 512],
                                AF.Exp,
                                deps=[tkk_h[half],
                                      tks[-1] if tks else None] +
                                list(exp_deps))
                    tks.append(p.stt(
                        out_sb.rearrange("p (hh s) -> p hh s", hh=8)
                        [:, half * 4:(half + 1) * 4, :],
                        expd[:, :].rearrange("p (hh s) -> p hh s", hh=4),
                        float(sign),
                        sqgb[:, :].rearrange("p (hh kq s) -> p hh kq s", hh=8, kq=2)
                        [:, half * 4:(half + 1) * 4, kq, :],
                        op.mult, op.mult, deps=[tke, tk_gram]))
                return p.nopf("dve", deps=tks), tkk_h[1], tke

            tk_n0, tke0, txp0 = diff_build(0, 0, n0b[:, 0:1024], -1.0, 0,
                                           [tk_gram], [tk_gram])
            s1_drive()
            tk_nb0, tke1, txp1 = diff_build(1, 1, n0b[:, 1024:2048], -1.0, 0,
                                            [txp0], [tk_n0])
            s1_drive()
            tk_wb, tke2, txp2 = diff_build(2, 2,
                                           wbarb[:, par * 2048 + g * 1024:
                                                 par * 2048 + (g + 1) * 1024],
                                           1.0, 1, [txp1], [tk_nb0])
            out[("wbar", g)] = tk_wb
            out["lhs_free"] = tke2
            if locstop == 5:
                Ldict[cc] = out
                return out

            tkmh = [None, None]
            for hh in range(8):
                q_, k_, v_ = kqv(hh)
                tkmh[hh // 4] = p.mm(pCH[:, hh * 128:(hh + 1) * 128], v_,
                                     ident_b[:, :], True, True,
                                     deps=[tk_qc[cc], txp2, tk_wb],
                                     fence=(hh % 4 == 3))
            tk_r0 = [p.cp("dve",
                          r0b[:, bh * 1024:(bh + 1) * 1024]
                          .rearrange("p (hh vk t) -> p hh vk t",
                                     hh=4, vk=2)[:, :, 0, :],
                          pCH[:, bh * 512:(bh + 1) * 512]
                          .rearrange("p (hh t) -> p hh t", hh=4),
                          deps=[tkmh[bh]]) for bh in range(2)]
            tk_r0 = p.nopf("dve", deps=tk_r0)
            s1_drive()
            if locstop == 6:
                Ldict[cc] = out
                return out

            for ps in range(2):
                hh0 = ps * 4
                tk_w0 = p.nopf("dve", deps=[tk_r0, tk_w7[-1] if tk_w7 else None])
                tk_wh = [tk_w0, tk_w0]
                tk_ncur = p.nopf("act", deps=[tk_n0, tk_sq_last])
                tk_bcur = tk_nb0
                ncur = 0
                for lev in range(7):
                    tkh = [None, None]
                    for half in range(2):
                        for hh4 in (half * 2, half * 2 + 1):
                            hh = hh0 + hh4
                            lhsT = (n0b[:, 1024 + hh * 128:1024 + (hh + 1) * 128]
                                    if lev == 0 else
                                    nbar[:, ncur * 512 + hh4 * 128:
                                         ncur * 512 + (hh4 + 1) * 128])
                            rhs = (r0b[:, hh * 256:(hh + 1) * 256] if lev == 0
                                   else wchn[:, (lev % 2) * 1024 + hh4 * 256:
                                             (lev % 2) * 1024 + (hh4 + 1) * 256])
                            p.mm(pCH[:, hh4 * 256:(hh4 + 1) * 256], lhsT, rhs,
                                 hh4 % 2 == 0, False,
                                 deps=[tk_wh[half], tk_bcur], fence=False)
                        for hh4 in (half * 2, half * 2 + 1):
                            hh = hh0 + hh4
                            rhs = (r0b[:, hh * 256:(hh + 1) * 256] if lev == 0
                                   else wchn[:, (lev % 2) * 1024 + hh4 * 256:
                                             (lev % 2) * 1024 + (hh4 + 1) * 256])
                            tkh[half] = p.mm(
                                pCH[:, hh4 * 256:(hh4 + 1) * 256],
                                ident_b[:, :], rhs, False, hh4 % 2 == 1,
                                deps=[], fence=(hh4 == half * 2 + 1))
                    if lev == 6:
                        base = par * 2048 + g * 1024 + hh0 * 128
                        tvl = []
                        for h in range(2):
                            tvl.append(p.cp(
                                "dve",
                                trvb[:, base + h * 256: base + h * 256 + 256]
                                .rearrange("p (hh t) -> p hh t", hh=2),
                                pCH[:, h * 512:(h + 1) * 512].rearrange(
                                    "p (hh vk t) -> p hh vk t", hh=2, vk=2)
                                [:, :, 0, :],
                                deps=[tkh[h],
                                      Ldict.get(cc - 1, {}).get("trv_free")
                                      if h == 0 else None]))
                        tgl = []
                        for h in range(2):
                            tgl.append(p.cp(
                                "act",
                                gtrb[:, hh0 * 128 + h * 256:
                                     hh0 * 128 + h * 256 + 256]
                                .rearrange("p (hh t) -> p hh t", hh=2),
                                pCH[:, h * 512:(h + 1) * 512].rearrange(
                                    "p (hh vk t) -> p hh vk t", hh=2, vk=2)
                                [:, :, 1, :],
                                deps=[tkh[h], tvl[h]]))
                        tk_wcur = p.nopf("dve", deps=tvl + tgl)
                    else:
                        ce = "dve" if lev % 2 == 0 else "act"
                        nxs = ((lev + 1) % 2) * 1024
                        tk_wh = [p.cp(ce, wchn[:, nxs + h * 512:
                                               nxs + (h + 1) * 512],
                                      pCH[:, h * 512:(h + 1) * 512],
                                      deps=[tkh[h]]) for h in range(2)]
                    if lev < 6:
                        tks = None
                        for hh4 in range(4):
                            hh = hh0 + hh4
                            lhsT = (n0b[:, 1024 + hh * 128:1024 + (hh + 1) * 128]
                                    if lev == 0 else
                                    nbar[:, ncur * 512 + hh4 * 128:
                                         ncur * 512 + (hh4 + 1) * 128])
                            rhsn = (n0b[:, hh * 128:(hh + 1) * 128] if lev == 0
                                    else ntok[:, ncur * 512 + hh4 * 128:
                                              ncur * 512 + (hh4 + 1) * 128])
                            tks = p.mm(pSQ[:, hh4 * 128:(hh4 + 1) * 128],
                                       lhsT, rhsn, True, True,
                                       deps=[tk_ncur, tk_bcur],
                                       fence=(hh4 == 3))
                        tksA = tks
                        for hh4 in range(4):
                            hh = hh0 + hh4
                            lhsTn = (n0b[:, hh * 128:(hh + 1) * 128] if lev == 0
                                     else ntok[:, ncur * 512 + hh4 * 128:
                                               ncur * 512 + (hh4 + 1) * 128])
                            rhsb = (n0b[:, 1024 + hh * 128:1024 + (hh + 1) * 128]
                                    if lev == 0 else
                                    nbar[:, ncur * 512 + hh4 * 128:
                                         ncur * 512 + (hh4 + 1) * 128])
                            tks = p.mm(pSQ[:, 512 + hh4 * 128:
                                           512 + (hh4 + 1) * 128],
                                       lhsTn, rhsb, True, True,
                                       deps=[], fence=(hh4 == 3))
                        nnx = 1 - ncur
                        cn = "act" if lev % 2 == 0 else "dve"
                        tk_ncur = p.cp(cn, ntok[:, nnx * 512:(nnx + 1) * 512],
                                       pSQ[:, 0:512], deps=[tksA])
                        tk_bcur = p.cp(cn, nbar[:, nnx * 512:(nnx + 1) * 512],
                                       pSQ[:, 512:1024], deps=[tks])
                        tk_sq_last = tk_bcur
                        ncur = nnx
                    s1_drive()
                    if locstop == 60 + ps * 10 + lev:
                        Ldict[cc] = out
                        return out
                tk_w7.append(tk_wcur)
            if locstop == 7:
                Ldict[cc] = out
                return out

            tkmh = [None, None]
            for hh in range(8):
                tkmh[hh // 4] = p.mm(pCH[:, hh * 128:(hh + 1) * 128],
                                     gtrb[:, hh * 128:(hh + 1) * 128],
                                     ident_b[:, :], True, True,
                                     deps=tk_w7[-2:], fence=(hh % 4 == 3))
            for bh in range(2):
                tk_gbar.append(p.cp(
                    "dve",
                    gbarb[:, par * 2048 + g * 1024 + bh * 512:
                          par * 2048 + g * 1024 + (bh + 1) * 512],
                    pCH[:, bh * 512:(bh + 1) * 512],
                    deps=[tkmh[bh], prevL.get("gbar_free")]))
            tk_last_ch = tk_gbar[-1]
            s1_drive()
            if locstop == 8:
                Ldict[cc] = out
                return out

        out["w7"] = p.nopf("dve", deps=tk_w7)
        out["gbar"] = p.nopf("dve", deps=tk_gbar)
        out["gram_free"] = out[("wbar", 1)]
        out["kdec"] = tk_ktil
        out["ch_last"] = tk_last_ch
        out["sqbank_free"] = tk_sq_last
        out["sqbank_free2"] = tk_sq_last
        out["mst_free"] = out["cols"]
        Ldict[cc] = out
        return out

    def emit_serial(cc):
        par = cc % 2
        qcs = qc[:, par * 6144:(par + 1) * 6144]
        cols = colsb[:, par * 64:(par + 1) * 64]
        gcb = gcbb[:, par * 16:(par + 1) * 16]
        lc = Ldict[cc]
        out = {}
        tk_qsl = []
        tk_o = []
        tk_sup = []
        for g in range(2):
            h0 = g * 8
            tkmh = [None, None]
            for hh in range(8):
                h = h0 + hh
                tkmh[hh // 4] = p.mm(
                    pSU[:, hh * 128:(hh + 1) * 128],
                    gbarb[:, par * 2048 + h * 128:par * 2048 + (h + 1) * 128],
                    S_bf[:, h * 128:(h + 1) * 128], True, True,
                    deps=[lc["gbar"], tk_Sw[h],
                          tk_o[-1] if tk_o else None,
                          tk_sup[-1] if tk_sup else
                          Ldict.get(cc - 1, {}).get("su_free")],
                    fence=(hh % 4 == 3))
            tk_ueh = []
            for bh in range(2):
                tk_ueh.append(p.stt(
                    ub[:, bh * 512:(bh + 1) * 512]
                    .rearrange("p (hh t) -> p hh t", hh=4),
                    pSU[:, bh * 512:(bh + 1) * 512]
                    .rearrange("p (hh t) -> p hh t", hh=4), 1.0,
                    trvb[:, par * 2048 + g * 1024 + bh * 512:
                         par * 2048 + g * 1024 + (bh + 1) * 512]
                    .rearrange("p (hh t) -> p hh t", hh=4),
                    op.mult, op.add, deps=[tkmh[bh], lc["w7"]]))
            tk_ue = tk_ueh[-1]
            out[("u", g)] = tk_ue
            s1_drive()
            tkqh = [None, None]
            for hh in range(8):
                q_ = qcs[:, ((h0 + hh) * 3) * 128 + 128:
                         ((h0 + hh) * 3) * 128 + 256]
                h = h0 + hh
                tkqh[hh // 4] = p.mm(pSU[:, hh * 128:(hh + 1) * 128], q_,
                                     S_bf[:, h * 128:(h + 1) * 128], True, True,
                                     deps=[tk_Sw[h], tk_ueh[hh // 4]],
                                     fence=(hh % 4 == 3))
            tqs = []
            for hh in range(8):
                h = h0 + hh
                tqs.append(p.ts(qst[:, g * 1024 + hh * 128:
                                    g * 1024 + (hh + 1) * 128],
                                pSU[:, hh * 128:(hh + 1) * 128],
                                cols[:, 32 + h:32 + h + 1], None, op.mult,
                                deps=[tkqh[hh // 4]]))
            tk_qs = p.nopf("dve", deps=tqs)
            tk_qsl.append(tk_qs)
            s1_drive()
            tkm4h = [None, None]
            for hh in range(8):
                h = h0 + hh
                tkm4h[hh // 4] = p.mm(
                    pSU[:, hh * 128:(hh + 1) * 128],
                    kdecb[:, par * 2048 + h * 128:par * 2048 + (h + 1) * 128],
                    ub[:, hh * 128:(hh + 1) * 128], True, True,
                    deps=[tk_ueh[hh // 4], lc["kdec"], tqs[hh // 4 * 4 + 3]],
                    fence=(hh % 4 == 3))
            for hh in range(8):
                h = h0 + hh
                tk_su = p.stt(S_sb[:, h * 128:(h + 1) * 128],
                              S_sb[:, h * 128:(h + 1) * 128],
                              gcb[:, h:h + 1], pSU[:, hh * 128:(hh + 1) * 128],
                              op.mult, op.add, deps=[tkm4h[hh // 4], lc["gcb"]])
                tk_Sw[h] = p.cp("dve", S_bf[:, h * 128:(h + 1) * 128],
                                S_sb[:, h * 128:(h + 1) * 128], deps=[tk_su])
                tk_sup.append(tk_Sw[h])
            s1_drive()
            tk3h = [None, None]
            for hh in range(8):
                h = h0 + hh
                tk3h[hh // 4] = p.mm(
                    pSU[:, hh * 128:(hh + 1) * 128],
                    wbarb[:, par * 2048 + h * 128:par * 2048 + (h + 1) * 128],
                    ub[:, hh * 128:(hh + 1) * 128], True, True,
                    deps=[tk_ueh[hh // 4], lc[("wbar", g)],
                          tk_sup[g * 8 + hh // 4 * 4 + 3]],
                    fence=(hh % 4 == 3))
            for bh in range(2):
                tk_o.append(p.stt(
                    qst[:, g * 1024 + bh * 512:g * 1024 + (bh + 1) * 512],
                    qst[:, g * 1024 + bh * 512:g * 1024 + (bh + 1) * 512], 1.0,
                    pSU[:, bh * 512:(bh + 1) * 512], op.mult, op.add,
                    deps=[tk3h[bh], tqs[bh * 4 + 3]]))
            s1_drive()
        out["o"] = tk_o
        out["s_done"] = tk_sup[-1]
        lc["su_free"] = tk_o[-1]
        lc["qc_free"] = p.nopf("pe", deps=[])
        lc["trv_free"] = out[("u", 1)]
        lc["gbar_free"] = out[("u", 1)]
        lc["kdec_free"] = out["s_done"]
        lc["cols_free"] = p.nopf("dve", deps=tk_qsl + [out["s_done"]])
        lc["gcb_free"] = out["s_done"]
        return out

    def emit_opost(cc, ser):
        par = cc % 2
        sgs = sgc[:, par * D:(par + 1) * D]
        lc = Ldict[cc]
        tk_ofd = []
        if cc not in tk_sg:
            tk_sg[cc] = p.dma("sp", sgc[:, par * D:(par + 1) * D],
                              sg_s[cc * 128:(cc + 1) * 128, :],
                              deps=[s1t["tile"][(cc // CPT, "done")],
                                    Ldict.get(cc - 2, {}).get("sg_free")])
        cn = cc + 1
        if (cn < NCH and cn not in tk_sg
                and (cn // CPT, "done") in s1t["tile"]):
            tk_sg[cn] = p.dma("sp", sgc[:, (1 - par) * D:(2 - par) * D],
                              sg_s[cn * 128:(cn + 1) * 128, :],
                              deps=[s1t["tile"][(cn // CPT, "done")],
                                    Ldict.get(cc - 1, {}).get("sg_free")])
        tk_og_l = []
        for g in range(2):
            h0 = g * 8
            tk_og = p.tt(ost[:, :], qst[:, g * 1024:(g + 1) * 1024],
                         sgs[:, g * 1024:(g + 1) * 1024], op.mult,
                         deps=[ser["o"][2 * g + 1], tk_sg[cc]])
            tk_og_l.append(tk_og)
            tk_s2 = p.act(qst[:, g * 1024:(g + 1) * 1024], ost[:, :],
                          AF.Square, deps=[tk_og])
            tk_red = p.red(rms[:, g * 8:(g + 1) * 8],
                           qst[:, g * 1024:(g + 1) * 1024].rearrange(
                               "p (hh s) -> p hh s", hh=8),
                           op.add, deps=[tk_s2])
            p.act(rms[:, 16 + g * 8:16 + (g + 1) * 8], rms[:, g * 8:(g + 1) * 8],
                  AF.Ln, bias=EPS, scale=1.0 / 128.0, deps=[tk_red], fence=False)
            tk_rs = p.act(rms[:, 32 + g * 8:32 + (g + 1) * 8],
                          rms[:, 16 + g * 8:16 + (g + 1) * 8], AF.Exp, scale=-0.5)
            tk_on = []
            for hh in range(8):
                h = h0 + hh
                tk_on.append(p.stt(onb[:, hh * 128:(hh + 1) * 128],
                                   ost[:, hh * 128:(hh + 1) * 128],
                                   rms[:, 32 + g * 8 + hh:32 + g * 8 + hh + 1],
                                   wtile[:, h * 128:(h + 1) * 128],
                                   op.mult, op.mult, deps=[tk_rs, tk_og]))
            tkmh = [None, None]
            for hh in range(8):
                tkmh[hh // 4] = p.mm(pSU[:, hh * 128:(hh + 1) * 128],
                                     onb[:, hh * 128:(hh + 1) * 128],
                                     ident_b[:, :], True, True,
                                     deps=[tk_on[hh], ser["o"][3]],
                                     fence=(hh % 4 == 3))
            tk_ofe = [p.cp("dve", ofst[:, bh * 512:(bh + 1) * 512],
                           pSU[:, bh * 512:(bh + 1) * 512],
                           deps=[tkmh[bh], tk_ofd[-1] if tk_ofd else
                                 Ldict.get(cc - 1, {}).get("opost_done")])
                      for bh in range(2)]
            tk_ofe = p.nopf("dve", deps=tk_ofe)
            s1_drive()
            tk_ofd.append(p.dma("pool",
                                of_s[cc, h0:h0 + 8, :, :].rearrange(
                                    "hh pp tt -> pp hh tt"),
                                ofst[:, :].rearrange("pp (hh tt) -> pp hh tt",
                                                     tt=128),
                                deps=[tk_ofe]))
            lc["su_free"] = tk_ofe
        lc["sg_free"] = p.nopf("dve", deps=tk_og_l)
        lc["opost_done"] = p.nopf("pool", deps=tk_ofd)
        return tk_ofd

    if phases == "s1":
        s1_max[0] = NT1
        s1_finish(NT1 - 1)
        p.barrier()
        esA.close()
        es12.close()
        p.replay()
        es.close()
        return nc
    s1_finish(0)
    s1_drive(12)
    tk_sg[0] = p.dma("sp", sgc[:, 0:D], sg_s[0:128, :],
                     deps=[s1t["tile"][(0, "done")]])
    emit_local(0)
    stop_cc = NCH
    stop_ser_only = False
    if phases.startswith("cc"):
        stop_cc = min(int(phases[2:].rstrip("s")), NCH)
        stop_ser_only = phases.endswith("s")
    for cc in range(stop_cc):
        s1_max[0] = cc // CPT + 2
        ser = emit_serial(cc)
        if stop_ser_only and cc == stop_cc - 1:
            break
        if cc + 1 < NCH:
            s1_finish((cc + 1) // CPT)
            emit_local(cc + 1)
        emit_opost(cc, ser)
    s1_max[0] = NT1
    s1_finish(NT1 - 1)
    if stop_cc < NCH or stop_ser_only:
        p.barrier()
        esA.close()
        es12.close()
        p.replay()
        es.close()
        return nc
    tk_of_all = [Ldict[cc]["opost_done"] for cc in range(NCH)]
    p.barrier()
    esA.close()
    es12.close()
    if phases == "A":
        p.replay()
        es.close()
        return nc

    esf = ExitStack()
    SF = esf.enter_context
    fw = SF(nc.sbuf_tensor("fw", [128, 2 * D], F32)).ap()
    facc = SF(nc.sbuf_tensor("facc", [128, 16], F32)).ap()
    ftmp = SF(nc.sbuf_tensor("ftmp", [128, D], F32)).ap()
    ftrn = SF(nc.sbuf_tensor("ftrn", [128, D], BF16)).ap()
    ftrn2 = SF(nc.sbuf_tensor("ftrn2", [128, D], BF16)).ap()
    fsm = SF(nc.sbuf_tensor("fsm", [1, 16], F32)).ap()
    fcol = SF(nc.sbuf_tensor("fcol", [128, 1], F32)).ap()
    ps_f = SF(nc.psum_tensor("ps_f", [128, 1024], F32)).ap()

    tk_blk = []
    for rb in range(16):
        sl = rb % 2
        tk_l = p.dma("sp", fw[:, sl * D:(sl + 1) * D],
                     ffn_w[rb * 128:(rb + 1) * 128, :],
                     deps=[tk_blk[rb - 2] if rb >= 2 else None])
        tk_blk.append(p.red(facc[:, rb:rb + 1], fw[:, sl * D:(sl + 1) * D],
                            op.add, absv=True, deps=[tk_l]))
    tk_r = p.red(fcol[:, 0:1], facc[:, :], op.add, deps=tk_blk)
    tk_t = p.tr(ps_f[0:1, 0:128], fcol[:, 0:1], ident_f[:, :], deps=[tk_r])
    tk_r2 = p.red(fsm[0:1, 0:1], ps_f[0:1, 0:128], op.add, deps=[tk_t])
    p.ts(fsm[0:1, 1:2], fsm[0:1, 0:1], 1.0 / (D * D), EPS, op.mult, op.max,
         deps=[tk_r2], fence=False)
    p.raw("dve", lambda v: v.reciprocal(fsm[0:1, 2:3], fsm[0:1, 1:2]), fence=False)
    tk_al = p.ts(fsm[0:1, 3:4], fsm[0:1, 1:2], 1.0 / 127.0, None, op.mult)
    if dbg:
        p.dma("sp", alpha_d[:, :], fsm[0:1, 0:4], deps=[tk_al])
    tk_bc = p.mm(ps_f[:, 128:132], ones1f[:, :], fsm[0:1, 0:4], True, True,
                 deps=[tk_al, tk_t] + list(TC), fence=True)
    tk_at = p.cp("dve", alpha_t[:, 0:4], ps_f[:, 128:132], deps=[tk_bc])

    tk_rd = {}
    for rb in range(16):
        sl = rb % 2
        tk_l = p.dma("sp", fw[:, sl * D:(sl + 1) * D],
                     ffn_w[rb * 128:(rb + 1) * 128, :],
                     deps=[tk_at, tk_rd.get(rb - 2)])
        tk1 = p.act(ftmp[:, :], fw[:, sl * D:(sl + 1) * D], AF.Copy,
                    bias=MAGIC, scale=alpha_t[:, 2:3],
                    deps=[tk_l, tk_at, tk_rd.get(("ts", rb - 1))])
        tk_rd[rb] = tk1
        tk2 = p.act(ftmp[:, :], ftmp[:, :], AF.Copy, bias=-MAGIC, deps=[tk1])
        tk3 = p.ts(ftrn[:, :], ftmp[:, :], 1.0, -1.0, op.min, op.max,
                   deps=[tk2, tk_rd.get(("tr", rb - 1))])
        tk_rd[("ts", rb)] = tk3
        tk_cps = []
        for ib in range(16):
            poff = (ib % 2) * 512
            tkt = p.mm(ps_f[:, poff:poff + 128],
                       ftrn[:, ib * 128:(ib + 1) * 128], ident_b[:, :], True, True,
                       deps=[tk3, tk_cps[ib - 2] if ib >= 2 else None],
                       fence=True)
            tk_cps.append(p.cp("dve", ftrn2[:, ib * 128:(ib + 1) * 128],
                               ps_f[:, poff:poff + 128],
                               deps=[tkt, tk_rd.get(("dma", rb - 1, ib))]))
            tk_rd[("tr", rb)] = tkt
            tk_rd[("dma", rb, ib)] = p.dma(
                "pool", tern_s[ib * 128:(ib + 1) * 128, rb * 128:(rb + 1) * 128],
                ftrn2[:, ib * 128:(ib + 1) * 128], deps=[tk_cps[-1]])
    p.barrier()
    esf.close()
    if phases == "ffn":
        p.replay()
        es.close()
        return nc

    _stage3(nc, p, es, T, T3, NT3, x_in, w_o, of_s, tern_s, y_out,
            ident_b, ident_f, btile, alpha_t)

    p.replay()
    es.close()
    return nc


def _stage3(nc, p, es, T, T3, NT3, x_in, w_o, of_s, tern_s, y_out,
            ident_b, ident_f, btile, alpha_t):
    SG = es.enter_context
    wo_sb = SG(nc.sbuf_tensor("wo_sb", [128, 16 * D], BF16)).ap()
    tern_sb = SG(nc.sbuf_tensor("tern_sb", [128, 16 * D], BF16)).ap()
    ofsb = SG(nc.sbuf_tensor("ofsb", [128, 2 * D], BF16)).ap()
    xin = SG(nc.sbuf_tensor("xin", [128, D], F32)).ap()
    scr = SG(nc.sbuf_tensor("scr", [128, D], F32)).ap()
    dsc = SG(nc.sbuf_tensor("dsc", [128, 512], F32)).ap()
    yb = SG(nc.sbuf_tensor("yb", [128, 2 * D], F32)).ap()
    qx = SG(nc.sbuf_tensor("qx", [128, 2 * D], BF16)).ap()
    qxf = SG(nc.sbuf_tensor("qxf", [128, D], BF16)).ap()
    zb = SG(nc.sbuf_tensor("zb", [128, D], F32)).ap()
    outb = SG(nc.sbuf_tensor("outb", [128, D], F32)).ap()
    stats = SG(nc.sbuf_tensor("stats", [128, 48], F32)).ap()
    ps2 = [SG(nc.psum_tensor("psA", [128, 2048], F32)).ap(),
           SG(nc.psum_tensor("psB", [128, 2048], F32)).ap()]

    tk_wo = [p.dma("pool", wo_sb[:, kt * D:(kt + 1) * D],
                   w_o[kt * 128:(kt + 1) * 128, :]) for kt in range(16)]
    tk_tern = [p.dma("sp", tern_sb[:, kt * D:(kt + 1) * D],
                     tern_s[kt * 128:(kt + 1) * 128, :]) for kt in range(16)]
    tk_out = []
    prev = {}
    ofld = {}
    xld = {}
    NTB = T // 128

    def emit_attn(tb):
        sl = tb % 2
        ps = ps2[sl]
        st = stats[:, sl * 24:(sl + 1) * 24]
        yv = yb[:, sl * D:(sl + 1) * D]
        qv = qx[:, sl * D:(sl + 1) * D]
        if tb not in ofld:
            ofld[tb] = p.dma("sp", ofsb[:, sl * D:(sl + 1) * D].rearrange(
                                 "pp (h tt) -> pp h tt", tt=128),
                             of_s[tb, :, :, :].rearrange("h pp tt -> pp h tt"),
                             deps=[prev.get(("attnpe", tb - 2))])
        tk_of = ofld[tb]
        if tb + 1 < NTB:
            ofld[tb + 1] = p.dma(
                "sp", ofsb[:, (1 - sl) * D:(2 - sl) * D].rearrange(
                    "pp (h tt) -> pp h tt", tt=128),
                of_s[tb + 1, :, :, :].rearrange("h pp tt -> pp h tt"),
                deps=[prev.get(("attnpe", tb - 1))])
        tk_x = xld.pop(tb, None)
        if tk_x is None:
            tk_x = p.dma("sp", xin[:, :], x_in[tb * 128:(tb + 1) * 128, :],
                         deps=[prev.get(("s1", tb - 1))])
        tkm = None
        for oc in range(4):
            for kt in range(16):
                tkm = p.mm(ps[:, oc * 512:(oc + 1) * 512],
                           ofsb[:, sl * D + kt * 128: sl * D + (kt + 1) * 128],
                           wo_sb[:, kt * D + oc * 512: kt * D + (oc + 1) * 512],
                           kt == 0, kt == 15,
                           deps=[tk_of, prev.get(("z", tb - 2, oc)),
                                 prev.get(("qxf", tb - 2))] + tk_wo,
                           fence=(kt == 15))
        prev[("attnpe", tb)] = tkm
        tk_s1 = p.stt(yv[:, :], xin[:, :], 1.0, ps[:, :], op.mult, op.add,
                      deps=[tkm, tk_x, prev.get(("s2", tb - 1))],
                      accum=st[:, 0:1])
        prev[("s1", tb)] = tk_s1
        if tb + 1 < NTB:
            xld[tb + 1] = p.dma("sp", xin[:, :],
                                x_in[(tb + 1) * 128:(tb + 2) * 128, :],
                                deps=[tk_s1])
        hD = D // 2
        tk_s2a = p.act(scr[:, 0:hD], yv[:, 0:hD], AF.Square, deps=[tk_s1],
                       accum=st[:, 1:2])
        tk_sq1 = p.stt(dsc[:, :], yv[:, hD:hD + 512], 1.0,
                       yv[:, hD:hD + 512], op.mult, op.mult,
                       deps=[tk_s1], accum=st[:, 19:20])
        tk_s2b = p.stt(dsc[:, :], yv[:, hD + 512:D], 1.0, yv[:, hD + 512:D],
                       op.mult, op.mult, deps=[tk_sq1], accum=st[:, 21:22])
        prev[("s2", tb)] = tk_s2b
        p.tt(st[:, 19:20], st[:, 19:20], st[:, 21:22], op.add,
             deps=[tk_s2b], fence=False)
        p.tt(st[:, 1:2], st[:, 1:2], st[:, 19:20], op.add,
             deps=[tk_s2a], fence=False)
        p.ts(st[:, 2:3], st[:, 0:1], 1.0 / D, None, op.mult, fence=False)
        p.tt(st[:, 3:4], st[:, 2:3], st[:, 2:3], op.mult, fence=False)
        tk_var = p.stt(st[:, 4:5], st[:, 1:2], 1.0 / D, st[:, 3:4],
                       op.mult, op.subtract)
        p.act(st[:, 5:6], st[:, 4:5], AF.Ln, bias=EPS, deps=[tk_var],
              fence=False)
        tk_st = p.act(st[:, 6:7], st[:, 5:6], AF.Exp, scale=-0.5)
        tk_y = p.ts(yv[:, :], yv[:, :], st[:, 2:3], st[:, 6:7],
                    op.subtract, op.mult, deps=[tk_st, tk_s2a])
        tk_am = p.red(st[:, 7:8], yv[:, :], op.max, absv=True, deps=[tk_y])
        p.ts(st[:, 8:9], st[:, 7:8], EPS, None, op.max, deps=[tk_am],
             fence=False)
        p.raw("dve", lambda v, a=st: v.reciprocal(a[:, 9:10], a[:, 8:9]),
              fence=False)
        p.ts(st[:, 10:11], st[:, 9:10], 127.0, None, op.mult, fence=False)
        tk_so = p.tt(st[:, 11:12], st[:, 8:9], alpha_t[:, 3:4], op.mult)
        prev[("so", tb)] = tk_so
        tk_q1 = p.act(scr[:, :], yv[:, :], AF.Copy, bias=MAGIC,
                      scale=st[:, 10:11], deps=[tk_so])
        prev[("q2", tb)] = p.act(qv[:, :], scr[:, :], AF.Copy, bias=-MAGIC,
                                 deps=[tk_q1])

    def emit_post(tb):
        sl = tb % 2
        ps = ps2[sl]
        st = stats[:, sl * 24:(sl + 1) * 24]
        yv = yb[:, sl * D:(sl + 1) * D]
        qv = qx[:, sl * D:(sl + 1) * D]
        tk_tr = []
        for g4 in range(4):
            poff = g4 * 512
            tkm = None
            for j in range(4):
                dblk = g4 * 4 + j
                tkm = p.mm(ps[:, poff + j * 128: poff + (j + 1) * 128],
                           qv[:, dblk * 128:(dblk + 1) * 128], ident_b[:, :],
                           True, True,
                           deps=[prev[("q2", tb)], prev[("s1", tb)],
                                 tk_tr[g4 - 2] if g4 >= 2 else None],
                           fence=(j == 3))
            tk_tr.append(p.cp("dve", qxf[:, g4 * 512:(g4 + 1) * 512],
                              ps[:, poff:poff + 512], deps=[tkm]))
        tk_qxf = p.nopf("dve", deps=tk_tr)
        prev[("qxf", tb)] = tk_qxf
        tk_f = []
        for oc in range(4):
            tkm = None
            for kt in range(16):
                tkm = p.mm(ps[:, oc * 512:(oc + 1) * 512],
                           qxf[:, kt * 128:(kt + 1) * 128],
                           tern_sb[:, kt * D + oc * 512: kt * D + (oc + 1) * 512],
                           kt == 0, kt == 15,
                           deps=[tk_qxf, prev.get(("s1", tb + 1)),
                                 prev.get(("zr", tb - 1))] + tk_tern,
                           fence=(kt == 15))
            tk_f.append(tkm)
        tk_z = []
        for oc in range(4):
            tk_z.append(p.stt(zb[:, oc * 512:(oc + 1) * 512],
                              ps[:, oc * 512:(oc + 1) * 512], st[:, 11:12],
                              btile[:, oc * 512:(oc + 1) * 512],
                              op.mult, op.add,
                              deps=[tk_f[oc], prev[("so", tb)],
                                    prev.get(("ozb", tb - 1))]))
            prev[("z", tb, oc)] = tk_z[-1]
        tk_zr = p.tt(zb[:, :], zb[:, :], yv[:, :], op.add, deps=tk_z)
        prev[("zr", tb)] = tk_zr
        hD = D // 2
        tk_c1 = p.act(scr[:, 0:hD], zb[:, 0:hD], AF.Copy, deps=[tk_zr],
                      accum=st[:, 12:13])
        tk_c1b = p.ts(dsc[:, :], zb[:, hD:hD + 512], 1.0, 0.0,
                      op.mult, op.add, deps=[tk_zr], accum=st[:, 22:23])
        tk_c1c = p.ts(dsc[:, :], zb[:, hD + 512:D], 1.0, 0.0,
                      op.mult, op.add, deps=[tk_c1b], accum=st[:, 23:24])
        tk_c2 = p.act(scr[:, 0:hD], zb[:, 0:hD], AF.Square, deps=[tk_c1],
                      accum=st[:, 13:14])
        tk_c2b = p.stt(dsc[:, :], zb[:, hD:hD + 512], 1.0,
                       zb[:, hD:hD + 512], op.mult, op.mult,
                       deps=[tk_c1c], accum=st[:, 19:20])
        tk_c2c = p.stt(dsc[:, :], zb[:, hD + 512:D], 1.0, zb[:, hD + 512:D],
                       op.mult, op.mult, deps=[tk_c2b], accum=st[:, 21:22])
        prev[("c2", tb)] = p.nopf("dve", deps=[tk_c2, tk_c2c])
        p.tt(st[:, 22:23], st[:, 22:23], st[:, 23:24], op.add, fence=False)
        p.tt(st[:, 12:13], st[:, 12:13], st[:, 22:23], op.add,
             deps=[tk_c1], fence=False)
        p.tt(st[:, 19:20], st[:, 19:20], st[:, 21:22], op.add, fence=False)
        p.tt(st[:, 13:14], st[:, 13:14], st[:, 19:20], op.add,
             deps=[tk_c2], fence=False)
        p.ts(st[:, 14:15], st[:, 12:13], 1.0 / D, None, op.mult, fence=False)
        p.tt(st[:, 15:16], st[:, 14:15], st[:, 14:15], op.mult, fence=False)
        tk_v2 = p.stt(st[:, 16:17], st[:, 13:14], 1.0 / D, st[:, 15:16],
                      op.mult, op.subtract)
        p.act(st[:, 17:18], st[:, 16:17], AF.Ln, bias=EPS, deps=[tk_v2],
              fence=False)
        tk_st2 = p.act(st[:, 18:19], st[:, 17:18], AF.Exp, scale=-0.5)
        tk_o = p.ts(outb[:, :], zb[:, :], st[:, 14:15], st[:, 18:19],
                    op.subtract, op.mult,
                    deps=[tk_st2, tk_c2,
                          tk_out[-1] if tk_out else None])
        prev[("ozb", tb)] = tk_o
        tk_out.append(p.dma("sp", y_out[tb * 128:(tb + 1) * 128, :],
                            outb[:, :], deps=[tk_o]))

    for tb in range(NTB):
        emit_attn(tb)
        if tb >= 1:
            emit_post(tb - 1)
    emit_post(NTB - 1)

    p.wait_self_dma("sp", tk_out[-1])


def make_aux():
    ident = np.eye(128, dtype=np.float32)
    mA = np.where(np.tril(np.ones((128, 128)), -1) > 0, 0.0, NEG).astype(np.float32)
    mNb = np.where(np.triu(np.ones((128, 128)), 1) > 0, 0.0, NEG).astype(np.float32)
    mW = np.where(np.triu(np.ones((128, 128)), 0) > 0, 0.0, NEG).astype(np.float32)
    masks = np.concatenate([np.tile(m, (1, 8)) for m in (mA, mNb, mW)], axis=1)
    ind = np.zeros((48, D), np.float32)
    for j in range(3):
        for h in range(H):
            ind[j * 16 + h, h * 128:(h + 1) * 128] = 1.0
    oh = np.zeros((128, H * H), np.float32)
    for h in range(H):
        oh[:, h * 16 + h] = 1.0
    return ident, masks, ind, oh


def make_inputs(xb, W_qkv, W_ab, conv_w, A_log, dt_bias, o_norm_w, W_o, fw, fb):
    ident, masks, ind, oh = make_aux()
    wt = np.ascontiguousarray(
        np.broadcast_to(np.tile(np.asarray(o_norm_w, np.float32), H), (128, D)))
    bt = np.ascontiguousarray(
        np.broadcast_to(np.asarray(fb, np.float32), (128, D)))
    return dict(
        x=np.ascontiguousarray(xb, np.float32),
        w_qkv=np.ascontiguousarray(W_qkv, np.float32),
        w_ab=np.ascontiguousarray(W_ab, np.float32),
        conv_w=np.ascontiguousarray(conv_w, np.float32),
        a_log=np.asarray(A_log, np.float32).reshape(H, 1),
        dt_bias=np.asarray(dt_bias, np.float32).reshape(H, 1),
        w_o=np.ascontiguousarray(W_o, np.float32),
        ffn_w=np.ascontiguousarray(fw, np.float32),
        c_ident=ident, c_masks=masks, c_ind=ind, c_oh=oh,
        c_wtile=wt, c_btile=bt,
    )


def kernel(L0_attn_hidden_states, L0_attn_W_qkv, L0_attn_W_ab, L0_attn_conv_w,
           L0_attn_A_log, L0_attn_dt_bias, L0_attn_o_norm_w, L0_attn_W_o,
           L0_ffn_weight, L0_ffn_bias):
    x = np.ascontiguousarray(L0_attn_hidden_states, dtype=np.float32)
    B, T, _ = x.shape
    nc = build_program(T=T)
    in_maps = [make_inputs(x[b], L0_attn_W_qkv, L0_attn_W_ab, L0_attn_conv_w,
                           L0_attn_A_log, L0_attn_dt_bias, L0_attn_o_norm_w,
                           L0_attn_W_o, L0_ffn_weight, L0_ffn_bias)
               for b in range(B)]
    res = run_bass_kernel_spmd(nc, in_maps, core_ids=list(range(B)))
    return np.stack([res.results[b]["y"] for b in range(B)], axis=0).astype(np.float32)
